# version 07
# speedup vs baseline: 1.1202x; 1.1202x over previous
import sys
sys.path.insert(0, '/opt/trn_rl_repo')
from contextlib import ExitStack
import numpy as np
import ml_dtypes

import concourse.bass as bass
import concourse.mybir as mybir
from concourse.bass_utils import run_bass_kernel_spmd
from concourse.bass_core_rust import IndirectOffsetOnAxis

F32 = mybir.dt.float32
BF16 = mybir.dt.bfloat16
I32 = mybir.dt.int32
AF = mybir.ActivationFunctionType
ALU = mybir.AluOpType
AX = mybir.AxisListType

NCORE = 8
B = 4
S = 4096
D = 4096
HD = 128
NH = 32
NKV = 8
KVD = NKV * HD
NT = S // 128
OWN_T = 16
TOK = OWN_T * 128
E = 8
R = 64
ER = E * R
KQ = 128
EPS = 1e-5
NEG = -1.0e30

ENGS = ("sync", "vector", "scalar", "tensor", "gpsimd")



def host_prep(inputs):
    h = {}
    h["x"] = np.ascontiguousarray(inputs["L0_attn_x"], dtype=np.float32)
    h["wq0"] = np.ascontiguousarray(inputs["L0_attn_wq"], dtype=np.float32)
    h["wk0"] = np.ascontiguousarray(inputs["L0_attn_wk"], dtype=np.float32)
    h["wv0"] = np.ascontiguousarray(inputs["L0_attn_wv"], dtype=np.float32)
    h["wo0"] = np.ascontiguousarray(inputs["L0_attn_wo"], dtype=np.float32)
    h["wq1"] = np.ascontiguousarray(inputs["L1_attn_Wq"].T, dtype=np.float32)
    h["wo1"] = np.ascontiguousarray(inputs["L1_attn_Wo"].T, dtype=np.float32)
    h["worg"] = np.ascontiguousarray(inputs["L1_ffn_W_org"].T, dtype=np.float32)
    h["down"] = np.ascontiguousarray(
        np.transpose(inputs["L1_ffn_lora_down"], (1, 0, 2)).reshape(D, ER), dtype=np.float32)
    h["up"] = np.ascontiguousarray(
        np.asarray(inputs["L1_ffn_lora_up"]).reshape(ER, D), dtype=np.float32)
    h["wqf"] = np.ascontiguousarray(inputs["L1_ffn_Wq"].T, dtype=np.float32)
    h["keysT"] = np.ascontiguousarray(inputs["L1_ffn_lora_keys"].T, dtype=np.float32)
    h["bias"] = np.ascontiguousarray(np.asarray(inputs["L0_ffn_bias"]).reshape(1, D),
                                     dtype=np.float32)
    h["qw"] = np.ascontiguousarray(
        np.transpose(inputs["L0_ffn_qweight"], (1, 0, 2)), dtype=np.int32)
    h["lut4"] = np.ascontiguousarray(inputs["L0_ffn_lut4"], dtype=np.float32)
    h["lut8"] = np.ascontiguousarray(inputs["L0_ffn_lut8"], dtype=np.float32)
    h["kcache"] = np.ascontiguousarray(inputs["L1_attn_k_cache"], dtype=np.float32).reshape(512, -1)
    h["vcache"] = np.ascontiguousarray(inputs["L1_attn_v_cache"], dtype=np.float32).reshape(512, -1)
    h["bt"] = np.ascontiguousarray(inputs["L1_attn_block_table"], dtype=np.int32)

    h["ident"] = np.eye(128, dtype=np.float32)
    half = HD // 2
    inv0 = 1.0 / (10000.0 ** (np.arange(half, dtype=np.float32) / half))
    ang0 = np.arange(S, dtype=np.float32)[:, None] * inv0[None, :]
    c0, s0 = np.cos(ang0), np.sin(ang0)
    cosf = np.zeros((HD, S), np.float32)
    sinf = np.zeros((HD, S), np.float32)
    cosf[0::2, :] = c0.T
    cosf[1::2, :] = c0.T
    sinf[0::2, :] = -s0.T
    sinf[1::2, :] = s0.T
    h["rope0_cos_f"] = cosf
    h["rope0_sin_f"] = sinf
    h["rope0_cos_t"] = np.ascontiguousarray(cosf.T)
    h["rope0_sin_t"] = np.ascontiguousarray(sinf.T)
    inv1 = 1.0 / (10000.0 ** (np.arange(0, HD, 2, dtype=np.float32) / HD))
    ang1 = np.arange(S, dtype=np.float32)[:, None] * inv1[None, :]
    emb = np.concatenate([ang1, ang1], axis=1)
    h["rope1_cos_t"] = np.cos(emb).astype(np.float32)
    sin1 = np.sin(emb).astype(np.float32)
    sin1s = sin1.copy()
    sin1s[:, :half] = -sin1[:, :half]
    h["rope1_sin_t"] = sin1s
    m = np.zeros((128, 128), np.float32)
    r_idx = np.arange(128)
    m[r_idx[:, None] > r_idx[None, :]] = NEG
    h["diag_mask"] = m
    return h



class Ev:
    __slots__ = ("sem", "val")

    def __init__(self, sem, val):
        self.sem = sem
        self.val = val


class KB:

    def __init__(self, nc, ctx):
        self.nc = nc
        self.ctx = ctx
        self.q = {e: [] for e in ENGS}
        self._sems = {}
        self._waits = {e: {} for e in ENGS}

    def sem(self, name):
        if name not in self._sems:
            h = self.ctx.enter_context(self.nc.semaphore(name))
            self._sems[name] = [h, 0]
        return self._sems[name]

    def _bump(self, name, k):
        s = self.sem(name)
        s[1] += k
        return Ev(name, s[1])

    def wait(self, eng, ev):
        if ev is None:
            return
        evs = ev if isinstance(ev, (list, tuple)) else [ev]
        best = {}
        for e in evs:
            if e is None:
                continue
            if isinstance(e, (list, tuple)):
                for e2 in e:
                    if e2 is not None:
                        best[e2.sem] = max(best.get(e2.sem, -1), e2.val)
                continue
            best[e.sem] = max(best.get(e.sem, -1), e.val)
        for semname, v in best.items():
            known = self._waits[eng].get(semname, -1)
            if v <= known:
                continue
            self._waits[eng][semname] = v
            h = self.sem(semname)[0]
            self.q[eng].append(lambda p, h=h, v=v: p.wait_ge(h, v))

    def dma(self, eng, out, in_, wait=None, sem=None, **kw):
        self.wait(eng, wait)
        sem = sem or ("ld_" + eng)
        ev = self._bump(sem, 16)
        h = self.sem(sem)[0]
        self.q[eng].append(
            lambda p, h=h, out=out, in_=in_, kw=kw: p.dma_start(out=out, in_=in_, **kw)
            .then_inc(h, 16))
        return ev

    def op(self, eng, fn, wait=None):
        self.wait(eng, wait)
        self.q[eng].append(fn)

    def mark(self, eng, sem=None):
        sem = sem or ("ms_" + eng)
        ev = self._bump(sem, 1)
        h = self.sem(sem)[0]
        self.q[eng].append(lambda p, h=h: p.drain().then_inc(h, 1))
        return ev

    def raw_mark(self, eng, sem=None):
        sem = sem or ("ms_" + eng)
        ev = self._bump(sem, 1)
        h = self.sem(sem)[0]
        prev = self.q[eng].pop()
        self.q[eng].append(lambda p, h=h, prev=prev: prev(p).then_inc(h, 1))
        return ev

    def build(self, block):
        for eng in ENGS:
            items = self.q[eng]
            if not items:
                continue
            dec = getattr(block, eng)

            def body(proxy, items=items):
                for fn in items:
                    fn(proxy)
            dec(body)


class Rot:

    def __init__(self, bufs, name=None):
        self.bufs = bufs
        self.name = name
        self.free = [None] * len(bufs)
        self.i = 0

    def sem(self):
        return None if self.name is None else f"{self.name}{self.slot}"

    def get(self):
        b = self.bufs[self.i % len(self.bufs)]
        ev = self.free[self.i % len(self.bufs)]
        self.slot = self.i % len(self.bufs)
        self.i += 1
        return b, ev

    def set_free(self, slot, ev):
        self.free[slot] = ev


def emit_warmup(kb, warm_bf, warm_ps, warm_sb, warm_o):
    def mset(p):
        p.memset(warm_bf[:], 0.5)
        return p.memset(warm_sb[:], 0.5)
    kb.op("vector", mset)
    e = kb.mark("vector")

    def pe_warm(p):
        for _ in range(64):
            ins = p.matmul(out=warm_ps[:], lhsT=warm_bf[:], rhs=warm_bf[:],
                           start=True, stop=True)
        return p.drain()
    kb.op("tensor", pe_warm, wait=e)

    def sc_warm(p):
        for i in range(16):
            ins = p.activation(out=warm_o[:, 8 * i:8 * i + 8],
                               in_=warm_bf[:, 0:8], func=AF.Copy)
        return p.drain()
    kb.op("scalar", sc_warm, wait=e)



CAST_MATS = [
    ("wq0", D, D), ("wk0", D, KVD), ("wv0", D, KVD), ("wo0", D, D),
    ("wq1", D, D), ("wo1", D, D), ("worg", D, D),
    ("down", D, ER), ("up", ER, D), ("wqf", D, KQ),
]


def build_prep():
    nc = bass.Bass("TRN2", target_bir_lowering=False)
    ins, outs = {}, {}
    for name, rows, cols in CAST_MATS:
        ins[name] = nc.dram_tensor(name + "_s", [rows // NCORE, cols], F32,
                                   kind="ExternalInput").ap()
        outs[name] = nc.dram_tensor(name + "_b", [rows // NCORE, cols], BF16,
                                    kind="ExternalOutput").ap()
    ins["keysT"] = nc.dram_tensor("keysT_s", [KQ, E], F32, kind="ExternalInput").ap()
    outs["keysT"] = nc.dram_tensor("keysT_b", [KQ, E], BF16, kind="ExternalOutput").ap()
    qw = nc.dram_tensor("qw_s", [512, 8, 128], I32, kind="ExternalInput").ap()
    lut4 = nc.dram_tensor("lut4_s", [512, 16], F32, kind="ExternalInput").ap()
    lut8 = nc.dram_tensor("lut8_s", [512, 256], F32, kind="ExternalInput").ap()
    identin = nc.dram_tensor("identin", [128, 128], F32, kind="ExternalInput").ap()
    wlT = nc.dram_tensor("wlT_s", [D, 512], BF16, kind="ExternalOutput").ap()
    whT = nc.dram_tensor("whT_s", [D, 512], BF16, kind="ExternalOutput").ap()

    with ExitStack() as ctx:
        sb = lambda n, sh, dt: ctx.enter_context(nc.sbuf_tensor(n, sh, dt))
        ps = lambda n, sh, dt: ctx.enter_context(nc.psum_tensor(n, sh, dt))
        cin = Rot([sb(f"cin{i}", [128, D], F32) for i in range(2)])
        cout = Rot([sb(f"cout{i}", [128, D], BF16) for i in range(2)])
        qw_sb = sb("qw_sb", [128, 1024], I32)
        idx_i = sb("idx_i", [128, D], I32)
        bit_t = sb("bit_t", [128, 128], I32)
        bit_t2 = sb("bit_t2", [128, 128], I32)
        idx_bf = sb("idx_bf", [128, D], BF16)
        idx4_bf = sb("idx4_bf", [128, D], BF16)
        l4_sb = sb("l4_sb", [128, 16], F32)
        l8_sb = sb("l8_sb", [128, 256], F32)
        wtile = sb("wtile", [128, D], BF16)
        wtile2 = sb("wtile2", [128, D], BF16)
        wl_sb = sb("wl_sb", [128, 4 * D], BF16)
        wh_sb = sb("wh_sb", [128, 4 * D], BF16)
        identf = sb("identf", [128, 128], F32)
        ident_bf = sb("ident_bf", [128, 128], BF16)
        tev = Rot([sb(f"tev{i}", [128, 512], BF16) for i in range(2)])
        warm_sb = sb("warm_sb", [128, 128], F32)
        warm_bf = sb("warm_bf", [128, 128], BF16)
        warm_o = sb("warm_o", [128, 128], F32)
        ptr = Rot([ps(f"ptr{i}", [128, 512], F32) for i in range(2)])
        warm_ps = ps("warm_ps", [128, 128], F32)

        block = ctx.enter_context(nc.Block())
        kb = KB(nc, ctx)
        emit_warmup(kb, warm_bf, warm_ps, warm_sb, warm_o)

        e_id = kb.dma("sync", identf[:], identin)
        kb.op("vector", lambda p: p.tensor_copy(out=ident_bf[:], in_=identf[:]), wait=e_id)
        e_idbf = kb.mark("vector")

        cast_tiles = []
        for name, rows, cols in CAST_MATS:
            rs = rows // NCORE
            for t in range((rs + 127) // 128):
                cast_tiles.append((name, t, min(128, rs - 128 * t), cols))
        cast_tiles.append(("keysT", 0, KQ, E))

        store_evs = []
        for (name, t, r, cols) in cast_tiles:
            bi, ev_i = cin.get()
            si = cin.slot
            e_ld = kb.dma("scalar", bi[0:r, 0:cols], ins[name][128 * t:128 * t + r, :],
                          wait=ev_i)
            bo, ev_o = cout.get()
            so = cout.slot
            kb.op("scalar",
                  lambda p, bo=bo, bi=bi, r=r, cols=cols:
                  p.activation(out=bo[0:r, 0:cols], in_=bi[0:r, 0:cols], func=AF.Copy),
                  wait=[e_ld, ev_o])
            e_cast = kb.mark("scalar")
            cin.set_free(si, e_cast)
            e_st = kb.dma("gpsimd", outs[name][128 * t:128 * t + r, :], bo[0:r, 0:cols],
                          wait=e_cast)
            cout.set_free(so, e_st)
            store_evs.append(e_st)

        e_unpack_prev = None
        e_tile_prev = None
        for ot in range(4):
            e_qw = kb.dma("sync", qw_sb[:], qw[128 * ot:128 * ot + 128, :, :],
                          wait=e_unpack_prev)
            e_l4 = kb.dma("sync", l4_sb[:], lut4[128 * ot:128 * ot + 128, :],
                          wait=e_tile_prev)
            e_l8 = kb.dma("sync", l8_sb[:], lut8[128 * ot:128 * ot + 128, :])

            def unpack(p, qw_sb=qw_sb):
                def ext(tgt, j, pl):
                    src = qw_sb[:, 128 * pl:128 * pl + 128]
                    if j >= pl:
                        sh, shop = j - pl, ALU.logical_shift_right
                    else:
                        sh, shop = pl - j, ALU.logical_shift_left
                    p.tensor_scalar(out=tgt, in0=src, scalar1=sh, scalar2=1 << pl,
                                    op0=shop, op1=ALU.bitwise_and)
                for j in range(0, 32, 2):
                    d0 = idx_i[:, 128 * j:128 * j + 128]
                    d1 = idx_i[:, 128 * (j + 1):128 * (j + 2)]
                    ext(d0, j, 0)
                    ext(d1, j + 1, 0)
                    for pl in range(1, 8):
                        ext(bit_t[:], j, pl)
                        ext(bit_t2[:], j + 1, pl)
                        p.tensor_tensor(out=d0, in0=d0, in1=bit_t[:], op=ALU.bitwise_or)
                        p.tensor_tensor(out=d1, in0=d1, in1=bit_t2[:], op=ALU.bitwise_or)
                p.drain()
                p.tensor_copy(out=idx_bf[:], in_=idx_i[:])
                p.drain()
                p.tensor_scalar(out=idx_i[:], in0=idx_i[:], scalar1=15, scalar2=None,
                                op0=ALU.bitwise_and)
                p.drain()
                return p.tensor_copy(out=idx4_bf[:], in_=idx_i[:])
            kb.op("vector", unpack, wait=[e_qw, e_l4, e_l8])
            e_unpack_prev = kb.mark("vector")

            for nb, src_idx, lsb, wsb in [(16, idx4_bf, l4_sb, wl_sb),
                                          (256, idx_bf, l8_sb, wh_sb)]:
                def codeloop(p, nb=nb, src_idx=src_idx, lsb=lsb, wsb=wsb, ot=ot):
                    dst = wsb[:, D * ot:D * ot + D]
                    def sel(tgt, cde):
                        p.tensor_scalar(out=tgt, in0=src_idx[:],
                                        scalar1=float(cde), scalar2=lsb[:, cde:cde + 1],
                                        op0=ALU.is_equal, op1=ALU.mult)
                    sel(wtile[:], 0)
                    sel(wtile2[:], 1)
                    p.tensor_copy(out=dst, in_=wtile[:])
                    p.tensor_tensor(out=dst, in0=dst, in1=wtile2[:], op=ALU.add)
                    for cde in range(2, nb, 2):
                        sel(wtile[:], cde)
                        sel(wtile2[:], cde + 1)
                        p.tensor_tensor(out=dst, in0=dst, in1=wtile[:], op=ALU.add)
                        p.tensor_tensor(out=dst, in0=dst, in1=wtile2[:], op=ALU.add)
                    return p.drain()
                kb.op("vector", codeloop)
            e_tile_prev = kb.mark("vector")
        e_deq = e_tile_prev

        for li, wsb, dst in [(0, wl_sb, wlT), (1, wh_sb, whT)]:
            for icb in range(32):
                pb, ev_p = ptr.get()
                sp = ptr.slot

                def tr(p, pb=pb, wsb=wsb, icb=icb):
                    for ot in range(4):
                        p.matmul(out=pb[:, 128 * ot:128 * ot + 128],
                                 lhsT=wsb[:, D * ot + 128 * icb:D * ot + 128 * icb + 128],
                                 rhs=ident_bf[:], start=True, stop=True)
                    return p.drain()
                kb.op("tensor", tr, wait=[e_deq, e_idbf, ev_p])
                e_tr = kb.mark("tensor")
                bo, ev_o = tev.get()
                so = tev.slot
                kb.op("vector", lambda p, bo=bo, pb=pb: p.tensor_copy(out=bo[:], in_=pb[:]),
                      wait=[e_tr, ev_o])
                e_ev = kb.mark("vector")
                ptr.set_free(sp, e_ev)
                e_st = kb.dma("gpsimd",
                              dst.rearrange("(w j) c -> j w c", j=32)[icb], bo[:],
                              wait=e_ev)
                tev.set_free(so, e_st)
                store_evs.append(e_st)

        kb.wait("gpsimd", store_evs[-1])
        kb.wait("gpsimd", store_evs)
        kb.build(block)
    return nc


def prep_in_maps(h):
    maps = []
    for c in range(NCORE):
        m = {"identin": h["ident"], "keysT_s": h["keysT"]}
        for name, rows, cols in CAST_MATS:
            rs = rows // NCORE
            m[name + "_s"] = np.ascontiguousarray(h[name][rs * c:rs * c + rs])
        m["qw_s"] = np.ascontiguousarray(h["qw"][512 * c:512 * c + 512])
        m["lut4_s"] = np.ascontiguousarray(h["lut4"][512 * c:512 * c + 512])
        m["lut8_s"] = np.ascontiguousarray(h["lut8"][512 * c:512 * c + 512])
        maps.append(m)
    return maps


def assemble_prep(res):
    w = {}
    for name, rows, cols in CAST_MATS:
        w[name] = np.concatenate([res.results[c][name + "_b"] for c in range(NCORE)], axis=0)
    w["keysT"] = res.results[0]["keysT_b"]
    w["wlT"] = np.concatenate([res.results[c]["wlT_s"] for c in range(NCORE)], axis=1)
    w["whT"] = np.concatenate([res.results[c]["whT_s"] for c in range(NCORE)], axis=1)
    return w




def build_pa():
    nc = bass.Bass("TRN2", target_bir_lowering=False)
    xc = nc.dram_tensor("xc", [S, D], F32, kind="ExternalInput").ap()
    wk_cols = nc.dram_tensor("wk_cols", [D, 512], BF16, kind="ExternalInput").ap()
    wv_cols = nc.dram_tensor("wv_cols", [D, 512], BF16, kind="ExternalInput").ap()
    ropec = nc.dram_tensor("ropec", [S, HD], F32, kind="ExternalInput").ap()
    ropes = nc.dram_tensor("ropes", [S, HD], F32, kind="ExternalInput").ap()
    identin = nc.dram_tensor("identin", [128, 128], F32, kind="ExternalInput").ap()
    xnT0 = nc.dram_tensor("xnT0", [NT, 128, D], BF16, kind="ExternalOutput").ap()
    kT0h = nc.dram_tensor("kT0h", [4, HD, S], BF16, kind="ExternalOutput").ap()
    v0h = nc.dram_tensor("v0h", [4, S, HD], BF16, kind="ExternalOutput").ap()
    maskv = nc.dram_tensor("maskv", [S], F32, kind="ExternalOutput").ap()

    with ExitStack() as ctx:
        sb = lambda n, sh, dt: ctx.enter_context(nc.sbuf_tensor(n, sh, dt))
        ps = lambda n, sh, dt: ctx.enter_context(nc.psum_tensor(n, sh, dt))
        xg = Rot([sb(f"xg{i}", [128, D], F32) for i in range(2)])
        rc = Rot([sb(f"rc{i}", [128, HD], F32) for i in range(2)])
        rs = Rot([sb(f"rs{i}", [128, HD], F32) for i in range(2)])
        wk_sb = sb("wk_sb", [128, 32 * 512], BF16)
        wv_sb = sb("wv_sb", [128, 32 * 512], BF16)
        xnt = Rot([sb(f"xnt{i}", [128, D], BF16) for i in range(2)])
        xnbf = sb("xnbf", [128, D], BF16)
        ssum = sb("ssum", [128, 1], F32)
        ssq = sb("ssq", [128, 1], F32)
        negm = sb("negm", [128, 1], F32)
        msq = sb("msq", [128, 1], F32)
        var = sb("var", [128, 1], F32)
        stdv = sb("stdv", [128, 1], F32)
        rstd = sb("rstd", [128, 1], F32)
        r2 = sb("r2", [128, 1], F32)
        e2 = sb("e2", [128, 1], F32)
        maskb = sb("maskb", [128, 1], F32)
        epsb = sb("epsb", [128, 1], F32)
        sqs = sb("sqs", [128, D], F32)
        kv_ev = Rot([sb(f"kv_ev{i}", [128, 512], BF16) for i in range(2)])
        kf = sb("kf", [128, 512], F32)
        kt1 = sb("kt1", [128, 512], F32)
        kt2 = sb("kt2", [128, 512], F32)
        krbf = sb("krbf", [128, 512], BF16)
        ktb = Rot([sb(f"ktb{i}", [128, 512], BF16) for i in range(2)])
        identf = sb("identf", [128, 128], F32)
        ident_bf = sb("ident_bf", [128, 128], BF16)
        warm_sb = sb("warm_sb", [128, 128], F32)
        warm_bf = sb("warm_bf", [128, 128], BF16)
        warm_o = sb("warm_o", [128, 128], F32)
        ptx = Rot([ps(f"ptx{i}", [128, 512], F32) for i in range(2)])
        pv = ps("pv", [128, 512], F32)
        pk = ps("pk", [128, 512], F32)
        pkt = ps("pkt", [128, 512], F32)
        warm_ps = ps("warm_ps", [128, 128], F32)

        block = ctx.enter_context(nc.Block())
        kb = KB(nc, ctx)
        emit_warmup(kb, warm_bf, warm_ps, warm_sb, warm_o)

        e_id = kb.dma("sync", identf[:], identin)
        kb.op("vector", lambda p: p.tensor_copy(out=ident_bf[:], in_=identf[:]), wait=e_id)
        kb.op("vector", lambda p: p.memset(epsb[:], EPS))
        e_prep = kb.mark("vector")
        for kk in range(32):
            e_wk = kb.dma("gpsimd", wk_sb[:, 512 * kk:512 * kk + 512],
                          wk_cols[128 * kk:128 * kk + 128, :])
            e_wv = kb.dma("gpsimd", wv_sb[:, 512 * kk:512 * kk + 512],
                          wv_cols[128 * kk:128 * kk + 128, :])

        store_evs = []
        for g in range(NT):
            own = True
            t_local = g
            bx, evx = xg.get()
            sx_ = xg.slot
            e_x = kb.dma("sync", bx[:], xc[128 * g:128 * g + 128, :], wait=evx)
            brc, evrc = rc.get()
            src_ = rc.slot
            e_rc = kb.dma("sync", brc[:], ropec[128 * g:128 * g + 128, :], wait=evrc)
            brs, evrs = rs.get()
            srs_ = rs.slot
            e_rs = kb.dma("sync", brs[:], ropes[128 * g:128 * g + 128, :], wait=evrs)

            def stats_v(p, bx=bx):
                p.tensor_reduce(out=ssum[:], in_=bx[:], axis=AX.X, op=ALU.add)
                p.drain()
                p.tensor_scalar(out=negm[:], in0=ssum[:], scalar1=-1.0 / D,
                                scalar2=None, op0=ALU.mult)
                return p.drain()
            kb.op("vector", stats_v, wait=e_x)
            e_negm = kb.mark("vector")

            def stats_s(p, bx=bx):
                p.activation(out=sqs[:], in_=bx[:], func=AF.Square, accum_out=ssq[:])
                p.drain()
                p.activation(out=msq[:], in_=negm[:], func=AF.Square)
                return p.drain()
            kb.op("scalar", stats_s, wait=[e_x, e_negm])
            e_msq = kb.mark("scalar")

            def var_v(p):
                p.scalar_tensor_tensor(out=var[:], in0=ssq[:], scalar=1.0 / D,
                                       in1=msq[:], op0=ALU.mult, op1=ALU.subtract)
                return p.drain()
            kb.op("vector", var_v, wait=e_msq)
            e_var = kb.mark("vector")

            def std_s(p):
                p.activation(out=stdv[:], in_=var[:], func=AF.Sqrt, bias=epsb[:])
                return p.drain()
            kb.op("scalar", std_s, wait=[e_var, e_prep])
            e_std = kb.mark("scalar")

            def xn_v(p, bx=bx, own=own):
                p.reciprocal(out=rstd[:], in_=stdv[:])
                p.drain()
                p.tensor_scalar(out=xnbf[:], in0=bx[:], scalar1=negm[:], scalar2=rstd[:],
                                op0=ALU.add, op1=ALU.mult)
                if own:
                    p.tensor_tensor(out=r2[:], in0=rstd[:], in1=rstd[:], op=ALU.mult)
                    p.drain()
                    p.scalar_tensor_tensor(out=e2[:], in0=var[:], scalar=float(D),
                                           in1=r2[:], op0=ALU.mult, op1=ALU.mult)
                    p.drain()
                    p.tensor_scalar(out=maskb[:], in0=e2[:], scalar1=float(D),
                                    scalar2=None, op0=ALU.is_gt)
                return p.drain()
            kb.op("vector", xn_v, wait=e_std)
            e_xn = kb.mark("vector")
            if own:
                store_evs.append(kb.dma(
                    "gpsimd", maskv[128 * t_local:128 * t_local + 128]
                    .rearrange("(p one) -> p one", one=1), maskb[:], wait=e_xn))

            bxn, evxn = xnt.get()
            sxn_ = xnt.slot
            e_ev_last = []
            for grp in range(8):
                pb, evp = ptx.get()
                sp = ptx.slot

                def tr(p, pb=pb, grp=grp):
                    for q4 in range(4):
                        ic = 4 * grp + q4
                        p.matmul(out=pb[:, 128 * q4:128 * q4 + 128],
                                 lhsT=xnbf[:, 128 * ic:128 * ic + 128],
                                 rhs=ident_bf[:], start=True, stop=True)
                    return p.drain()
                kb.op("tensor", tr, wait=[e_xn, e_prep, evp] + ([evxn] if grp == 0 and evxn else []))
                e_tr = kb.mark("tensor")

                kb.op("vector",
                      lambda p, bxn=bxn, grp=grp, pb=pb:
                      p.tensor_copy(out=bxn[:, 512 * grp:512 * grp + 512], in_=pb[:]),
                      wait=e_tr)
                e_ev = kb.mark("vector")
                ptx.set_free(sp, e_ev)
                e_ev_last.append(e_ev)
            e_xnt = e_ev_last[-1]
            if own:
                store_evs.append(kb.dma("gpsimd", xnT0[t_local], bxn[:], wait=e_xnt))

            def kv_mm(p, bxn=bxn):
                for ic in range(32):
                    p.matmul(out=pv[:], lhsT=bxn[:, 128 * ic:128 * ic + 128],
                             rhs=wv_sb[:, 512 * ic:512 * ic + 512],
                             start=(ic == 0), stop=(ic == 31))
                for ic in range(32):
                    p.matmul(out=pk[:], lhsT=bxn[:, 128 * ic:128 * ic + 128],
                             rhs=wk_sb[:, 512 * ic:512 * ic + 512],
                             start=(ic == 0), stop=(ic == 31))
                return p.drain()
            kb.op("tensor", kv_mm, wait=[e_xnt, e_wk, e_wv])
            e_kv = kb.mark("tensor")
            xnt.set_free(sxn_, e_kv)

            bv, evv = kv_ev.get()
            sv_ = kv_ev.slot
            kb.op("vector", lambda p, bv=bv: p.tensor_copy(out=bv[:], in_=pv[:]),
                  wait=[e_kv, evv])
            kb.op("vector", lambda p: p.tensor_copy(out=kf[:], in_=pk[:]))
            e_kevac = kb.mark("vector")
            for hh in range(4):
                store_evs.append(kb.dma(
                    "gpsimd", v0h[hh, 128 * g:128 * g + 128, :],
                    bv[:, 128 * hh:128 * hh + 128], wait=e_kevac))
            kv_ev.set_free(sv_, store_evs[-1])

            def rope_v(p, brc=brc, brs=brs):
                kview = kf[:].rearrange("p (h d) -> p h d", h=4)
                cos4 = brc[:].rearrange("p (o d) -> p o d", o=1).broadcast_to([128, 4, HD])
                k4 = kf[:].rearrange("p (h pr two) -> p h pr two", h=4, two=2)
                o4 = kt2[:].rearrange("p (h pr two) -> p h pr two", h=4, two=2)
                s2 = brs[:].rearrange("p (pr two) -> p pr two", two=2)
                p.tensor_tensor(out=kt1[:].rearrange("p (h d) -> p h d", h=4),
                                in0=kview, in1=cos4, op=ALU.mult)
                p.tensor_tensor(
                    out=o4[:, :, :, 0:1], in0=k4[:, :, :, 1:2],
                    in1=s2[:, :, 0:1].unsqueeze(1)
                    .broadcast_to([128, 4, HD // 2, 1]), op=ALU.mult)
                p.tensor_tensor(
                    out=o4[:, :, :, 1:2], in0=k4[:, :, :, 0:1],
                    in1=s2[:, :, 1:2].unsqueeze(1)
                    .broadcast_to([128, 4, HD // 2, 1]), op=ALU.mult)
                p.drain()
                p.tensor_tensor(out=krbf[:], in0=kt1[:], in1=kt2[:], op=ALU.add)
                return p.drain()
            kb.op("vector", rope_v, wait=[e_kevac, e_rc, e_rs])
            e_rope = kb.mark("vector")

            def ktr(p):
                for hh in range(4):
                    p.matmul(out=pkt[:, 128 * hh:128 * hh + 128],
                             lhsT=krbf[:, 128 * hh:128 * hh + 128],
                             rhs=ident_bf[:], start=True, stop=True)
                return p.drain()
            kb.op("tensor", ktr, wait=e_rope)
            e_ktr = kb.mark("tensor")
            bkt, evkt = ktb.get()
            skt_ = ktb.slot
            kb.op("vector", lambda p, bkt=bkt: p.tensor_copy(out=bkt[:], in_=pkt[:]),
                  wait=[e_ktr, evkt])
            e_ktb = kb.mark("vector")
            for hh in range(4):
                store_evs.append(kb.dma(
                    "gpsimd", kT0h[hh, :, 128 * g:128 * g + 128],
                    bkt[:, 128 * hh:128 * hh + 128], wait=e_ktb))
            ktb.set_free(skt_, store_evs[-1])
            xg.set_free(sx_, e_xn)
            rc.set_free(src_, e_rope)
            rs.set_free(srs_, e_rope)

        kb.wait("gpsimd", store_evs)
        kb.build(block)
    return nc





def tick(kb, eng, sem=None):
    return kb.raw_mark(eng, sem)


def igather(kb, out_ap, in_ap, idx_col, wait=None, sem="sig", eoff=0):
    kb.wait("gpsimd", wait)
    ev = kb._bump(sem, 16)
    h = kb.sem(sem)[0]
    kb.q["gpsimd"].append(
        lambda p, out_ap=out_ap, in_ap=in_ap, idx_col=idx_col, eoff=eoff, h=h:
        p.indirect_dma_start(out=out_ap, out_offset=None, in_=in_ap,
                             in_offset=IndirectOffsetOnAxis(idx_col, 0),
                             element_offset=eoff).then_inc(h, 16))
    return ev


def kb_barrier(kb):
    evs = [kb.mark(e) for e in ("vector", "scalar", "tensor")]
    for name, (h, n) in list(kb._sems.items()):
        if n:
            evs.append(Ev(name, n))
    for eng in ENGS:
        kb.wait(eng, evs)


def kb_collective(kb, tag, ins, outs, groups, wait=None):
    kb.wait("gpsimd", wait)
    evs = []
    for i, (a_in, a_out) in enumerate(zip(ins, outs)):
        ev = kb._bump(f"cc_{tag}{i}", 1)
        h = kb.sem(f"cc_{tag}{i}")[0]
        kb.q["gpsimd"].append(
            lambda p, a_in=a_in, a_out=a_out, groups=groups, h=h:
            p.collective_compute("AllGather", ALU.bypass, replica_groups=groups,
                                 ins=[a_in.opt()],
                                 outs=[a_out.opt()]).then_inc(h, 1))
        evs.append(ev)
    return evs


def build_pc(n_groups=4, stop=None):
    nc = bass.Bass("TRN2", target_bir_lowering=False)
    xnT0o = nc.dram_tensor("xnT0o", [OWN_T, 128, D], BF16, kind="ExternalInput").ap()
    kT0 = nc.dram_tensor("kT0", [NKV, HD, S], BF16, kind="ExternalInput").ap()
    v0 = nc.dram_tensor("v0", [NKV, S, HD], BF16, kind="ExternalInput").ap()
    maskvo = nc.dram_tensor("maskvo", [TOK], F32, kind="ExternalInput").ap()
    xo_in = nc.dram_tensor("xo_in", [TOK, D], F32, kind="ExternalInput").ap()
    wq0b = nc.dram_tensor("wq0b", [D, D], BF16, kind="ExternalInput").ap()
    wo0b = nc.dram_tensor("wo0b", [D, D], BF16, kind="ExternalInput").ap()
    wlT = nc.dram_tensor("wlT", [D, D], BF16, kind="ExternalInput").ap()
    whT = nc.dram_tensor("whT", [D, D], BF16, kind="ExternalInput").ap()
    bias_in = nc.dram_tensor("bias_in", [1, D], F32, kind="ExternalInput").ap()
    rqc = nc.dram_tensor("rqc", [TOK, HD], F32, kind="ExternalInput").ap()
    rqs = nc.dram_tensor("rqs", [TOK, HD], F32, kind="ExternalInput").ap()
    m0in = nc.dram_tensor("m0in", [128, 128], F32, kind="ExternalInput").ap()
    m1in = nc.dram_tensor("m1in", [128, 128], F32, kind="ExternalInput").ap()
    identin = nc.dram_tensor("identin", [128, 128], F32, kind="ExternalInput").ap()
    bias_dram = nc.dram_tensor("bias_dram", [1, D], BF16, kind="Internal").ap()
    x1_out = nc.dram_tensor("x1_out", [TOK, D], F32, kind="ExternalOutput").ap()
    xnT1 = nc.dram_tensor("xnT1", [OWN_T, 128, D], BF16, kind="ExternalOutput").ap()
    msum_out = nc.dram_tensor("msum_out", [128, 32], F32, kind="ExternalOutput").ap()

    with ExitStack() as ctx:
        sb = lambda n, sh, dt: ctx.enter_context(nc.sbuf_tensor(n, sh, dt))
        xg = [sb(f"xg{t}", [128, D], BF16) for t in range(4)]
        wqc = Rot([sb(f"wqc{i}", [128, 512], BF16) for i in range(2)], "swqc")
        q_sb = sb("q_sb", [128, 512], F32)
        qt1 = sb("qt1", [128, 512], F32)
        qt2 = sb("qt2", [128, 512], F32)
        qrbf = sb("qrbf", [128, 512], BF16)
        rqc_sb = [sb(f"rqc{t}", [128, HD], F32) for t in range(4)]
        rqs_sb = [sb(f"rqs{t}", [128, HD], F32) for t in range(4)]
        qT_sb = sb("qT_sb", [128, NH * 512], BF16)
        kt_sb = sb("kt_sb", [128, S], BF16)
        v_sb = sb("v_sb", [128, S], BF16)
        ptb = Rot([sb(f"ptb{i}", [128, 2048], BF16) for i in range(2)])
        absq = sb("absq", [128, 512], BF16)
        maxkf = sb("maxkf", [128, 1], F32)
        maxkb = sb("maxkb", [128, 1], BF16)
        bstar = sb("bstar", [1, 1], F32)
        btmp = sb("btmp", [1, 1], F32)
        bstar_b = sb("bstar_b", [1, 1], BF16)
        nb_sb = sb("nb_sb", [128, 1], F32)
        sum_sb = sb("sum_sb", [1, 512], F32)
        rec_bf = sb("rec_bf", [1, 512], BF16)
        recb_sb = sb("recb_sb", [128, 512], F32)
        ctx_sb = sb("ctx_sb", [128, NH * 512], BF16)
        ones_col = sb("ones_col", [128, 1], BF16)
        ones_row = sb("ones_row", [1, 128], BF16)
        m0_sb = sb("m0_sb", [128, 128], BF16)
        m1_sb = sb("m1_sb", [128, 128], BF16)
        mtmp = sb("mtmp", [128, 128], F32)
        wch = Rot([sb(f"wch{i}", [128, 1024], BF16) for i in range(3)], "swch")
        u_sb = [sb(f"u_sb{i}", [128, 1024], F32) for i in range(2)]
        yl_sb = [sb(f"yl_sb{i}", [128, 1024], F32) for i in range(2)]
        x1_sb = [sb(f"x1_sb{i}", [128, D], F32) for i in range(2)]
        om_sb = [sb(f"om_sb{i}", [128, 1], F32) for i in range(2)]
        mk_sb = [sb(f"mk_sb{i}", [128, 1], F32) for i in range(2)]
        bias_sb = sb("bias_sb", [1, 1024], BF16)
        ssum = sb("ssum", [128, 1], F32)
        ssq = sb("ssq", [128, 1], F32)
        negm = sb("negm", [128, 1], F32)
        msq = sb("msq", [128, 1], F32)
        var = sb("var", [128, 1], F32)
        stdv = sb("stdv", [128, 1], F32)
        rstd = sb("rstd", [128, 1], F32)
        epsb = sb("epsb", [128, 1], F32)
        xn1bf = sb("xn1bf", [128, D], BF16)
        msum_f = sb("msum_f", [128, 32], F32)
        msev = sb("msev", [128, 32], F32)
        identf = sb("identf", [128, 128], F32)
        ident_bf = sb("ident_bf", [128, 128], BF16)
        warm_sb = sb("warm_sb", [128, 128], F32)
        warm_bf = sb("warm_bf", [128, 128], BF16)
        warm_o = sb("warm_o", [128, 128], F32)

        pool_t = ctx.enter_context(nc.psum_tensor("pool", [128, 4096], F32))
        pool = pool_t[:]
        pq = [pool[:, 0:1024], pool[:, 1024:2048]]
        ptq = [pool[:, 2048:2560], pool[:, 2560:3072]]
        pst = pool[:, 0:2048]
        pctx = pool[:, 2048:2560]
        psum_s = pool[0:1, 2560:3072]
        pnb = pool[:, 3072:3073]
        prec = pool[:, 3584:4096]
        pa = [pool[:, 0:1024], pool[:, 1024:2048]]
        pmsum = pool[:, 3104:3136]
        warm_ps = pool[:, 0:128]

        block = ctx.enter_context(nc.Block())
        kb = KB(nc, ctx)
        emit_warmup(kb, warm_bf, warm_ps, warm_sb, warm_o)

        e_id = kb.dma("sync", identf[:], identin, sem="sprep")
        e_m0 = kb.dma("sync", mtmp[:], m0in, sem="sprep")

        def prep_v(p):
            p.tensor_copy(out=ident_bf[:], in_=identf[:])
            p.memset(epsb[:], EPS)
            p.memset(ones_col[:], 1.0)
            p.memset(ones_row[:], 1.0)
            p.memset(msum_f[:], 0.0)
            return p.tensor_copy(out=m0_sb[:], in_=mtmp[:])
        kb.op("vector", prep_v, wait=[e_id, e_m0])
        e_prep0 = kb.mark("vector")
        for bz4 in range(4):
            e_b = kb.dma("sync", u_sb[0][0:1, :],
                         bias_in[0:1, 1024 * bz4:1024 * bz4 + 1024],
                         wait=e_prep0, sem="sbia")
            kb.op("vector",
                  lambda p: p.tensor_copy(out=bias_sb[:], in_=u_sb[0][0:1, :]),
                  wait=e_b)
            e_bc = kb.mark("vector")
            e_prep0 = kb.dma("gpsimd", bias_dram[0:1, 1024 * bz4:1024 * bz4 + 1024],
                             bias_sb[:], wait=e_bc, sem="sbiag")
        e_m1 = kb.dma("sync", mtmp[:], m1in, wait=e_prep0, sem="sprep")
        kb.op("vector", lambda p: p.tensor_copy(out=m1_sb[:], in_=mtmp[:]), wait=e_m1)
        e_prep = kb.mark("vector")

        store_evs = []
        ev_evac = [None, None]
        e_bnd_ev = None
        e_bias_used = e_prep0
        e_kv_free = None
        e_prev_comb = None
        e_xg = [None] * 4
        e_qT_free = None
        e_ctx_free = None
        ln_free = [None, None]

        for G in range(n_groups):
            e_xg_ld = []
            for j in range(4):
                t = 4 * G + j
                e_xg_ld.append(kb.dma("sync", xg[j][:], xnT0o[t], wait=e_xg[j], sem=f"sxg{j}"))
                e_xg_ld.append(kb.dma("sync", rqc_sb[j][:], rqc[128 * t:128 * t + 128, :], sem=f"srq{j}"))
                e_xg_ld.append(kb.dma("sync", rqs_sb[j][:], rqs[128 * t:128 * t + 128, :], sem=f"srq{j}"))
            for r in range(8):
                for tp in range(2):
                    for ic in range(32):
                        bw, evw = wqc.get()
                        sw = wqc.slot
                        e_w = kb.dma("sync", bw[:],
                                     wq0b[128 * ic:128 * ic + 128,
                                          512 * r:512 * r + 512], wait=evw,
                                     sem=wqc.sem())

                        def qmm(p, bw=bw, ic=ic, tp=tp):
                            for jj in range(2):
                                j = 2 * tp + jj
                                ins = p.matmul(out=pq[jj][:, 0:512], rhs=bw[:],
                                               lhsT=xg[j][:, 128 * ic:128 * ic + 128],
                                               start=(ic == 0), stop=(ic == 31))
                            return ins
                        w0 = [e_w]
                        if ic == 0:
                            w0 += e_xg_ld
                            if e_qT_free is not None and r == 0 and tp == 0:
                                w0.append(e_qT_free)
                        kb.op("tensor", qmm, wait=w0)
                        wqc.set_free(sw, tick(kb, "tensor"))
                    e_pq = kb.mark("tensor")
                    for jj in range(2):
                        j = 2 * tp + jj

                        def roper(p, jj=jj, j=j):
                            p.tensor_copy(out=q_sb[:], in_=pq[jj][:, 0:512])
                            p.drain()
                            kview = q_sb[:].rearrange("p (h d) -> p h d", h=4)
                            cos8 = rqc_sb[j][:].rearrange("p (o d) -> p o d", o=1) \
                                .broadcast_to([128, 4, HD])
                            k4 = q_sb[:].rearrange("p (h pr two) -> p h pr two",
                                                   h=4, two=2)
                            o4 = qt2[:].rearrange("p (h pr two) -> p h pr two",
                                                  h=4, two=2)
                            s2 = rqs_sb[j][:].rearrange("p (pr two) -> p pr two", two=2)
                            p.tensor_tensor(out=qt1[:].rearrange("p (h d) -> p h d", h=4),
                                            in0=kview, in1=cos8, op=ALU.mult)
                            p.tensor_tensor(out=o4[:, :, :, 0:1], in0=k4[:, :, :, 1:2],
                                            in1=s2[:, :, 0:1].unsqueeze(1)
                                            .broadcast_to([128, 4, HD // 2, 1]),
                                            op=ALU.mult)
                            p.tensor_tensor(out=o4[:, :, :, 1:2], in0=k4[:, :, :, 0:1],
                                            in1=s2[:, :, 1:2].unsqueeze(1)
                                            .broadcast_to([128, 4, HD // 2, 1]),
                                            op=ALU.mult)
                            p.drain()
                            p.tensor_tensor(out=qrbf[:], in0=qt1[:], in1=qt2[:],
                                            op=ALU.add)
                            return p.drain()
                        kb.op("vector", roper, wait=e_pq)
                        e_rope = kb.mark("vector")

                        def qtr4(p, jj=jj):
                            for hq in range(4):
                                ins = p.matmul(
                                    out=ptq[jj][:, 128 * hq:128 * hq + 128],
                                    lhsT=qrbf[:, 128 * hq:128 * hq + 128],
                                    rhs=ident_bf[:], start=True, stop=True)
                            return ins
                        kb.op("tensor", qtr4, wait=[e_rope, e_prep])
                        e_qtr = kb.mark("tensor")

                        def qev(p, jj=jj, j=j, r=r):
                            for hq in range(4):
                                head = 4 * r + hq
                                ins = p.tensor_copy(
                                    out=qT_sb[:, 512 * head + 128 * j:
                                              512 * head + 128 * j + 128],
                                    in_=ptq[jj][:, 128 * hq:128 * hq + 128])
                            return ins
                        kb.op("vector", qev, wait=e_qtr)
                        e_qev = kb.mark("vector")
            e_stageA = e_qev
            if stop == "A":
                ev = kb.dma("gpsimd", x1_out[0:128, 0:128],
                            qT_sb[:, 0:256].bitcast(F32), wait=e_stageA, sem="stdbg")
                kb.wait("gpsimd", ev)
                kb.build(block)
                return nc

            nkc = 8 * G + 8
            e_ctx_done = []
            for kvg in range(NKV):
                bk, bv = kt_sb, v_sb
                evkv = e_kv_free
                e_k = None
                for kk in range(nkc):
                    e_k = kb.dma("sync", bk[:, 128 * kk:128 * kk + 128],
                                 kT0[kvg, :, 128 * kk:128 * kk + 128],
                                 wait=evkv if kk == 0 else None, sem="sk")
                e_v = None
                for kk in range(nkc):
                    e_v = kb.dma("sync", bv[:, 128 * kk:128 * kk + 128],
                                 v0[kvg, 128 * kk:128 * kk + 128, :],
                                 wait=evkv if kk == 0 else None, sem="sv")

                def maxk_v(p, bk=bk, nkc=nkc):
                    p.tensor_reduce(out=maxkf[:], in_=bk[:, 0:128 * nkc], axis=AX.X,
                                    op=ALU.max, apply_absolute_value=True)
                    p.drain()
                    return p.tensor_copy(out=maxkb[:], in_=maxkf[:])
                kb.op("vector", maxk_v, wait=e_k)
                e_maxk = kb.mark("vector")

                for z in range(4):
                    def absq_s(p, kvg=kvg, z=z):
                        return p.activation(
                            out=absq[:],
                            in_=qT_sb[:, 2048 * kvg + 512 * z:2048 * kvg + 512 * z + 512],
                            func=AF.Abs)
                    kb.op("scalar", absq_s,
                          wait=[e_stageA] + ([e_bz] if z else []))
                    e_absq = kb.mark("scalar")

                    def bz(p, z=z):
                        return p.matmul(out=pool[0:1, 3072:3584], lhsT=maxkb[:],
                                        rhs=absq[:], start=True, stop=True)
                    kb.op("tensor", bz, wait=[e_maxk, e_absq] +
                          ([e_bnd_ev] if e_bnd_ev is not None else []))
                    e_bz = kb.mark("tensor")

                    def bred(p, z=z):
                        p.tensor_copy(out=sum_sb[:], in_=pool[0:1, 3072:3584])
                        p.drain()
                        p.tensor_reduce(out=btmp[:], in_=sum_sb[:], axis=AX.X,
                                        op=ALU.max)
                        p.drain()
                        if z == 0:
                            return p.tensor_copy(out=bstar[:], in_=btmp[:])
                        return p.tensor_tensor(out=bstar[:], in0=bstar[:],
                                               in1=btmp[:], op=ALU.max)
                    kb.op("vector", bred, wait=e_bz)
                    e_bnd_ev = kb.mark("vector")

                def bstar_v(p):
                    return p.tensor_scalar(out=bstar_b[:], in0=bstar[:], scalar1=-1.0,
                                           scalar2=None, op0=ALU.mult)
                kb.op("vector", bstar_v, wait=e_bnd_ev)
                e_bstar = kb.mark("vector")

                def nbmm(p):
                    return p.matmul(out=pnb, lhsT=ones_row[:], rhs=bstar_b[:],
                                    start=True, stop=True)
                kb.op("tensor", nbmm, wait=e_bstar)
                e_nb = kb.mark("tensor")
                kb.op("vector", lambda p: p.tensor_copy(out=nb_sb[:], in_=pnb),
                      wait=e_nb)
                e_nbs = kb.mark("vector")

                for qh in range(4):
                    head = 4 * kvg + qh
                    qcol = 512 * head
                    n_batches = (nkc + 3) // 4
                    for bi in range(n_batches):
                        kchunk = list(range(4 * bi, min(4 * bi + 4, nkc)))

                        def smm(p, kchunk=kchunk, bk=bk, qcol=qcol):
                            for ii, kc in enumerate(kchunk):
                                ins = p.matmul(out=pst[:, 512 * ii:512 * ii + 512],
                                               lhsT=bk[:, 128 * kc:128 * kc + 128],
                                               rhs=qT_sb[:, qcol:qcol + 512],
                                               start=True, stop=True)
                            return ins
                        kb.op("tensor", smm, wait=[e_nbs, e_k])
                        e_smm = kb.mark("tensor")

                        bpt, evpt = ptb.get()
                        spt = ptb.slot

                        def expo(p, bpt=bpt, n=len(kchunk)):
                            return p.activation(out=bpt[:, 0:512 * n],
                                                in_=pst[:, 0:512 * n],
                                                func=AF.Exp, bias=nb_sb[:])
                        kb.op("scalar", expo, wait=[e_smm, evpt])
                        e_exp = kb.mark("scalar")

                        mask_ops = []
                        for ii, kc in enumerate(kchunk):
                            for j in range(4):
                                b0 = 8 * G + 2 * j
                                if kc == b0:
                                    mask_ops.append((ii, j, m0_sb))
                                elif kc == b0 + 1:
                                    mask_ops.append((ii, j, m1_sb))
                        if mask_ops:
                            def domask(p, mask_ops=mask_ops, bpt=bpt):
                                for ii, j, msk in mask_ops:
                                    sl = bpt[:, 512 * ii + 128 * j:
                                             512 * ii + 128 * j + 128]
                                    ins = p.tensor_tensor(out=sl, in0=sl, in1=msk[:],
                                                          op=ALU.mult)
                                return p.drain()
                            kb.op("vector", domask, wait=e_exp)
                            e_pt_ready = kb.mark("vector")
                        else:
                            e_pt_ready = e_exp

                        def pvmm(p, kchunk=kchunk, bv=bv, G=G, bpt=bpt,
                                 first=(kchunk[0] == 0), last=(kchunk[-1] == nkc - 1)):
                            for kx, kc in enumerate(kchunk):
                                jmin = min(3, max(0, -(-(kc - 8 * G - 1) // 2)))
                                s0 = 128 * jmin
                                p.matmul(out=pctx[:, s0:512],
                                         lhsT=bv[:, 128 * kc:128 * kc + 128],
                                         rhs=bpt[:, 512 * kx + s0:512 * kx + 512],
                                         start=(first and kx == 0),
                                         stop=(last and kx == len(kchunk) - 1))
                                ins = p.matmul(out=psum_s[:, s0:512],
                                               lhsT=ones_col[:],
                                               rhs=bpt[:, 512 * kx + s0:512 * kx + 512],
                                               start=(first and kx == 0),
                                               stop=(last and kx == len(kchunk) - 1))
                            return ins
                        if stop == "SB" and kvg == 0 and qh == 0 and bi == 0:
                            ev = kb.dma("gpsimd", x1_out[0:128, 0:1024],
                                        bpt[:].bitcast(F32), wait=e_pt_ready,
                                        sem="stdbg")
                            kb.wait("gpsimd", ev)
                            kb.build(block)
                            return nc
                        kb.op("tensor", pvmm, wait=[e_pt_ready, e_v])
                        e_pv_t = tick(kb, "tensor")
                        ptb.set_free(spt, e_pv_t)
                    e_pv = kb.mark("tensor")

                    def sums_v(p):
                        p.tensor_copy(out=sum_sb[:], in_=psum_s)
                        p.drain()
                        p.reciprocal(out=sum_sb[:], in_=sum_sb[:])
                        p.drain()
                        return p.tensor_copy(out=rec_bf[:], in_=sum_sb[:])
                    kb.op("vector", sums_v, wait=e_pv)
                    e_rec = kb.mark("vector")

                    def recmm(p):
                        return p.matmul(out=prec, lhsT=ones_row[:], rhs=rec_bf[:],
                                        start=True, stop=True)
                    kb.op("tensor", recmm, wait=e_rec)
                    e_recb = kb.mark("tensor")

                    def ctxev(p, qcol=qcol):
                        p.tensor_copy(out=recb_sb[:], in_=prec)
                        p.drain()
                        return p.tensor_tensor(out=ctx_sb[:, qcol:qcol + 512],
                                               in0=pctx, in1=recb_sb[:], op=ALU.mult)
                    w = [e_recb]
                    if e_ctx_free is not None and head == 0:
                        w.append(e_ctx_free)
                    kb.op("vector", ctxev, wait=w)
                    e_ctx_done.append(kb.mark("vector"))
                e_kv_free = e_ctx_done[-1]
            e_stageB = e_ctx_done[-1]
            e_qT_free = e_stageB
            if stop == "B":
                ev = kb.dma("gpsimd", x1_out[0:128, 0:128],
                            ctx_sb[:, 0:256].bitcast(F32), wait=e_stageB, sem="stdbg")
                kb.wait("gpsimd", ev)
                kb.build(block)
                return nc

            e_last_op_mm = None
            for pair in range(2):
                ts_ = [4 * G + 2 * pair, 4 * G + 2 * pair + 1]
                e_mk = [kb.dma("sync", mk_sb[ii][:],
                               maskvo[128 * t:128 * t + 128]
                               .rearrange("(p one) -> p one", one=1),
                               wait=ln_free[ii], sem=f"smk{ii}")
                        for ii, t in enumerate(ts_)]

                def omv(p):
                    for ii in range(2):
                        ins = p.tensor_scalar(out=om_sb[ii][:], in0=mk_sb[ii][:],
                                              scalar1=-1.0, scalar2=1.0,
                                              op0=ALU.mult, op1=ALU.add)
                    return ins
                kb.op("vector", omv, wait=e_mk)
                e_om = kb.mark("vector")

                e_x1 = []
                for qtr in range(4):
                    hc = 1024 * qtr
                    e_bld = kb.dma("sync", bias_sb[:], bias_dram[0:1, hc:hc + 1024],
                                   wait=e_bias_used, sem="sbia")
                    e_xo = []
                    for ii, t in enumerate(ts_):
                        e_xo.append(kb.dma("sync", u_sb[ii][:],
                                           xo_in[128 * t:128 * t + 128, hc:hc + 1024],
                                           wait=(e_x1[-1] if e_x1 else
                                                 (e_prev_comb if e_prev_comb
                                                  else e_prep)), sem=f"sxo{ii}"))
                    for ic in range(32):
                        bw, evw = wch.get()
                        sw = wch.slot
                        e_w = kb.dma("sync", bw[:],
                                     wo0b[128 * ic:128 * ic + 128, hc:hc + 1024],
                                     wait=evw, sem=wch.sem())

                        def opmm(p, bw=bw, ic=ic, ts_=ts_, G=G):
                            for ii, t in enumerate(ts_):
                                j = t - 4 * G
                                for hx in range(2):
                                    ins = p.matmul(
                                        out=pa[ii][:, 512 * hx:512 * hx + 512],
                                        lhsT=ctx_sb[:, 512 * ic + 128 * j:
                                                    512 * ic + 128 * j + 128],
                                        rhs=bw[:, 512 * hx:512 * hx + 512],
                                        start=(ic == 0), stop=(ic == 31))
                            return ins
                        w_op = [e_w]
                        if ic == 0:
                            w_op.append(e_stageB)
                            if e_x1:
                                w_op.append(e_x1[-1])
                            elif e_prev_comb is not None:
                                w_op.append(e_prev_comb)
                        kb.op("tensor", opmm, wait=w_op)
                        wch.set_free(sw, tick(kb, "tensor"))
                    e_op = kb.mark("tensor")

                    def uev(p):
                        p.tensor_tensor(out=u_sb[0][:], in0=pa[0], in1=u_sb[0][:],
                                        op=ALU.add)
                        ins = p.tensor_tensor(out=u_sb[1][:], in0=pa[1], in1=u_sb[1][:],
                                              op=ALU.add)
                        return ins
                    kb.op("vector", uev, wait=[e_op] + e_xo)
                    e_u = kb.mark("vector")

                    for ic in range(32):
                        bw, evw = wch.get()
                        sw = wch.slot
                        e_w = kb.dma("sync", bw[:],
                                     wlT[128 * ic:128 * ic + 128, hc:hc + 1024],
                                     wait=evw, sem=wch.sem())

                        def ylmm(p, bw=bw, ic=ic, ts_=ts_, G=G):
                            for ii, t in enumerate(ts_):
                                for hx in range(2):
                                    ins = p.matmul(
                                        out=pa[ii][:, 512 * hx:512 * hx + 512],
                                        lhsT=xg[t - 4 * G][:, 128 * ic:128 * ic + 128],
                                        rhs=bw[:, 512 * hx:512 * hx + 512],
                                        start=(ic == 0), stop=False)
                            return ins
                        kb.op("tensor", ylmm, wait=[e_w] + ([e_u] if ic == 0 else []))
                        wch.set_free(sw, tick(kb, "tensor"))

                    def ylbias(p):
                        for ii in range(2):
                            for hx in range(2):
                                ins = p.matmul(
                                    out=pa[ii][:, 512 * hx:512 * hx + 512],
                                    lhsT=ones_row[:],
                                    rhs=bias_sb[0:1, 512 * hx:512 * hx + 512],
                                    start=False, stop=True)
                        return ins
                    kb.op("tensor", ylbias, wait=e_bld)
                    e_yl = kb.mark("tensor")

                    def ylev(p):
                        p.tensor_copy(out=yl_sb[0][:], in_=pa[0])
                        return p.tensor_copy(out=yl_sb[1][:], in_=pa[1])
                    kb.op("vector", ylev, wait=e_yl)
                    e_ylev = kb.mark("vector")

                    for ic in range(32):
                        bw, evw = wch.get()
                        sw = wch.slot
                        e_w = kb.dma("sync", bw[:],
                                     whT[128 * ic:128 * ic + 128, hc:hc + 1024],
                                     wait=evw, sem=wch.sem())

                        def yhmm(p, bw=bw, ic=ic, ts_=ts_, G=G):
                            for ii, t in enumerate(ts_):
                                for hx in range(2):
                                    ins = p.matmul(
                                        out=pa[ii][:, 512 * hx:512 * hx + 512],
                                        lhsT=xg[t - 4 * G][:, 128 * ic:128 * ic + 128],
                                        rhs=bw[:, 512 * hx:512 * hx + 512],
                                        start=(ic == 0), stop=False)
                            return ins
                        kb.op("tensor", yhmm, wait=[e_w] + ([e_ylev] if ic == 0 else []))
                        wch.set_free(sw, tick(kb, "tensor"))

                    def yhbias(p):
                        for ii in range(2):
                            for hx in range(2):
                                ins = p.matmul(
                                    out=pa[ii][:, 512 * hx:512 * hx + 512],
                                    lhsT=ones_row[:],
                                    rhs=bias_sb[0:1, 512 * hx:512 * hx + 512],
                                    start=False, stop=True)
                        return ins
                    kb.op("tensor", yhbias, wait=e_bld)
                    e_bias_used = tick(kb, "tensor")
                    e_yh = kb.mark("tensor")

                    def comb(p, hc=hc):
                        p.tensor_scalar(out=yl_sb[0][:], in0=yl_sb[0][:],
                                        scalar1=om_sb[0][:], scalar2=None, op0=ALU.mult)
                        p.tensor_scalar(out=yl_sb[1][:], in0=yl_sb[1][:],
                                        scalar1=om_sb[1][:], scalar2=None, op0=ALU.mult)
                        p.scalar_tensor_tensor(out=yl_sb[0][:], in0=pa[0],
                                               scalar=mk_sb[0][:], in1=yl_sb[0][:],
                                               op0=ALU.mult, op1=ALU.add)
                        p.scalar_tensor_tensor(out=yl_sb[1][:], in0=pa[1],
                                               scalar=mk_sb[1][:], in1=yl_sb[1][:],
                                               op0=ALU.mult, op1=ALU.add)
                        p.tensor_tensor(out=x1_sb[0][:, hc:hc + 1024], in0=yl_sb[0][:],
                                        in1=u_sb[0][:], op=ALU.add)
                        ins = p.tensor_tensor(out=x1_sb[1][:, hc:hc + 1024],
                                              in0=yl_sb[1][:], in1=u_sb[1][:],
                                              op=ALU.add)
                        return ins
                    kb.op("vector", comb, wait=[e_yh, e_om, e_ylev])
                    e_x1.append(kb.mark("vector"))
                    if stop == "C" + str(qtr) and pair == 0:
                        ev = kb.dma("gpsimd", x1_out[0:128, 0:1024],
                                    x1_sb[0][:, 0:1024], wait=e_x1[-1], sem="stdbg")
                        kb.wait("gpsimd", ev)
                        kb.build(block)
                        return nc
                e_ffn_done = e_x1[-1]
                e_prev_comb = e_ffn_done
                if stop == "CF" and pair == 0:
                    ev = kb.dma("gpsimd", x1_out[0:128, :], x1_sb[0][:],
                                wait=e_ffn_done, sem="stdbg")
                    kb.wait("gpsimd", ev)
                    kb.build(block)
                    return nc
                for j in (2 * pair, 2 * pair + 1):
                    e_xg[j] = e_ffn_done

                for ii, t in enumerate(ts_):
                    store_evs.append(kb.dma("gpsimd", x1_out[128 * t:128 * t + 128, :],
                                            x1_sb[ii][:], wait=e_ffn_done,
                                            sem=f"stx1_{ii}"))
                    e_x1st = store_evs[-1]

                    def stats_v(p, ii=ii):
                        p.tensor_reduce(out=ssum[:], in_=x1_sb[ii][:], axis=AX.X,
                                        op=ALU.add)
                        p.drain()
                        p.tensor_scalar(out=negm[:], in0=ssum[:], scalar1=-1.0 / D,
                                        scalar2=None, op0=ALU.mult)
                        return p.drain()
                    kb.op("vector", stats_v, wait=e_ffn_done)
                    e_negm = kb.mark("vector")

                    def stats_s(p, ii=ii):
                        p.activation(out=xn1bf[:], in_=x1_sb[ii][:], func=AF.Square,
                                     accum_out=ssq[:])
                        p.drain()
                        return p.activation(out=msq[:], in_=negm[:], func=AF.Square)
                    kb.op("scalar", stats_s, wait=[e_ffn_done, e_negm])
                    e_msq = kb.mark("scalar")

                    def var_v(p):
                        return p.scalar_tensor_tensor(out=var[:], in0=ssq[:],
                                                      scalar=1.0 / D, in1=msq[:],
                                                      op0=ALU.mult, op1=ALU.subtract)
                    kb.op("vector", var_v, wait=e_msq)
                    e_var = kb.mark("vector")

                    def std_s(p):
                        return p.activation(out=stdv[:], in_=var[:], func=AF.Sqrt,
                                            bias=epsb[:])
                    kb.op("scalar", std_s, wait=e_var)
                    e_std = kb.mark("scalar")

                    def xn_v(p, ii=ii):
                        p.reciprocal(out=rstd[:], in_=stdv[:])
                        p.drain()
                        return p.tensor_scalar(out=xn1bf[:], in0=x1_sb[ii][:],
                                               scalar1=negm[:], scalar2=rstd[:],
                                               op0=ALU.add, op1=ALU.mult)
                    kb.op("vector", xn_v, wait=e_std)
                    e_xn1 = kb.mark("vector")
                    ln_free[ii] = [e_xn1, e_x1st]
                    if stop == "L1" and pair == 0 and ii == 0:
                        ev = kb.dma("gpsimd", x1_out[0:128, 0:D // 2], xn1bf[:].bitcast(F32),
                                    wait=e_xn1, sem="stdbg")
                        kb.wait("gpsimd", [ev, e_x1st])
                        kb.build(block)
                        return nc

                    e_tr_ev = [None, None]
                    for half2 in range(1 if stop in ("L2A", "L2B") else 2):
                        if half2 == 0 and ii == 0 and pair == 0 and G == 0:
                            pass
                        bxt = ptb.bufs[half2]
                        evxt = ptb.free[half2]
                        for g4 in range(4):
                            grp = 4 * half2 + g4

                            def tr(p, grp=grp):
                                for q4 in range(4):
                                    icb = 4 * grp + q4
                                    ins = p.matmul(
                                        out=ptq[grp % 2][:, 128 * q4:128 * q4 + 128],
                                        lhsT=xn1bf[:, 128 * icb:128 * icb + 128],
                                        rhs=ident_bf[:], start=True, stop=True)
                                return ins
                            kb.op("tensor", tr,
                                  wait=[e_xn1] + ([ev_evac[grp % 2]]
                                                  if ev_evac[grp % 2] else []))
                            e_trm = kb.mark("tensor")
                            kb.op("vector",
                                  lambda p, bxt=bxt, g4=g4, grp=grp:
                                  p.tensor_copy(out=bxt[:, 512 * g4:512 * g4 + 512],
                                                in_=ptq[grp % 2]),
                                  wait=[e_trm] + ([evxt] if g4 == 0 and evxt else []))
                            e_tr_ev[half2] = kb.mark("vector")
                            ev_evac[grp % 2] = e_tr_ev[half2]
                        if stop != "L2B":
                            store_evs.append(kb.dma(
                                "gpsimd", xnT1[t][:, 2048 * half2:2048 * half2 + 2048],
                                bxt[:], wait=e_tr_ev[half2], sem=f"stxn{half2}"))
                            ptb.free[half2] = store_evs[-1]
                    e_tr_ev = e_tr_ev[0] if stop in ("L2A", "L2B") else e_tr_ev[1]
                    if stop in ("L2", "L2A", "L2B") and pair == 0 and ii == 0:
                        if stop == "L2B":
                            ev = kb.dma("gpsimd", x1_out[0:128, 0:1024],
                                        ptb.bufs[0][:].bitcast(F32), wait=e_tr_ev,
                                        sem="stdbg")
                            kb.wait("gpsimd", [ev] + store_evs)
                            kb.build(block)
                            return nc
                        kb.wait("gpsimd", store_evs)
                        kb.build(block)
                        return nc
                    def msum_mm(p):
                        for icb in range(32):
                            ins = p.matmul(out=pmsum[:, icb:icb + 1],
                                           lhsT=xn1bf[:, 128 * icb:128 * icb + 128],
                                           rhs=ones_col[:], start=True, stop=True)
                        return ins
                    kb.op("tensor", msum_mm, wait=e_tr_ev)
                    e_msmm = kb.mark("tensor")

                    def msum_acc(p):
                        p.tensor_copy(out=msev[:], in_=pmsum)
                        p.drain()
                        return p.tensor_tensor(out=msum_f[:], in0=msum_f[:],
                                               in1=msev[:], op=ALU.add)
                    kb.op("vector", msum_acc, wait=e_msmm)
                    e_msacc = kb.mark("vector")
                if stop == "LN" and pair == 0:
                    ev = kb.dma("gpsimd", x1_out[0:128, 0:32], msum_f[:],
                                wait=e_msacc, sem="stdbg")
                    kb.wait("gpsimd", ev)
                    kb.wait("gpsimd", store_evs)
                    kb.build(block)
                    return nc
            e_ctx_free = e_op

        store_evs.append(kb.dma("gpsimd", msum_out, msum_f[:], wait=e_msacc,
                                sem="stms"))
        kb.wait("gpsimd", store_evs)
        kb.build(block)
    return nc




def build_pd():
    nc = bass.Bass("TRN2", target_bir_lowering=False)
    kcache = nc.dram_tensor("kcache", [512, 16384], F32, kind="ExternalInput").ap()
    vcache = nc.dram_tensor("vcache", [512, 16384], F32, kind="ExternalInput").ap()
    bt = nc.dram_tensor("bt", [256], I32, kind="ExternalInput").ap()
    msum_pair = nc.dram_tensor("msum_pair", [2, 128, 32], F32, kind="ExternalInput").ap()
    wqfb = nc.dram_tensor("wqfb", [D, KQ], BF16, kind="ExternalInput").ap()
    keysb = nc.dram_tensor("keysb", [KQ, E], BF16, kind="ExternalInput").ap()
    identin = nc.dram_tensor("identin", [128, 128], F32, kind="ExternalInput").ap()
    kT1 = nc.dram_tensor("kT1", [NKV, HD, S], BF16, kind="ExternalOutput").ap()
    v1 = nc.dram_tensor("v1", [NKV, S, HD], BF16, kind="ExternalOutput").ap()
    coeffs = nc.dram_tensor("coeffs", [1, E], F32, kind="ExternalOutput").ap()
    kg_d = nc.dram_tensor("kg_d", [256, 16384], F32, kind="Internal").ap()
    vg_d = nc.dram_tensor("vg_d", [256, 16384], F32, kind="Internal").ap()

    with ExitStack() as ctx:
        sb = lambda n, sh, dt: ctx.enter_context(nc.sbuf_tensor(n, sh, dt))
        gbuf = Rot([sb(f"gbuf{i}", [128, 4096], F32) for i in range(2)], "sg")
        idxs = sb("idxs", [128, 2], I32)
        ktile = Rot([sb(f"ktile{i}", [128, 128], F32) for i in range(2)], "skt")
        kbf = sb("kbf", [128, 128], BF16)
        kbf2 = sb("kbf2", [128, 128], BF16)
        ktr_b = Rot([sb(f"ktr{i}", [128, 128], BF16) for i in range(2)])
        ms0 = sb("ms0", [128, 32], F32)
        ms1 = sb("ms1", [128, 32], F32)
        xbar = sb("xbar", [128, 32], BF16)
        wqf_sb = sb("wqf_sb", [128, 32 * KQ], BF16)
        qv = sb("qv", [128, 1], BF16)
        keys_sb = sb("keys_sb", [128, E], BF16)
        sc8 = sb("sc8", [1, E], F32)
        mx8 = sb("mx8", [1, 1], F32)
        nmx8 = sb("nmx8", [1, 1], F32)
        ex8 = sb("ex8", [1, E], F32)
        sm8 = sb("sm8", [1, 1], F32)
        co8 = sb("co8", [1, E], F32)
        identf = sb("identf", [128, 128], F32)
        ident_bf = sb("ident_bf", [128, 128], BF16)
        warm_sb = sb("warm_sb", [128, 128], F32)
        warm_bf = sb("warm_bf", [128, 128], BF16)
        warm_o = sb("warm_o", [128, 128], F32)
        pool_t = ctx.enter_context(nc.psum_tensor("pool", [128, 2048], F32))
        pool = pool_t[:]
        warm_ps = pool[:, 0:128]
        ptr = [pool[:, 512:640], pool[:, 640:768]]
        pq128 = pool[:, 1024:1025]
        psc = pool[0:1, 1536:1544]

        block = ctx.enter_context(nc.Block())
        kb = KB(nc, ctx)
        emit_warmup(kb, warm_bf, warm_ps, warm_sb, warm_o)

        e_id = kb.dma("sync", identf[:], identin, sem="sprep")
        e_bt0 = kb.dma("sync", idxs[:, 0:1],
                       bt[0:128].rearrange("(p one) -> p one", one=1), sem="sprep")
        e_bt = kb.dma("sync", idxs[:, 1:2],
                      bt[128:256].rearrange("(p one) -> p one", one=1), sem="sprep")
        kb.op("vector", lambda p: p.tensor_copy(out=ident_bf[:], in_=identf[:]),
              wait=e_id)
        e_prep = kb.mark("vector")

        e_g = []
        for src, dst in [(kcache, kg_d), (vcache, vg_d)]:
            for hh in range(2):
                for z in range(4):
                    bg, evg = gbuf.get()
                    sg_ = gbuf.slot

                    def gat(p, src=src, bg=bg, hh=hh, z=z):
                        return p.indirect_dma_start(
                            out=bg[:], out_offset=None,
                            in_=src,
                            in_offset=IndirectOffsetOnAxis(idxs[:, hh:hh + 1], 0),
                            element_offset=4096 * z)
                    kb.wait("gpsimd", [e_bt] + ([evg] if evg else []))
                    ev1 = kb._bump("sgat", 16)
                    hname = kb.sem("sgat")[0]
                    kb.q["gpsimd"].append(
                        lambda p, fn=gat, hname=hname: fn(p).then_inc(hname, 16))
                    ev2 = kb.dma("gpsimd", dst[128 * hh:128 * hh + 128,
                                              4096 * z:4096 * z + 4096],
                                 bg[:], wait=ev1, sem="sgst")
                    gbuf.set_free(sg_, ev2)
                    e_g.append(ev2)

        st_evs = []
        ev_tr_ev = [None, None]
        e_vst_prev = None
        e_tr_prev = None
        for kvh in range(NKV):
            for ch in range(32):
                bkt, evkt = ktile.get()
                skt_ = ktile.slot
                e_t = kb.dma("sync", bkt[:],
                             vg_d.rearrange("(c q) (h ps d) -> c q h ps d",
                                            q=8, h=NKV, ps=16)[ch][:, kvh, :, :],
                             wait=([evkt] + e_g) if (ch == 0 and kvh == 0) else evkt,
                             sem=ktile.sem())

                def cast_v(p, bkt=bkt):
                    return p.tensor_copy(out=kbf[:], in_=bkt[:])
                kb.op("vector", cast_v, wait=[e_t] + ([e_vst_prev] if e_vst_prev else []))
                e_c = kb.mark("vector")
                ktile.set_free(skt_, e_c)
                st_evs.append(kb.dma("gpsimd", v1[kvh, 128 * ch:128 * ch + 128, :],
                                     kbf[:], wait=e_c, sem="stv"))
                e_vst_prev = st_evs[-1]

                bkt2, evkt2 = ktile.get()
                skt2_ = ktile.slot
                e_t2 = kb.dma("sync", bkt2[:],
                              kg_d.rearrange("(c q) (h ps d) -> c q h ps d",
                                             q=8, h=NKV, ps=16)[ch][:, kvh, :, :],
                              wait=evkt2, sem=ktile.sem())

                def cast_k(p, bkt2=bkt2):
                    return p.tensor_copy(out=kbf2[:], in_=bkt2[:])
                kb.op("vector", cast_k,
                      wait=[e_t2] + ([e_tr_prev] if e_tr_prev else []))
                e_c2 = kb.mark("vector")
                ktile.set_free(skt2_, e_c2)

                def trk(p, ch=ch):
                    return p.matmul(out=ptr[ch % 2], lhsT=kbf2[:], rhs=ident_bf[:],
                                    start=True, stop=True)
                kb.op("tensor", trk, wait=[e_c2, e_prep] +
                      ([ev_tr_ev[ch % 2]] if ev_tr_ev[ch % 2] else []))
                e_tr = kb.mark("tensor")
                e_tr_prev = e_tr
                btr, evtr = ktr_b.get()
                str_ = ktr_b.slot
                kb.op("vector",
                      lambda p, btr=btr, ch=ch: p.tensor_copy(out=btr[:], in_=ptr[ch % 2]),
                      wait=[e_tr, evtr])
                e_te = kb.mark("vector")
                ev_tr_ev[ch % 2] = e_te
                st_evs.append(kb.dma("gpsimd", kT1[kvh, :, 128 * ch:128 * ch + 128],
                                     btr[:], wait=e_te, sem="stk"))
                ktr_b.set_free(str_, st_evs[-1])

        e_m0 = kb.dma("sync", ms0[:], msum_pair[0], sem="sms")
        e_m1 = kb.dma("sync", ms1[:], msum_pair[1], sem="sms")
        e_wqf = None
        for z in range(32):
            e_wqf = kb.dma("sync", wqf_sb[:, KQ * z:KQ * z + KQ],
                           wqfb[128 * z:128 * z + 128, :], sem="swqf")
        e_keys = kb.dma("sync", keys_sb[:], keysb, sem="swqf")

        def xbar_v(p):
            p.tensor_tensor(out=ms0[:], in0=ms0[:], in1=ms1[:], op=ALU.add)
            p.drain()
            return p.tensor_scalar(out=xbar[:], in0=ms0[:], scalar1=1.0 / S,
                                   scalar2=None, op0=ALU.mult)
        kb.op("vector", xbar_v, wait=[e_m0, e_m1])
        e_xb = kb.mark("vector")

        def qmm(p):
            for z in range(32):
                ins = p.matmul(out=pq128, lhsT=wqf_sb[:, KQ * z:KQ * z + KQ],
                               rhs=xbar[:, z:z + 1], start=(z == 0), stop=(z == 31))
            return ins
        kb.op("tensor", qmm, wait=[e_xb, e_wqf])
        e_q = kb.mark("tensor")
        kb.op("vector", lambda p: p.tensor_copy(out=qv[:], in_=pq128), wait=e_q)
        e_qv = kb.mark("vector")

        def scmm(p):
            return p.matmul(out=psc, lhsT=qv[:], rhs=keys_sb[:], start=True, stop=True)
        kb.op("tensor", scmm, wait=[e_qv, e_keys])
        e_sc = kb.mark("tensor")

        def smax(p):
            p.tensor_scalar(out=sc8[:], in0=psc, scalar1=1.0 / np.sqrt(float(KQ)),
                            scalar2=None, op0=ALU.mult)
            p.drain()
            p.tensor_reduce(out=mx8[:], in_=sc8[:], axis=AX.X, op=ALU.max)
            p.drain()
            return p.tensor_scalar(out=nmx8[:], in0=mx8[:], scalar1=-1.0,
                                   scalar2=None, op0=ALU.mult)
        kb.op("vector", smax, wait=e_sc)
        e_mx = kb.mark("vector")

        def sexp(p):
            return p.activation(out=ex8[:], in_=sc8[:], func=AF.Exp, bias=nmx8[:],
                                accum_out=sm8[:])
        kb.op("scalar", sexp, wait=e_mx)
        e_ex = kb.mark("scalar")

        def sdiv(p):
            p.reciprocal(out=sm8[:], in_=sm8[:])
            p.drain()
            return p.tensor_scalar(out=co8[:], in0=ex8[:], scalar1=sm8[:],
                                   scalar2=None, op0=ALU.mult)
        kb.op("vector", sdiv, wait=e_ex)
        e_co = kb.mark("vector")
        st_evs.append(kb.dma("gpsimd", coeffs, co8[:], wait=e_co, sem="stc"))

        kb.wait("gpsimd", st_evs)
        kb.build(block)
    return nc




def build_pe(n_groups=4):
    nc = bass.Bass("TRN2", target_bir_lowering=False)
    xnT1o = nc.dram_tensor("xnT1o", [OWN_T, 128, D], BF16, kind="ExternalInput").ap()
    kT1 = nc.dram_tensor("kT1", [NKV, HD, S], BF16, kind="ExternalInput").ap()
    v1 = nc.dram_tensor("v1", [NKV, S, HD], BF16, kind="ExternalInput").ap()
    x1_in = nc.dram_tensor("x1_in", [TOK, D], F32, kind="ExternalInput").ap()
    wq1b = nc.dram_tensor("wq1b", [D, D], BF16, kind="ExternalInput").ap()
    wo1b = nc.dram_tensor("wo1b", [D, D], BF16, kind="ExternalInput").ap()
    worgb = nc.dram_tensor("worgb", [D, D], BF16, kind="ExternalInput").ap()
    downb = nc.dram_tensor("downb", [D, ER], BF16, kind="ExternalInput").ap()
    upb = nc.dram_tensor("upb", [ER, D], BF16, kind="ExternalInput").ap()
    co_in = nc.dram_tensor("co_in", [1, E], F32, kind="ExternalInput").ap()
    rqc = nc.dram_tensor("rqc", [TOK, HD], F32, kind="ExternalInput").ap()
    rqs = nc.dram_tensor("rqs", [TOK, HD], F32, kind="ExternalInput").ap()
    identin = nc.dram_tensor("identin", [128, 128], F32, kind="ExternalInput").ap()
    out_d = nc.dram_tensor("out_d", [TOK, D], F32, kind="ExternalOutput").ap()

    with ExitStack() as ctx:
        sb = lambda n, sh, dt: ctx.enter_context(nc.sbuf_tensor(n, sh, dt))
        xg = [sb(f"xg{t}", [128, D], BF16) for t in range(4)]
        wqc = Rot([sb(f"wqc{i}", [128, 512], BF16) for i in range(2)], "swqc")
        q_sb = sb("q_sb", [128, 512], F32)
        qt1 = sb("qt1", [128, 512], F32)
        qt2 = sb("qt2", [128, 512], F32)
        qrbf = sb("qrbf", [128, 512], BF16)
        rqc_sb = [sb(f"rqc{t}", [128, HD], F32) for t in range(4)]
        rqs_sb = [sb(f"rqs{t}", [128, HD], F32) for t in range(4)]
        qT_sb = sb("qT_sb", [128, NH * 512], BF16)
        kt_sb = sb("kt_sb", [128, S], BF16)
        v_sb = sb("v_sb", [128, S], BF16)
        ptb = Rot([sb(f"ptb{i}", [128, 2048], BF16) for i in range(2)])
        absq = sb("absq", [128, 512], BF16)
        maxkf = sb("maxkf", [128, 1], F32)
        maxkb = sb("maxkb", [128, 1], BF16)
        bstar = sb("bstar", [1, 1], F32)
        btmp = sb("btmp", [1, 1], F32)
        bstar_b = sb("bstar_b", [1, 1], BF16)
        nb_sb = sb("nb_sb", [128, 1], F32)
        sum_sb = sb("sum_sb", [1, 512], F32)
        rec_bf = sb("rec_bf", [1, 512], BF16)
        recb_sb = sb("recb_sb", [128, 512], F32)
        ctx_sb = sb("ctx_sb", [128, NH * 512], BF16)
        ones_row = sb("ones_row", [1, 128], BF16)
        ones_col = sb("ones_col", [128, 1], BF16)
        co_row = sb("co_row", [1, E], F32)
        coex = sb("coex", [1, ER], BF16)
        cb_sb = sb("cb_sb", [128, ER], F32)
        hsc = [sb(f"hsc{t}", [128, ER], BF16) for t in range(4)]
        hT_sb = [sb(f"hT{t}", [128, ER], BF16) for t in range(4)]
        wch = Rot([sb(f"wch{i}", [128, 1024], BF16) for i in range(3)], "swch")
        upc = sb("upc", [128, 4 * 1024], BF16)
        u_sb = [sb(f"u_sb{i}", [128, 1024], F32) for i in range(2)]
        out_sb = [sb(f"out_sb{i}", [128, 1024], F32) for i in range(2)]
        identf = sb("identf", [128, 128], F32)
        ident_bf = sb("ident_bf", [128, 128], BF16)
        warm_sb = sb("warm_sb", [128, 128], F32)
        warm_bf = sb("warm_bf", [128, 128], BF16)
        warm_o = sb("warm_o", [128, 128], F32)

        pool_t = ctx.enter_context(nc.psum_tensor("pool", [128, 4096], F32))
        pool = pool_t[:]
        pq = [pool[:, 0:512], pool[:, 512:1024]]
        ptq = [pool[:, 2048:2560], pool[:, 2560:3072]]
        pst = pool[:, 0:2048]
        pctx = pool[:, 2048:2560]
        psum_s = pool[0:1, 2560:3072]
        pnb = pool[:, 3072:3073]
        prec = pool[:, 3584:4096]
        pa = [pool[:, 0:1024], pool[:, 1024:2048]]
        ph = pool[:, 2048:2560]
        pcb = pool[:, 3584:4096]
        warm_ps = pool[:, 0:128]

        block = ctx.enter_context(nc.Block())
        kb = KB(nc, ctx)
        emit_warmup(kb, warm_bf, warm_ps, warm_sb, warm_o)

        e_id = kb.dma("sync", identf[:], identin, sem="sprep")
        e_co = kb.dma("sync", co_row[:], co_in, sem="sprep")

        def prep_v(p):
            p.tensor_copy(out=ident_bf[:], in_=identf[:])
            p.memset(ones_row[:], 1.0)
            p.memset(ones_col[:], 1.0)
            return p.tensor_copy(
                out=coex[:].rearrange("o (e r) -> o e r", e=E),
                in_=co_row[:].unsqueeze(2).broadcast_to([1, E, R]))
        kb.op("vector", prep_v, wait=[e_id, e_co])
        e_prep = kb.mark("vector")

        def cbmm(p):
            return p.matmul(out=pcb[:, 0:512], lhsT=ones_row[:], rhs=coex[:],
                            start=True, stop=True)
        kb.op("tensor", cbmm, wait=e_prep)
        e_cb = kb.mark("tensor")
        kb.op("vector", lambda p: p.tensor_copy(out=cb_sb[:], in_=pcb[:, 0:512]),
              wait=e_cb)
        e_cbs = kb.mark("vector")

        store_evs = []
        e_up_used = None
        e_xg = [None] * 4
        e_qT_free = None
        e_ctx_free = None
        e_kv_free = None
        e_out_free = [None, None]
        e_hT_free = [None] * 4
        e_prev_comb = None

        for G in range(n_groups):
            e_xg_ld = []
            for j in range(4):
                t = 4 * G + j
                e_xg_ld.append(kb.dma("sync", xg[j][:], xnT1o[t], wait=e_xg[j],
                                      sem=f"sxg{j}"))
                e_xg_ld.append(kb.dma("sync", rqc_sb[j][:], rqc[128 * t:128 * t + 128, :],
                                      sem=f"srq{j}"))
                e_xg_ld.append(kb.dma("sync", rqs_sb[j][:], rqs[128 * t:128 * t + 128, :],
                                      sem=f"srq{j}"))
            for r in range(8):
                for tp in range(2):
                    for ic in range(32):
                        bw, evw = wqc.get()
                        sw = wqc.slot
                        e_w = kb.dma("sync", bw[:],
                                     wq1b[128 * ic:128 * ic + 128,
                                          512 * r:512 * r + 512], wait=evw,
                                     sem=wqc.sem())

                        def qmm(p, bw=bw, ic=ic, tp=tp):
                            for jj in range(2):
                                j = 2 * tp + jj
                                ins = p.matmul(out=pq[jj], rhs=bw[:],
                                               lhsT=xg[j][:, 128 * ic:128 * ic + 128],
                                               start=(ic == 0), stop=(ic == 31))
                            return ins
                        w0 = [e_w]
                        if ic == 0:
                            w0 += e_xg_ld
                            if e_qT_free is not None and r == 0 and tp == 0:
                                w0.append(e_qT_free)
                        kb.op("tensor", qmm, wait=w0)
                        wqc.set_free(sw, tick(kb, "tensor"))
                    e_pq = kb.mark("tensor")
                    for jj in range(2):
                        j = 2 * tp + jj

                        def roper(p, jj=jj, j=j):
                            p.tensor_copy(out=q_sb[:], in_=pq[jj])
                            p.drain()
                            kview = q_sb[:].rearrange("p (h d) -> p h d", h=4)
                            cos4 = rqc_sb[j][:].rearrange("p (o d) -> p o d", o=1) \
                                .broadcast_to([128, 4, HD])
                            k4 = q_sb[:].rearrange("p (h two pr) -> p h two pr",
                                                   h=4, two=2)
                            o4 = qt2[:].rearrange("p (h two pr) -> p h two pr",
                                                  h=4, two=2)
                            s2 = rqs_sb[j][:].rearrange("p (two pr) -> p two pr", two=2)
                            p.tensor_tensor(out=qt1[:].rearrange("p (h d) -> p h d", h=4),
                                            in0=kview, in1=cos4, op=ALU.mult)
                            p.tensor_tensor(out=o4[:, :, 0:1, :], in0=k4[:, :, 1:2, :],
                                            in1=s2[:, 0:1, :].unsqueeze(1)
                                            .broadcast_to([128, 4, 1, HD // 2]),
                                            op=ALU.mult)
                            p.tensor_tensor(out=o4[:, :, 1:2, :], in0=k4[:, :, 0:1, :],
                                            in1=s2[:, 1:2, :].unsqueeze(1)
                                            .broadcast_to([128, 4, 1, HD // 2]),
                                            op=ALU.mult)
                            p.drain()
                            p.tensor_tensor(out=qrbf[:], in0=qt1[:], in1=qt2[:],
                                            op=ALU.add)
                            return p.drain()
                        kb.op("vector", roper, wait=e_pq)
                        e_rope = kb.mark("vector")

                        def qtr4(p, jj=jj):
                            for hq in range(4):
                                ins = p.matmul(
                                    out=ptq[jj][:, 128 * hq:128 * hq + 128],
                                    lhsT=qrbf[:, 128 * hq:128 * hq + 128],
                                    rhs=ident_bf[:], start=True, stop=True)
                            return ins
                        kb.op("tensor", qtr4, wait=[e_rope, e_prep])
                        e_qtr = kb.mark("tensor")

                        def qev(p, jj=jj, j=j, r=r):
                            for hq in range(4):
                                head = 4 * r + hq
                                ins = p.tensor_copy(
                                    out=qT_sb[:, 512 * head + 128 * j:
                                              512 * head + 128 * j + 128],
                                    in_=ptq[jj][:, 128 * hq:128 * hq + 128])
                            return ins
                        kb.op("vector", qev, wait=e_qtr)
                        e_qev = kb.mark("vector")
            e_stageA = e_qev

            e_hT = []
            for j in range(4):
                for ic in range(32):
                    bw, evw = wch.get()
                    sw = wch.slot
                    e_dn = kb.dma("sync", bw[:, 0:ER],
                                  downb[128 * ic:128 * ic + 128, :], wait=evw,
                                  sem=wch.sem())

                    def hmm(p, bw=bw, ic=ic, j=j):
                        return p.matmul(out=ph, lhsT=xg[j][:, 128 * ic:128 * ic + 128],
                                        rhs=bw[:, 0:ER], start=(ic == 0),
                                        stop=(ic == 31))
                    kb.op("tensor", hmm, wait=[e_dn] +
                          ([e_stageA] if ic == 0 else []) +
                          ([e_hT_free[j]] if ic == 0 and e_hT_free[j] else []) +
                          ([e_hT[-1]] if ic == 0 and e_hT else []))
                    wch.set_free(sw, tick(kb, "tensor"))
                e_hmm = kb.mark("tensor")

                def hscale(p, j=j):
                    return p.tensor_tensor(out=hsc[j][:], in0=ph, in1=cb_sb[:],
                                           op=ALU.mult)
                kb.op("vector", hscale, wait=[e_hmm, e_cbs])
                e_hs = kb.mark("vector")

                def htr(p, j=j):
                    for z in range(4):
                        ins = p.matmul(out=ptq[j % 2][:, 128 * z:128 * z + 128],
                                       lhsT=hsc[j][:, 128 * z:128 * z + 128],
                                       rhs=ident_bf[:], start=True, stop=True)
                    return ins
                kb.op("tensor", htr, wait=[e_hs])
                e_htr = kb.mark("tensor")

                def hev(p, j=j):
                    return p.tensor_copy(out=hT_sb[j][:], in_=ptq[j % 2])
                kb.op("vector", hev, wait=e_htr)
                e_hT.append(kb.mark("vector"))

            nkc = 32
            e_ctx_done = []
            for kvg in range(NKV):
                bk, bv = kt_sb, v_sb
                evkv = e_kv_free
                e_k = None
                for kk in range(nkc):
                    e_k = kb.dma("sync", bk[:, 128 * kk:128 * kk + 128],
                                 kT1[kvg, :, 128 * kk:128 * kk + 128],
                                 wait=evkv if kk == 0 else None, sem="sk")
                e_v = None
                for kk in range(nkc):
                    e_v = kb.dma("sync", bv[:, 128 * kk:128 * kk + 128],
                                 v1[kvg, 128 * kk:128 * kk + 128, :],
                                 wait=evkv if kk == 0 else None, sem="sv")

                def maxk_v(p, bk=bk):
                    p.tensor_reduce(out=maxkf[:], in_=bk[:], axis=AX.X,
                                    op=ALU.max, apply_absolute_value=True)
                    p.drain()
                    return p.tensor_copy(out=maxkb[:], in_=maxkf[:])
                kb.op("vector", maxk_v, wait=e_k)
                e_maxk = kb.mark("vector")

                e_bnd_ev = None
                for z in range(4):
                    def absq_s(p, kvg=kvg, z=z):
                        return p.activation(
                            out=absq[:],
                            in_=qT_sb[:, 2048 * kvg + 512 * z:2048 * kvg + 512 * z + 512],
                            func=AF.Abs)
                    kb.op("scalar", absq_s,
                          wait=[e_stageA] + ([e_bz] if z else []))
                    e_absq = kb.mark("scalar")

                    def bz(p, z=z):
                        return p.matmul(out=pool[0:1, 3072:3584], lhsT=maxkb[:],
                                        rhs=absq[:], start=True, stop=True)
                    kb.op("tensor", bz, wait=[e_maxk, e_absq] +
                          ([e_bnd_ev] if e_bnd_ev is not None else []))
                    e_bz = kb.mark("tensor")

                    def bred(p, z=z):
                        p.tensor_copy(out=sum_sb[:], in_=pool[0:1, 3072:3584])
                        p.drain()
                        p.tensor_reduce(out=btmp[:], in_=sum_sb[:], axis=AX.X,
                                        op=ALU.max)
                        p.drain()
                        if z == 0:
                            return p.tensor_copy(out=bstar[:], in_=btmp[:])
                        return p.tensor_tensor(out=bstar[:], in0=bstar[:],
                                               in1=btmp[:], op=ALU.max)
                    kb.op("vector", bred, wait=e_bz)
                    e_bnd_ev = kb.mark("vector")

                def bstar_v(p):
                    return p.tensor_scalar(out=bstar_b[:], in0=bstar[:], scalar1=-1.0,
                                           scalar2=None, op0=ALU.mult)
                kb.op("vector", bstar_v, wait=e_bnd_ev)
                e_bstar = kb.mark("vector")

                def nbmm(p):
                    return p.matmul(out=pnb, lhsT=ones_row[:], rhs=bstar_b[:],
                                    start=True, stop=True)
                kb.op("tensor", nbmm, wait=e_bstar)
                e_nb = kb.mark("tensor")
                kb.op("vector", lambda p: p.tensor_copy(out=nb_sb[:], in_=pnb),
                      wait=e_nb)
                e_nbs = kb.mark("vector")

                for qh in range(4):
                    head = 4 * kvg + qh
                    qcol = 512 * head
                    for bi in range(8):
                        kchunk = list(range(4 * bi, 4 * bi + 4))

                        def smm(p, kchunk=kchunk, bk=bk, qcol=qcol):
                            for ii, kc in enumerate(kchunk):
                                ins = p.matmul(out=pst[:, 512 * ii:512 * ii + 512],
                                               lhsT=bk[:, 128 * kc:128 * kc + 128],
                                               rhs=qT_sb[:, qcol:qcol + 512],
                                               start=True, stop=True)
                            return ins
                        kb.op("tensor", smm, wait=[e_nbs, e_k])
                        e_smm = kb.mark("tensor")

                        bpt, evpt = ptb.get()
                        spt = ptb.slot

                        def expo(p, bpt=bpt):
                            return p.activation(out=bpt[:], in_=pst[:],
                                                func=AF.Exp, bias=nb_sb[:])
                        kb.op("scalar", expo, wait=[e_smm, evpt])
                        e_exp = kb.mark("scalar")

                        def pvmm(p, kchunk=kchunk, bv=bv, bpt=bpt,
                                 first=(kchunk[0] == 0), last=(kchunk[-1] == 31)):
                            for kx, kc in enumerate(kchunk):
                                p.matmul(out=pctx,
                                         lhsT=bv[:, 128 * kc:128 * kc + 128],
                                         rhs=bpt[:, 512 * kx:512 * kx + 512],
                                         start=(first and kx == 0),
                                         stop=(last and kx == 3))
                                ins = p.matmul(out=psum_s,
                                               lhsT=ones_col[:],
                                               rhs=bpt[:, 512 * kx:512 * kx + 512],
                                               start=(first and kx == 0),
                                               stop=(last and kx == 3))
                            return ins
                        kb.op("tensor", pvmm, wait=[e_exp, e_v])
                        e_pv_t = tick(kb, "tensor")
                        ptb.set_free(spt, e_pv_t)
                    e_pv = kb.mark("tensor")

                    def sums_v(p):
                        p.tensor_copy(out=sum_sb[:], in_=psum_s)
                        p.drain()
                        p.reciprocal(out=sum_sb[:], in_=sum_sb[:])
                        p.drain()
                        return p.tensor_copy(out=rec_bf[:], in_=sum_sb[:])
                    kb.op("vector", sums_v, wait=e_pv)
                    e_rec = kb.mark("vector")

                    def recmm(p):
                        return p.matmul(out=prec, lhsT=ones_row[:], rhs=rec_bf[:],
                                        start=True, stop=True)
                    kb.op("tensor", recmm, wait=[e_rec, e_cbs])
                    e_recb = kb.mark("tensor")

                    def ctxev(p, qcol=qcol):
                        p.tensor_copy(out=recb_sb[:], in_=prec)
                        p.drain()
                        return p.tensor_tensor(out=ctx_sb[:, qcol:qcol + 512],
                                               in0=pctx, in1=recb_sb[:], op=ALU.mult)
                    w = [e_recb]
                    if e_ctx_free is not None and head == 0:
                        w.append(e_ctx_free)
                    kb.op("vector", ctxev, wait=w)
                    e_ctx_done.append(kb.mark("vector"))
                e_kv_free = e_ctx_done[-1]
            e_stageB = e_ctx_done[-1]
            e_qT_free = e_stageB

            for pair in range(2):
                ts_ = [4 * G + 2 * pair, 4 * G + 2 * pair + 1]
                e_x1 = []
                for qtr in range(4):
                    hc = 1024 * qtr
                    e_xo = []
                    for ii, t in enumerate(ts_):
                        e_xo.append(kb.dma("sync", u_sb[ii][:],
                                           x1_in[128 * t:128 * t + 128, hc:hc + 1024],
                                           wait=(e_x1[-1] if e_x1 else
                                                 (e_prev_comb if e_prev_comb
                                                  else e_prep)), sem=f"sxo{ii}"))
                    for ic in range(32):
                        bw, evw = wch.get()
                        sw = wch.slot
                        e_w = kb.dma("sync", bw[:],
                                     wo1b[128 * ic:128 * ic + 128, hc:hc + 1024],
                                     wait=evw, sem=wch.sem())

                        def opmm(p, bw=bw, ic=ic, ts_=ts_, G=G):
                            for ii, t in enumerate(ts_):
                                j = t - 4 * G
                                for hx in range(2):
                                    ins = p.matmul(
                                        out=pa[ii][:, 512 * hx:512 * hx + 512],
                                        lhsT=ctx_sb[:, 512 * ic + 128 * j:
                                                    512 * ic + 128 * j + 128],
                                        rhs=bw[:, 512 * hx:512 * hx + 512],
                                        start=(ic == 0), stop=False)
                            return ins
                        w_op = [e_w]
                        if ic == 0:
                            w_op.append(e_stageB)
                            if e_x1:
                                w_op.append(e_x1[-1])
                            elif e_prev_comb is not None:
                                w_op.append(e_prev_comb)
                        kb.op("tensor", opmm, wait=w_op)
                        wch.set_free(sw, tick(kb, "tensor"))
                    for ic in range(32):
                        bw, evw = wch.get()
                        sw = wch.slot
                        e_w = kb.dma("sync", bw[:],
                                     worgb[128 * ic:128 * ic + 128, hc:hc + 1024],
                                     wait=evw, sem=wch.sem())

                        def bamm(p, bw=bw, ic=ic, ts_=ts_, G=G):
                            for ii, t in enumerate(ts_):
                                j = t - 4 * G
                                for hx in range(2):
                                    ins = p.matmul(
                                        out=pa[ii][:, 512 * hx:512 * hx + 512],
                                        lhsT=xg[j][:, 128 * ic:128 * ic + 128],
                                        rhs=bw[:, 512 * hx:512 * hx + 512],
                                        start=False, stop=False)
                            return ins
                        kb.op("tensor", bamm, wait=[e_w])
                        wch.set_free(sw, tick(kb, "tensor"))
                    for z in range(4):
                        e_up = kb.dma("sync", upc[:, 1024 * z:1024 * z + 1024],
                                      upb[128 * z:128 * z + 128, hc:hc + 1024],
                                      sem="sup", wait=e_up_used if z == 0 else None)

                    def dmm(p, ts_=ts_, G=G):
                        for ii, t in enumerate(ts_):
                            j = t - 4 * G
                            for z in range(4):
                                last = (z == 3)
                                for hx in range(2):
                                    ins = p.matmul(
                                        out=pa[ii][:, 512 * hx:512 * hx + 512],
                                        lhsT=hT_sb[j][:, 128 * z:128 * z + 128],
                                        rhs=upc[:, 1024 * z + 512 * hx:
                                                1024 * z + 512 * hx + 512],
                                        start=False, stop=last)
                        return ins
                    kb.op("tensor", dmm, wait=[e_up] + e_hT)
                    e_up_used = tick(kb, "tensor")
                    e_mm = kb.mark("tensor")

                    def comb(p, hc=hc):
                        p.scalar_tensor_tensor(out=out_sb[0][:], in0=pa[0],
                                               scalar=1.0, in1=u_sb[0][:],
                                               op0=ALU.mult, op1=ALU.add)
                        ins = p.scalar_tensor_tensor(out=out_sb[1][:], in0=pa[1],
                                                     scalar=1.0, in1=u_sb[1][:],
                                                     op0=ALU.mult, op1=ALU.add)
                        return ins
                    kb.op("vector", comb, wait=[e_mm] + e_xo +
                          [e_out_free[0], e_out_free[1]])
                    e_x1.append(kb.mark("vector"))

                    for ii, t in enumerate(ts_):
                        store_evs.append(kb.dma(
                            "gpsimd", out_d[128 * t:128 * t + 128, hc:hc + 1024],
                            out_sb[ii][:], wait=e_x1[-1], sem=f"sto{ii}"))
                        e_out_free[ii] = store_evs[-1]
                e_prev_comb = e_x1[-1]
                for j in (2 * pair, 2 * pair + 1):
                    e_xg[j] = e_prev_comb
                    e_hT_free[j] = e_prev_comb
            e_ctx_free = e_mm

        kb.wait("gpsimd", store_evs)
        kb.build(block)
    return nc




FCAST_MATS = [
    ("wq0", D, D), ("wo0", D, D),
    ("wq1", D, D), ("wo1", D, D), ("worg", D, D),
    ("down", D, ER), ("up", ER, D), ("wqf", D, KQ),
]


def build_fused(dbg=False):
    nc = bass.Bass("TRN2", target_bir_lowering=False)
    EI = lambda n, sh, dt: nc.dram_tensor(n, sh, dt, kind="ExternalInput").ap()
    IN = lambda n, sh, dt: nc.dram_tensor(n, sh, dt, kind="Internal").ap()
    d_ins, d_outs = {}, {}
    for name, rows, cols in FCAST_MATS:
        d_ins[name] = EI(name + "_s", [rows // NCORE, cols], F32)
        d_outs[name] = IN(name + "_p", [rows // NCORE, cols], BF16)
    d_ins["keysT"] = EI("keysT_s", [KQ, E], F32)
    d_outs["keysT"] = IN("keysT_p", [KQ, E], BF16)
    qw_i = EI("qw_s", [512, 8, 128], I32)
    lut4_i = EI("lut4_s", [512, 16], F32)
    lut8_i = EI("lut8_s", [512, 256], F32)
    identin_a = EI("identin", [128, 128], F32)
    xc_a = EI("xc", [S, D], F32)
    wkc_a = EI("wkc_f", [D, 512], F32)
    wvc_a = EI("wvc_f", [D, 512], F32)
    ropec_a = EI("ropec", [S, HD], F32)
    ropes_a = EI("ropes", [S, HD], F32)
    xo_a = EI("xo_in", [TOK, D], F32)
    rqc0_a = EI("rqc0", [TOK, HD], F32)
    rqs0_a = EI("rqs0", [TOK, HD], F32)
    m0_a = EI("m0in", [128, 128], F32)
    m1_a = EI("m1in", [128, 128], F32)
    bias_a = EI("bias_in", [1, D], F32)
    tix_a = EI("tix", [128, OWN_T], I32)
    kcache_a = EI("kcache", [512, 16384], F32)
    vcache_a = EI("vcache", [512, 16384], F32)
    bt_a = EI("bt", [256], I32)
    rqc1_a = EI("rqc1", [TOK, HD], F32)
    rqs1_a = EI("rqs1", [TOK, HD], F32)
    out_a = nc.dram_tensor("out_d", [TOK, D], F32, kind="ExternalOutput").ap()
    wlT_p = IN("wlT_p", [D, 512], BF16)
    whT_p = IN("whT_p", [D, 512], BF16)
    wq0_all = IN("wq0_all", [D, D], BF16)
    wo0_all = IN("wo0_all", [D, D], BF16)
    wq1_all = IN("wq1_all", [D, D], BF16)
    wo1_all = IN("wo1_all", [D, D], BF16)
    worg_all = IN("worg_all", [D, D], BF16)
    down_all = IN("down_all", [D, ER], BF16)
    up_all = IN("up_all", [ER, D], BF16)
    wqf_all = IN("wqf_all", [D, KQ], BF16)
    keysT_all = IN("keysT_all", [NCORE * KQ, E], BF16)
    wlT_all = IN("wlT_all", [NCORE * D, 512], BF16)
    whT_all = IN("whT_all", [NCORE * D, 512], BF16)
    kvp_k = IN("kvp_k", [4 * HD, S], BF16)
    kvp_v = IN("kvp_v", [4 * HD, S], BF16)
    kT0_all = IN("kT0_all", [NKV * HD, S], BF16)
    v0_all = IN("v0_all", [NKV * HD, S], BF16)
    xnT0_d = IN("xnT0_d", [S, D], BF16)
    maskv_d = IN("maskv_d", [S], F32)
    bias_dram_t = IN("bias_dram", [1, D], BF16)
    x1_d = IN("x1_d", [TOK, D], F32)
    xnT1_d = IN("xnT1_d", [TOK, D], BF16)
    msum_d = IN("msum_d", [128, 32], F32)
    msum_all_t = IN("msum_all", [256, 32], F32)
    kg_d_t = IN("kg_d", [256, 16384], F32)
    vg_d_t = IN("vg_d", [256, 16384], F32)
    kT1_d = IN("kT1_d", [NKV, HD, S], BF16)
    v1_d = IN("v1_d", [NKV * HD, S], BF16)
    coeffs_d = IN("coeffs_d", [1, E], F32)

    with ExitStack() as ctx0:
        block = ctx0.enter_context(nc.Block())
        kb = KB(nc, ctx0)
        wsb_ = ctx0.enter_context(nc.sbuf_tensor("wsb_", [128, 128], F32))
        wbf_ = ctx0.enter_context(nc.sbuf_tensor("wbf_", [128, 128], BF16))
        wo__ = ctx0.enter_context(nc.sbuf_tensor("wo__", [128, 128], F32))
        with ExitStack() as wctx:
            wps_ = wctx.enter_context(nc.psum_tensor("wps_", [128, 128], F32))
            emit_warmup(kb, wbf_, wps_[:], wsb_, wo__)

        def _prep():
            ins, outs = d_ins, d_outs
            qw, lut4, lut8 = qw_i, lut4_i, lut8_i
            identin = identin_a
            wlT, whT = wlT_p, whT_p
            with ExitStack() as ctx:
                sb = lambda n, sh, dt: ctx.enter_context(nc.sbuf_tensor("P0" + n, sh, dt))
                ps = lambda n, sh, dt: ctx.enter_context(nc.psum_tensor("P0" + n, sh, dt))
                cin = Rot([sb(f"cin{i}", [128, D], F32) for i in range(2)])
                cout = Rot([sb(f"cout{i}", [128, D], BF16) for i in range(2)])
                qw_sb = sb("qw_sb", [128, 1024], I32)
                idx_i = sb("idx_i", [128, D], I32)
                bit_t = sb("bit_t", [128, 128], I32)
                bit_t2 = sb("bit_t2", [128, 128], I32)
                idx_bf = sb("idx_bf", [128, D], BF16)
                idx4_bf = sb("idx4_bf", [128, D], BF16)
                l4_sb = sb("l4_sb", [128, 16], F32)
                l8_sb = sb("l8_sb", [128, 256], F32)
                wtile = sb("wtile", [128, D], BF16)
                wtile2 = sb("wtile2", [128, D], BF16)
                wl_sb = sb("wl_sb", [128, 4 * D], BF16)
                wh_sb = sb("wh_sb", [128, 4 * D], BF16)
                identf = sb("identf", [128, 128], F32)
                ident_bf = sb("ident_bf", [128, 128], BF16)
                tev = Rot([sb(f"tev{i}", [128, 512], BF16) for i in range(2)])
                ptr = Rot([ps(f"ptr{i}", [128, 512], F32) for i in range(2)])


                e_id = kb.dma("sync", identf[:], identin)
                kb.op("vector", lambda p: p.tensor_copy(out=ident_bf[:], in_=identf[:]), wait=e_id)
                e_idbf = kb.mark("vector")

                cast_tiles = []
                for name, rows, cols in FCAST_MATS:
                    rs = rows // NCORE
                    for t in range((rs + 127) // 128):
                        cast_tiles.append((name, t, min(128, rs - 128 * t), cols))
                cast_tiles.append(("keysT", 0, KQ, E))

                store_evs = []
                for (name, t, r, cols) in cast_tiles:
                    bi, ev_i = cin.get()
                    si = cin.slot
                    e_ld = kb.dma("scalar", bi[0:r, 0:cols], ins[name][128 * t:128 * t + r, :],
                                  wait=ev_i)
                    bo, ev_o = cout.get()
                    so = cout.slot
                    kb.op("scalar",
                          lambda p, bo=bo, bi=bi, r=r, cols=cols:
                          p.activation(out=bo[0:r, 0:cols], in_=bi[0:r, 0:cols], func=AF.Copy),
                          wait=[e_ld, ev_o])
                    e_cast = kb.mark("scalar")
                    cin.set_free(si, e_cast)
                    e_st = kb.dma("gpsimd", outs[name][128 * t:128 * t + r, :], bo[0:r, 0:cols],
                                  wait=e_cast)
                    cout.set_free(so, e_st)
                    store_evs.append(e_st)

                e_unpack_prev = None
                e_tile_prev = None
                for ot in range(4):
                    e_qw = kb.dma("sync", qw_sb[:], qw[128 * ot:128 * ot + 128, :, :],
                                  wait=e_unpack_prev)
                    e_l4 = kb.dma("sync", l4_sb[:], lut4[128 * ot:128 * ot + 128, :],
                                  wait=e_tile_prev)
                    e_l8 = kb.dma("sync", l8_sb[:], lut8[128 * ot:128 * ot + 128, :])

                    def unpack(p, qw_sb=qw_sb):
                        def ext(tgt, j, pl):
                            src = qw_sb[:, 128 * pl:128 * pl + 128]
                            if j >= pl:
                                sh, shop = j - pl, ALU.logical_shift_right
                            else:
                                sh, shop = pl - j, ALU.logical_shift_left
                            p.tensor_scalar(out=tgt, in0=src, scalar1=sh, scalar2=1 << pl,
                                            op0=shop, op1=ALU.bitwise_and)
                        for j in range(0, 32, 2):
                            d0 = idx_i[:, 128 * j:128 * j + 128]
                            d1 = idx_i[:, 128 * (j + 1):128 * (j + 2)]
                            ext(d0, j, 0)
                            ext(d1, j + 1, 0)
                            for pl in range(1, 8):
                                ext(bit_t[:], j, pl)
                                ext(bit_t2[:], j + 1, pl)
                                p.tensor_tensor(out=d0, in0=d0, in1=bit_t[:], op=ALU.bitwise_or)
                                p.tensor_tensor(out=d1, in0=d1, in1=bit_t2[:], op=ALU.bitwise_or)
                        p.drain()
                        p.tensor_copy(out=idx_bf[:], in_=idx_i[:])
                        p.drain()
                        p.tensor_scalar(out=idx_i[:], in0=idx_i[:], scalar1=15, scalar2=None,
                                        op0=ALU.bitwise_and)
                        p.drain()
                        return p.tensor_copy(out=idx4_bf[:], in_=idx_i[:])
                    kb.op("vector", unpack, wait=[e_qw, e_l4, e_l8])
                    e_unpack_prev = kb.mark("vector")

                    for nb, src_idx, lsb, wsb in [(16, idx4_bf, l4_sb, wl_sb),
                                                  (256, idx_bf, l8_sb, wh_sb)]:
                        def codeloop(p, nb=nb, src_idx=src_idx, lsb=lsb, wsb=wsb, ot=ot):
                            dst = wsb[:, D * ot:D * ot + D]
                            def sel(tgt, cde):
                                p.tensor_scalar(out=tgt, in0=src_idx[:],
                                                scalar1=float(cde), scalar2=lsb[:, cde:cde + 1],
                                                op0=ALU.is_equal, op1=ALU.mult)
                            sel(wtile[:], 0)
                            sel(wtile2[:], 1)
                            p.tensor_copy(out=dst, in_=wtile[:])
                            p.tensor_tensor(out=dst, in0=dst, in1=wtile2[:], op=ALU.add)
                            for cde in range(2, nb, 2):
                                sel(wtile[:], cde)
                                sel(wtile2[:], cde + 1)
                                p.tensor_tensor(out=dst, in0=dst, in1=wtile[:], op=ALU.add)
                                p.tensor_tensor(out=dst, in0=dst, in1=wtile2[:], op=ALU.add)
                            return p.drain()
                        kb.op("vector", codeloop)
                    e_tile_prev = kb.mark("vector")
                e_deq = e_tile_prev

                for li, wsb, dst in [(0, wl_sb, wlT), (1, wh_sb, whT)]:
                    for icb in range(32):
                        pb, ev_p = ptr.get()
                        sp = ptr.slot

                        def tr(p, pb=pb, wsb=wsb, icb=icb):
                            for ot in range(4):
                                p.matmul(out=pb[:, 128 * ot:128 * ot + 128],
                                         lhsT=wsb[:, D * ot + 128 * icb:D * ot + 128 * icb + 128],
                                         rhs=ident_bf[:], start=True, stop=True)
                            return p.drain()
                        kb.op("tensor", tr, wait=[e_deq, e_idbf, ev_p])
                        e_tr = kb.mark("tensor")
                        bo, ev_o = tev.get()
                        so = tev.slot
                        kb.op("vector", lambda p, bo=bo, pb=pb: p.tensor_copy(out=bo[:], in_=pb[:]),
                              wait=[e_tr, ev_o])
                        e_ev = kb.mark("vector")
                        ptr.set_free(sp, e_ev)
                        e_st = kb.dma("gpsimd",
                                      dst.rearrange("(w j) c -> j w c", j=32)[icb], bo[:],
                                      wait=e_ev)
                        tev.set_free(so, e_st)
                        store_evs.append(e_st)

                kb.wait("gpsimd", store_evs[-1])
                kb.wait("gpsimd", store_evs)
                pass
        _prep()
        kb_barrier(kb)
        pieces = ([d_outs[n] for n, _, _ in FCAST_MATS] + [d_outs["keysT"],
                  wlT_p, whT_p])
        alls = [wq0_all, wo0_all, wq1_all, wo1_all, worg_all, down_all,
                up_all, wqf_all, keysT_all, wlT_all, whT_all]
        evw = kb_collective(kb, "w", pieces, alls, [list(range(NCORE))])
        for eng in ENGS:
            kb.wait(eng, evw)

        def _pa():
            xc = xc_a
            wk_cols, wv_cols = wkc_a, wvc_a
            ropec, ropes = ropec_a, ropes_a
            identin = identin_a
            xnT0 = xnT0_d.rearrange("(t p) d -> t p d", p=128)
            kT0h = kvp_k.rearrange("(h d) s -> h d s", h=4)
            v0h = kvp_v.rearrange("(h p) (c dd) -> h p c dd", p=128, dd=128)
            maskv = maskv_d
            with ExitStack() as ctx:
                sb = lambda n, sh, dt: ctx.enter_context(nc.sbuf_tensor("PA" + n, sh, dt))
                ps = lambda n, sh, dt: ctx.enter_context(nc.psum_tensor("PA" + n, sh, dt))
                xg = Rot([sb(f"xg{i}", [128, D], F32) for i in range(2)])
                wfr = Rot([sb(f"wfr{i}", [128, 512], F32) for i in range(2)], "swf")
                rc = Rot([sb(f"rc{i}", [128, HD], F32) for i in range(2)])
                rs = Rot([sb(f"rs{i}", [128, HD], F32) for i in range(2)])
                wk_sb = sb("wk_sb", [128, 32 * 512], BF16)
                wv_sb = sb("wv_sb", [128, 32 * 512], BF16)
                xnt = Rot([sb(f"xnt{i}", [128, D], BF16) for i in range(2)])
                xnbf = sb("xnbf", [128, D], BF16)
                ssum = sb("ssum", [128, 1], F32)
                ssq = sb("ssq", [128, 1], F32)
                negm = sb("negm", [128, 1], F32)
                msq = sb("msq", [128, 1], F32)
                var = sb("var", [128, 1], F32)
                stdv = sb("stdv", [128, 1], F32)
                rstd = sb("rstd", [128, 1], F32)
                r2 = sb("r2", [128, 1], F32)
                e2 = sb("e2", [128, 1], F32)
                maskb = sb("maskb", [128, 1], F32)
                epsb = sb("epsb", [128, 1], F32)
                sqs = sb("sqs", [128, D], F32)
                kv_ev = Rot([sb(f"kv_ev{i}", [128, 512], BF16) for i in range(2)])
                kf = sb("kf", [128, 512], F32)
                kt1 = sb("kt1", [128, 512], F32)
                kt2 = sb("kt2", [128, 512], F32)
                krbf = sb("krbf", [128, 512], BF16)
                ktb = Rot([sb(f"ktb{i}", [128, 512], BF16) for i in range(2)])
                identf = sb("identf", [128, 128], F32)
                ident_bf = sb("ident_bf", [128, 128], BF16)
                ptx = Rot([ps(f"ptx{i}", [128, 512], F32) for i in range(2)])
                pv = ps("pv", [128, 512], F32)
                pk = ps("pk", [128, 512], F32)
                pkt = ps("pkt", [128, 512], F32)


                e_id = kb.dma("sync", identf[:], identin)
                kb.op("vector", lambda p: p.tensor_copy(out=ident_bf[:], in_=identf[:]), wait=e_id)
                kb.op("vector", lambda p: p.memset(epsb[:], EPS))
                e_prep = kb.mark("vector")
                e_cast_pv = None
                for kk in range(32):
                    for src_w, dst_w in ((wk_cols, wk_sb), (wv_cols, wv_sb)):
                        bwf, evwf = wfr.get()
                        swf = wfr.slot
                        e_ldw = kb.dma("sync", bwf[:], src_w[128 * kk:128 * kk + 128, :],
                                       wait=evwf, sem=wfr.sem())
                        kb.op("scalar",
                              lambda p, bwf=bwf, dst_w=dst_w, kk=kk:
                              p.activation(out=dst_w[:, 512 * kk:512 * kk + 512],
                                           in_=bwf[:], func=AF.Copy), wait=e_ldw)
                        e_cast_pv = kb.mark("scalar")
                        wfr.set_free(swf, e_cast_pv)
                e_wk = e_cast_pv
                e_wv = e_cast_pv

                store_evs = []
                for g in range(NT):
                    own = True
                    t_local = g
                    bx, evx = xg.get()
                    sx_ = xg.slot
                    e_x = kb.dma("sync", bx[:], xc[128 * g:128 * g + 128, :], wait=evx)
                    brc, evrc = rc.get()
                    src_ = rc.slot
                    e_rc = kb.dma("sync", brc[:], ropec[128 * g:128 * g + 128, :], wait=evrc)
                    brs, evrs = rs.get()
                    srs_ = rs.slot
                    e_rs = kb.dma("sync", brs[:], ropes[128 * g:128 * g + 128, :], wait=evrs)

                    def stats_v(p, bx=bx):
                        p.tensor_reduce(out=ssum[:], in_=bx[:], axis=AX.X, op=ALU.add)
                        p.drain()
                        p.tensor_scalar(out=negm[:], in0=ssum[:], scalar1=-1.0 / D,
                                        scalar2=None, op0=ALU.mult)
                        return p.drain()
                    kb.op("vector", stats_v, wait=e_x)
                    e_negm = kb.mark("vector")

                    def stats_s(p, bx=bx):
                        p.activation(out=sqs[:], in_=bx[:], func=AF.Square, accum_out=ssq[:])
                        p.drain()
                        p.activation(out=msq[:], in_=negm[:], func=AF.Square)
                        return p.drain()
                    kb.op("scalar", stats_s, wait=[e_x, e_negm])
                    e_msq = kb.mark("scalar")

                    def var_v(p):
                        p.scalar_tensor_tensor(out=var[:], in0=ssq[:], scalar=1.0 / D,
                                               in1=msq[:], op0=ALU.mult, op1=ALU.subtract)
                        return p.drain()
                    kb.op("vector", var_v, wait=e_msq)
                    e_var = kb.mark("vector")

                    def std_s(p):
                        p.activation(out=stdv[:], in_=var[:], func=AF.Sqrt, bias=epsb[:])
                        return p.drain()
                    kb.op("scalar", std_s, wait=[e_var, e_prep])
                    e_std = kb.mark("scalar")

                    def xn_v(p, bx=bx, own=own):
                        p.reciprocal(out=rstd[:], in_=stdv[:])
                        p.drain()
                        p.tensor_scalar(out=xnbf[:], in0=bx[:], scalar1=negm[:], scalar2=rstd[:],
                                        op0=ALU.add, op1=ALU.mult)
                        if own:
                            p.tensor_tensor(out=r2[:], in0=rstd[:], in1=rstd[:], op=ALU.mult)
                            p.drain()
                            p.scalar_tensor_tensor(out=e2[:], in0=var[:], scalar=float(D),
                                                   in1=r2[:], op0=ALU.mult, op1=ALU.mult)
                            p.drain()
                            p.tensor_scalar(out=maskb[:], in0=e2[:], scalar1=float(D),
                                            scalar2=None, op0=ALU.is_gt)
                        return p.drain()
                    kb.op("vector", xn_v, wait=e_std)
                    e_xn = kb.mark("vector")
                    if own:
                        store_evs.append(kb.dma(
                            "gpsimd", maskv[128 * t_local:128 * t_local + 128]
                            .rearrange("(p one) -> p one", one=1), maskb[:], wait=e_xn))

                    bxn, evxn = xnt.get()
                    sxn_ = xnt.slot
                    e_ev_last = []
                    for grp in range(8):
                        pb, evp = ptx.get()
                        sp = ptx.slot

                        def tr(p, pb=pb, grp=grp):
                            for q4 in range(4):
                                ic = 4 * grp + q4
                                p.matmul(out=pb[:, 128 * q4:128 * q4 + 128],
                                         lhsT=xnbf[:, 128 * ic:128 * ic + 128],
                                         rhs=ident_bf[:], start=True, stop=True)
                            return p.drain()
                        kb.op("tensor", tr, wait=[e_xn, e_prep, evp] + ([evxn] if grp == 0 and evxn else []))
                        e_tr = kb.mark("tensor")

                        kb.op("vector",
                              lambda p, bxn=bxn, grp=grp, pb=pb:
                              p.tensor_copy(out=bxn[:, 512 * grp:512 * grp + 512], in_=pb[:]),
                              wait=e_tr)
                        e_ev = kb.mark("vector")
                        ptx.set_free(sp, e_ev)
                        e_ev_last.append(e_ev)
                    e_xnt = e_ev_last[-1]
                    if own:
                        store_evs.append(kb.dma("gpsimd", xnT0[t_local], bxn[:], wait=e_xnt))

                    def kv_mm(p, bxn=bxn):
                        for ic in range(32):
                            p.matmul(out=pv[:], lhsT=bxn[:, 128 * ic:128 * ic + 128],
                                     rhs=wv_sb[:, 512 * ic:512 * ic + 512],
                                     start=(ic == 0), stop=(ic == 31))
                        for ic in range(32):
                            p.matmul(out=pk[:], lhsT=bxn[:, 128 * ic:128 * ic + 128],
                                     rhs=wk_sb[:, 512 * ic:512 * ic + 512],
                                     start=(ic == 0), stop=(ic == 31))
                        return p.drain()
                    kb.op("tensor", kv_mm, wait=[e_xnt, e_wk, e_wv])
                    e_kv = kb.mark("tensor")
                    xnt.set_free(sxn_, e_kv)

                    bv, evv = kv_ev.get()
                    sv_ = kv_ev.slot
                    kb.op("vector", lambda p, bv=bv: p.tensor_copy(out=bv[:], in_=pv[:]),
                          wait=[e_kv, evv])
                    kb.op("vector", lambda p: p.tensor_copy(out=kf[:], in_=pk[:]))
                    e_kevac = kb.mark("vector")
                    for hh in range(4):
                        store_evs.append(kb.dma(
                            "gpsimd", v0h[hh, :, g, :],
                            bv[:, 128 * hh:128 * hh + 128], wait=e_kevac))
                    kv_ev.set_free(sv_, store_evs[-1])

                    def rope_v(p, brc=brc, brs=brs):
                        kview = kf[:].rearrange("p (h d) -> p h d", h=4)
                        cos4 = brc[:].rearrange("p (o d) -> p o d", o=1).broadcast_to([128, 4, HD])
                        k4 = kf[:].rearrange("p (h pr two) -> p h pr two", h=4, two=2)
                        o4 = kt2[:].rearrange("p (h pr two) -> p h pr two", h=4, two=2)
                        s2 = brs[:].rearrange("p (pr two) -> p pr two", two=2)
                        p.tensor_tensor(out=kt1[:].rearrange("p (h d) -> p h d", h=4),
                                        in0=kview, in1=cos4, op=ALU.mult)
                        p.tensor_tensor(
                            out=o4[:, :, :, 0:1], in0=k4[:, :, :, 1:2],
                            in1=s2[:, :, 0:1].unsqueeze(1)
                            .broadcast_to([128, 4, HD // 2, 1]), op=ALU.mult)
                        p.tensor_tensor(
                            out=o4[:, :, :, 1:2], in0=k4[:, :, :, 0:1],
                            in1=s2[:, :, 1:2].unsqueeze(1)
                            .broadcast_to([128, 4, HD // 2, 1]), op=ALU.mult)
                        p.drain()
                        p.tensor_tensor(out=krbf[:], in0=kt1[:], in1=kt2[:], op=ALU.add)
                        return p.drain()
                    kb.op("vector", rope_v, wait=[e_kevac, e_rc, e_rs])
                    e_rope = kb.mark("vector")

                    def ktr(p):
                        for hh in range(4):
                            p.matmul(out=pkt[:, 128 * hh:128 * hh + 128],
                                     lhsT=krbf[:, 128 * hh:128 * hh + 128],
                                     rhs=ident_bf[:], start=True, stop=True)
                        return p.drain()
                    kb.op("tensor", ktr, wait=e_rope)
                    e_ktr = kb.mark("tensor")
                    bkt, evkt = ktb.get()
                    skt_ = ktb.slot
                    kb.op("vector", lambda p, bkt=bkt: p.tensor_copy(out=bkt[:], in_=pkt[:]),
                          wait=[e_ktr, evkt])
                    e_ktb = kb.mark("vector")
                    for hh in range(4):
                        store_evs.append(kb.dma(
                            "gpsimd", kT0h[hh, :, 128 * g:128 * g + 128],
                            bkt[:, 128 * hh:128 * hh + 128], wait=e_ktb))
                    ktb.set_free(skt_, store_evs[-1])
                    xg.set_free(sx_, e_xn)
                    rc.set_free(src_, e_rope)
                    rs.set_free(srs_, e_rope)

                kb.wait("gpsimd", store_evs)
                pass
        _pa()
        kb_barrier(kb)
        evkv = kb_collective(kb, "kv", [kvp_k, kvp_v], [kT0_all, v0_all],
                             [[2 * c, 2 * c + 1] for c in range(B)])
        for eng in ENGS:
            kb.wait(eng, evkv)

        def _pc(n_groups=4, stop=None):
            kT0 = kT0_all.rearrange("(h d) s -> h d s", h=NKV)
            v0flat = v0_all
            xo_in = xo_a
            wq0b, wo0b = wq0_all, wo0_all
            bias_in = bias_a
            rqc, rqs = rqc0_a, rqs0_a
            m0in, m1in = m0_a, m1_a
            identin = identin_a
            tix_in = tix_a
            xnT0_flat = xnT0_d
            maskv2d = maskv_d.rearrange("(s one) -> s one", one=1)
            x1_out = x1_d
            xnT1 = xnT1_d.rearrange("(t p) d -> t p d", p=128)
            msum_out = msum_d
            bias_dram = bias_dram_t
            with ExitStack() as ctx:
                sb = lambda n, sh, dt: ctx.enter_context(nc.sbuf_tensor("PC" + n, sh, dt))
                xg = [sb(f"xg{t}", [128, D], BF16) for t in range(4)]
                wqc = Rot([sb(f"wqc{i}", [128, 512], BF16) for i in range(2)], "swqc")
                q_sb = sb("q_sb", [128, 512], F32)
                qt1 = sb("qt1", [128, 512], F32)
                qt2 = sb("qt2", [128, 512], F32)
                qrbf = sb("qrbf", [128, 512], BF16)
                rqc_sb = [sb(f"rqc{t}", [128, HD], F32) for t in range(4)]
                rqs_sb = [sb(f"rqs{t}", [128, HD], F32) for t in range(4)]
                qT_sb = sb("qT_sb", [128, NH * 512], BF16)
                kt_sb = sb("kt_sb", [128, S], BF16)
                v_sb = sb("v_sb", [128, S], BF16)
                ptb = Rot([sb(f"ptb{i}", [128, 2048], BF16) for i in range(2)])
                absq = sb("absq", [128, 512], BF16)
                maxkf = sb("maxkf", [128, 1], F32)
                maxkb = sb("maxkb", [128, 1], BF16)
                bstar = sb("bstar", [1, 1], F32)
                btmp = sb("btmp", [1, 1], F32)
                bstar_b = sb("bstar_b", [1, 1], BF16)
                nb_sb = sb("nb_sb", [128, 1], F32)
                sum_sb = sb("sum_sb", [1, 512], F32)
                rec_bf = sb("rec_bf", [1, 512], BF16)
                recb_sb = sb("recb_sb", [128, 512], F32)
                ctx_sb = sb("ctx_sb", [128, NH * 512], BF16)
                ones_col = sb("ones_col", [128, 1], BF16)
                ones_row = sb("ones_row", [1, 128], BF16)
                m0_sb = sb("m0_sb", [128, 128], BF16)
                m1_sb = sb("m1_sb", [128, 128], BF16)
                mtmp = sb("mtmp", [128, 128], F32)
                tix_sb = sb("tix_sb", [128, OWN_T], I32)
                wo_r = Rot([sb(f"wo_r{i}", [128, 1024], BF16) for i in range(2)], "swo")
                wl_r = Rot([sb(f"wl_r{i}", [128, 1024], BF16) for i in range(2)], "swl")
                wh_r = Rot([sb(f"wh_r{i}", [128, 1024], BF16) for i in range(2)], "swh")
                u_sb = [sb(f"u_sb{i}", [128, 1024], F32) for i in range(2)]
                yl_sb = [ptb.bufs[i][:].bitcast(F32) for i in range(2)]
                x1_sb = [sb(f"x1_sb{i}", [128, D], F32) for i in range(2)]
                om_sb = [sb(f"om_sb{i}", [128, 1], F32) for i in range(2)]
                mk_sb = [sb(f"mk_sb{i}", [128, 1], F32) for i in range(2)]
                bias_sb = sb("bias_sb", [1, 1024], BF16)
                ssum = sb("ssum", [128, 1], F32)
                ssq = sb("ssq", [128, 1], F32)
                negm = sb("negm", [128, 1], F32)
                msq = sb("msq", [128, 1], F32)
                var = sb("var", [128, 1], F32)
                stdv = sb("stdv", [128, 1], F32)
                rstd = sb("rstd", [128, 1], F32)
                epsb = sb("epsb", [128, 1], F32)
                xn1bf = sb("xn1bf", [128, D], BF16)
                msum_f = sb("msum_f", [128, 32], F32)
                msev = sb("msev", [128, 32], F32)
                identf = sb("identf", [128, 128], F32)
                ident_bf = sb("ident_bf", [128, 128], BF16)

                pool_t = ctx.enter_context(nc.psum_tensor("PCpool", [128, 4096], F32))
                pool = pool_t[:]
                pq = [pool[:, 0:1024], pool[:, 1024:2048]]
                ptq = [pool[:, 2048:2560], pool[:, 2560:3072]]
                pst = pool[:, 0:2048]
                pctx = pool[:, 2048:2560]
                psum_s = pool[0:1, 2560:3072]
                pnb = pool[:, 3072:3073]
                prec = pool[:, 3584:4096]
                pa = [pool[:, 0:1024], pool[:, 1024:2048]]
                pmsum = pool[:, 3104:3136]
                warm_ps = pool[:, 0:128]


                e_id = kb.dma("sync", identf[:], identin, sem="sprep")
                e_m0 = kb.dma("sync", mtmp[:], m0in, sem="sprep")

                def prep_v(p):
                    p.tensor_copy(out=ident_bf[:], in_=identf[:])
                    p.memset(epsb[:], EPS)
                    p.memset(ones_col[:], 1.0)
                    p.memset(ones_row[:], 1.0)
                    p.memset(msum_f[:], 0.0)
                    return p.tensor_copy(out=m0_sb[:], in_=mtmp[:])
                kb.op("vector", prep_v, wait=[e_id, e_m0])
                e_prep0 = kb.mark("vector")
                for bz4 in range(4):
                    e_b = kb.dma("sync", u_sb[0][0:1, :],
                                 bias_in[0:1, 1024 * bz4:1024 * bz4 + 1024],
                                 wait=e_prep0, sem="sbia")
                    kb.op("vector",
                          lambda p: p.tensor_copy(out=bias_sb[:], in_=u_sb[0][0:1, :]),
                          wait=e_b)
                    e_bc = kb.mark("vector")
                    e_prep0 = kb.dma("gpsimd", bias_dram[0:1, 1024 * bz4:1024 * bz4 + 1024],
                                     bias_sb[:], wait=e_bc, sem="sbiag")
                e_m1 = kb.dma("sync", mtmp[:], m1in, wait=e_prep0, sem="sprep")
                kb.op("vector", lambda p: p.tensor_copy(out=m1_sb[:], in_=mtmp[:]), wait=e_m1)
                e_prep = kb.mark("vector")
                e_tix = kb.dma("sync", tix_sb[:], tix_in, sem="sprep")

                store_evs = []
                ev_evac = [None, None]
                e_bnd_ev = None
                e_bias_used = e_prep0
                e_kv_free = None
                e_prev_comb = None
                e_xg = [None] * 4
                e_qT_free = None
                e_ctx_free = None
                ln_free = [None, None]

                for G in range(n_groups):
                    e_xg_ld = []
                    for j in range(4):
                        t = 4 * G + j
                        e_xg_ld.append(igather(kb, xg[j][:], xnT0_flat,
                                               tix_sb[:, t:t + 1],
                                               wait=[e_xg[j], e_tix], sem=f"sxg{j}"))
                        e_xg_ld.append(kb.dma("sync", rqc_sb[j][:], rqc[128 * t:128 * t + 128, :], sem=f"srq{j}"))
                        e_xg_ld.append(kb.dma("sync", rqs_sb[j][:], rqs[128 * t:128 * t + 128, :], sem=f"srq{j}"))
                    for r in range(8):
                        for tp in range(2):
                            for ic in range(32):
                                bw, evw = wqc.get()
                                sw = wqc.slot
                                e_w = kb.dma("sync" if sw == 0 else "scalar", bw[:],
                                             wq0b[128 * ic:128 * ic + 128,
                                                  512 * r:512 * r + 512], wait=evw,
                                             sem=wqc.sem())

                                def qmm(p, bw=bw, ic=ic, tp=tp):
                                    for jj in range(2):
                                        j = 2 * tp + jj
                                        ins = p.matmul(out=pq[jj][:, 0:512], rhs=bw[:],
                                                       lhsT=xg[j][:, 128 * ic:128 * ic + 128],
                                                       start=(ic == 0), stop=(ic == 31))
                                    return ins
                                w0 = [e_w]
                                if ic == 0:
                                    w0 += e_xg_ld
                                    if e_qT_free is not None and r == 0 and tp == 0:
                                        w0.append(e_qT_free)
                                kb.op("tensor", qmm, wait=w0)
                                wqc.set_free(sw, tick(kb, "tensor"))
                            e_pq = kb.mark("tensor")
                            for jj in range(2):
                                j = 2 * tp + jj

                                def roper(p, jj=jj, j=j):
                                    p.tensor_copy(out=q_sb[:], in_=pq[jj][:, 0:512])
                                    p.drain()
                                    kview = q_sb[:].rearrange("p (h d) -> p h d", h=4)
                                    cos8 = rqc_sb[j][:].rearrange("p (o d) -> p o d", o=1) \
                                        .broadcast_to([128, 4, HD])
                                    k4 = q_sb[:].rearrange("p (h pr two) -> p h pr two",
                                                           h=4, two=2)
                                    o4 = qt2[:].rearrange("p (h pr two) -> p h pr two",
                                                          h=4, two=2)
                                    s2 = rqs_sb[j][:].rearrange("p (pr two) -> p pr two", two=2)
                                    p.tensor_tensor(out=qt1[:].rearrange("p (h d) -> p h d", h=4),
                                                    in0=kview, in1=cos8, op=ALU.mult)
                                    p.tensor_tensor(out=o4[:, :, :, 0:1], in0=k4[:, :, :, 1:2],
                                                    in1=s2[:, :, 0:1].unsqueeze(1)
                                                    .broadcast_to([128, 4, HD // 2, 1]),
                                                    op=ALU.mult)
                                    p.tensor_tensor(out=o4[:, :, :, 1:2], in0=k4[:, :, :, 0:1],
                                                    in1=s2[:, :, 1:2].unsqueeze(1)
                                                    .broadcast_to([128, 4, HD // 2, 1]),
                                                    op=ALU.mult)
                                    p.drain()
                                    p.tensor_tensor(out=qrbf[:], in0=qt1[:], in1=qt2[:],
                                                    op=ALU.add)
                                    return p.drain()
                                kb.op("vector", roper, wait=e_pq)
                                e_rope = kb.mark("vector")

                                def qtr4(p, jj=jj):
                                    for hq in range(4):
                                        ins = p.matmul(
                                            out=ptq[jj][:, 128 * hq:128 * hq + 128],
                                            lhsT=qrbf[:, 128 * hq:128 * hq + 128],
                                            rhs=ident_bf[:], start=True, stop=True)
                                    return ins
                                kb.op("tensor", qtr4, wait=[e_rope, e_prep])
                                e_qtr = kb.mark("tensor")

                                def qev(p, jj=jj, j=j, r=r):
                                    for hq in range(4):
                                        head = 4 * r + hq
                                        ins = p.tensor_copy(
                                            out=qT_sb[:, 512 * head + 128 * j:
                                                      512 * head + 128 * j + 128],
                                            in_=ptq[jj][:, 128 * hq:128 * hq + 128])
                                    return ins
                                kb.op("vector", qev, wait=e_qtr)
                                e_qev = kb.mark("vector")
                    e_stageA = e_qev
                    if stop == "A":
                        ev = kb.dma("gpsimd", x1_out[0:128, 0:128],
                                    qT_sb[:, 0:256].bitcast(F32), wait=e_stageA, sem="stdbg")
                        kb.wait("gpsimd", ev)
                        pass
                        pass

                    nkc = 8 * G + 8
                    e_ctx_done = []
                    for kvg in range(NKV):
                        bk, bv = kt_sb, v_sb
                        evkv = e_kv_free
                        e_k = kb.dma("scalar", bk[:, 0:128 * nkc],
                                     kT0[kvg, :, 0:128 * nkc],
                                     wait=evkv, sem="sk")
                        e_v = kb.dma("scalar", bv[:, 0:128 * nkc],
                                     v0flat[128 * kvg:128 * kvg + 128, 0:128 * nkc],
                                     wait=evkv, sem="sv")

                        def maxk_v(p, bk=bk, nkc=nkc):
                            p.tensor_reduce(out=maxkf[:], in_=bk[:, 0:128 * nkc], axis=AX.X,
                                            op=ALU.max, apply_absolute_value=True)
                            p.drain()
                            return p.tensor_copy(out=maxkb[:], in_=maxkf[:])
                        kb.op("vector", maxk_v, wait=e_k)
                        e_maxk = kb.mark("vector")

                        for z in range(4):
                            def absq_s(p, kvg=kvg, z=z):
                                return p.activation(
                                    out=absq[:],
                                    in_=qT_sb[:, 2048 * kvg + 512 * z:2048 * kvg + 512 * z + 512],
                                    func=AF.Abs)
                            kb.op("scalar", absq_s,
                                  wait=[e_stageA] + ([e_bz] if z else []))
                            e_absq = kb.mark("scalar")

                            def bz(p, z=z):
                                return p.matmul(out=pool[0:1, 3072:3584], lhsT=maxkb[:],
                                                rhs=absq[:], start=True, stop=True)
                            kb.op("tensor", bz, wait=[e_maxk, e_absq] +
                                  ([e_bnd_ev] if e_bnd_ev is not None else []))
                            e_bz = kb.mark("tensor")

                            def bred(p, z=z):
                                p.tensor_copy(out=sum_sb[:], in_=pool[0:1, 3072:3584])
                                p.drain()
                                p.tensor_reduce(out=btmp[:], in_=sum_sb[:], axis=AX.X,
                                                op=ALU.max)
                                p.drain()
                                if z == 0:
                                    return p.tensor_copy(out=bstar[:], in_=btmp[:])
                                return p.tensor_tensor(out=bstar[:], in0=bstar[:],
                                                       in1=btmp[:], op=ALU.max)
                            kb.op("vector", bred, wait=e_bz)
                            e_bnd_ev = kb.mark("vector")

                        def bstar_v(p):
                            return p.tensor_scalar(out=bstar_b[:], in0=bstar[:], scalar1=-1.0,
                                                   scalar2=None, op0=ALU.mult)
                        kb.op("vector", bstar_v, wait=e_bnd_ev)
                        e_bstar = kb.mark("vector")

                        def nbmm(p):
                            return p.matmul(out=pnb, lhsT=ones_row[:], rhs=bstar_b[:],
                                            start=True, stop=True)
                        kb.op("tensor", nbmm, wait=e_bstar)
                        e_nb = kb.mark("tensor")
                        kb.op("vector", lambda p: p.tensor_copy(out=nb_sb[:], in_=pnb),
                              wait=e_nb)
                        e_nbs = kb.mark("vector")

                        for qh in range(4):
                            head = 4 * kvg + qh
                            qcol = 512 * head
                            n_batches = (nkc + 3) // 4
                            for bi in range(n_batches):
                                kchunk = list(range(4 * bi, min(4 * bi + 4, nkc)))

                                def smm(p, kchunk=kchunk, bk=bk, qcol=qcol):
                                    for ii, kc in enumerate(kchunk):
                                        ins = p.matmul(out=pst[:, 512 * ii:512 * ii + 512],
                                                       lhsT=bk[:, 128 * kc:128 * kc + 128],
                                                       rhs=qT_sb[:, qcol:qcol + 512],
                                                       start=True, stop=True)
                                    return ins
                                kb.op("tensor", smm, wait=[e_nbs, e_k])
                                e_smm = kb.mark("tensor")

                                bpt, evpt = ptb.get()
                                spt = ptb.slot

                                def expo(p, bpt=bpt, n=len(kchunk)):
                                    return p.activation(out=bpt[:, 0:512 * n],
                                                        in_=pst[:, 0:512 * n],
                                                        func=AF.Exp, bias=nb_sb[:])
                                kb.op("scalar", expo, wait=[e_smm, evpt])
                                e_exp = kb.mark("scalar")

                                mask_ops = []
                                for ii, kc in enumerate(kchunk):
                                    for j in range(4):
                                        b0 = 8 * G + 2 * j
                                        if kc == b0:
                                            mask_ops.append((ii, j, m0_sb))
                                        elif kc == b0 + 1:
                                            mask_ops.append((ii, j, m1_sb))
                                if mask_ops:
                                    def domask(p, mask_ops=mask_ops, bpt=bpt):
                                        for ii, j, msk in mask_ops:
                                            sl = bpt[:, 512 * ii + 128 * j:
                                                     512 * ii + 128 * j + 128]
                                            ins = p.tensor_tensor(out=sl, in0=sl, in1=msk[:],
                                                                  op=ALU.mult)
                                        return p.drain()
                                    kb.op("vector", domask, wait=e_exp)
                                    e_pt_ready = kb.mark("vector")
                                else:
                                    e_pt_ready = e_exp

                                def pvmm(p, kchunk=kchunk, bv=bv, G=G, bpt=bpt,
                                         first=(kchunk[0] == 0), last=(kchunk[-1] == nkc - 1)):
                                    for kx, kc in enumerate(kchunk):
                                        jmin = min(3, max(0, -(-(kc - 8 * G - 1) // 2)))
                                        s0 = 128 * jmin
                                        p.matmul(out=pctx[:, s0:512],
                                                 lhsT=bv[:, 128 * kc:128 * kc + 128],
                                                 rhs=bpt[:, 512 * kx + s0:512 * kx + 512],
                                                 start=(first and kx == 0),
                                                 stop=(last and kx == len(kchunk) - 1))
                                        ins = p.matmul(out=psum_s[:, s0:512],
                                                       lhsT=ones_col[:],
                                                       rhs=bpt[:, 512 * kx + s0:512 * kx + 512],
                                                       start=(first and kx == 0),
                                                       stop=(last and kx == len(kchunk) - 1))
                                    return ins
                                if stop == "SB" and kvg == 0 and qh == 0 and bi == 0:
                                    ev = kb.dma("gpsimd", x1_out[0:128, 0:1024],
                                                bpt[:].bitcast(F32), wait=e_pt_ready,
                                                sem="stdbg")
                                    kb.wait("gpsimd", ev)
                                    pass
                                    pass
                                kb.op("tensor", pvmm, wait=[e_pt_ready, e_v])
                                e_pv_t = tick(kb, "tensor")
                                ptb.set_free(spt, e_pv_t)
                            e_pv = kb.mark("tensor")

                            def sums_v(p):
                                p.tensor_copy(out=sum_sb[:], in_=psum_s)
                                p.drain()
                                p.reciprocal(out=sum_sb[:], in_=sum_sb[:])
                                p.drain()
                                return p.tensor_copy(out=rec_bf[:], in_=sum_sb[:])
                            kb.op("vector", sums_v, wait=e_pv)
                            e_rec = kb.mark("vector")

                            def recmm(p):
                                return p.matmul(out=prec, lhsT=ones_row[:], rhs=rec_bf[:],
                                                start=True, stop=True)
                            kb.op("tensor", recmm, wait=e_rec)
                            e_recb = kb.mark("tensor")

                            def ctxev(p, qcol=qcol):
                                p.tensor_copy(out=recb_sb[:], in_=prec)
                                p.drain()
                                return p.tensor_tensor(out=ctx_sb[:, qcol:qcol + 512],
                                                       in0=pctx, in1=recb_sb[:], op=ALU.mult)
                            w = [e_recb]
                            if e_ctx_free is not None and head == 0:
                                w.append(e_ctx_free)
                            kb.op("vector", ctxev, wait=w)
                            e_ctx_done.append(kb.mark("vector"))
                        e_kv_free = e_ctx_done[-1]
                    e_stageB = e_ctx_done[-1]
                    e_qT_free = e_stageB
                    if stop == "B":
                        ev = kb.dma("gpsimd", x1_out[0:128, 0:128],
                                    ctx_sb[:, 0:256].bitcast(F32), wait=e_stageB, sem="stdbg")
                        kb.wait("gpsimd", ev)
                        pass
                        pass

                    e_last_op_mm = None
                    for pair in range(2):
                        ts_ = [4 * G + 2 * pair, 4 * G + 2 * pair + 1]
                        e_mk = [igather(kb, mk_sb[ii][:], maskv2d,
                                        tix_sb[:, t:t + 1],
                                        wait=[ln_free[ii], e_tix] if ln_free[ii] else e_tix,
                                        sem=f"smk{ii}")
                                for ii, t in enumerate(ts_)]

                        def omv(p):
                            for ii in range(2):
                                ins = p.tensor_scalar(out=om_sb[ii][:], in0=mk_sb[ii][:],
                                                      scalar1=-1.0, scalar2=1.0,
                                                      op0=ALU.mult, op1=ALU.add)
                            return ins
                        kb.op("vector", omv, wait=e_mk)
                        e_om = kb.mark("vector")

                        e_x1 = []
                        for qtr in range(4):
                            hc = 1024 * qtr
                            e_bld = kb.dma("sync", bias_sb[:], bias_dram[0:1, hc:hc + 1024],
                                           wait=e_bias_used, sem="sbia")
                            e_xo = []
                            for ii, t in enumerate(ts_):
                                e_xo.append(kb.dma("sync", u_sb[ii][:],
                                                   xo_in[128 * t:128 * t + 128, hc:hc + 1024],
                                                   wait=(e_x1[-1] if e_x1 else
                                                         (e_prev_comb if e_prev_comb
                                                          else e_prep)), sem=f"sxo{ii}"))
                            for ic in range(32):
                                bw, evw = wo_r.get()
                                sw = wo_r.slot
                                e_w = kb.dma("scalar", bw[:],
                                             wo0b[128 * ic:128 * ic + 128, hc:hc + 1024],
                                             wait=evw, sem=wo_r.sem())

                                def opmm(p, bw=bw, ic=ic, ts_=ts_, G=G):
                                    for ii, t in enumerate(ts_):
                                        j = t - 4 * G
                                        for hx in range(2):
                                            ins = p.matmul(
                                                out=pa[ii][:, 512 * hx:512 * hx + 512],
                                                lhsT=ctx_sb[:, 512 * ic + 128 * j:
                                                            512 * ic + 128 * j + 128],
                                                rhs=bw[:, 512 * hx:512 * hx + 512],
                                                start=(ic == 0), stop=(ic == 31))
                                    return ins
                                w_op = [e_w]
                                if ic == 0:
                                    w_op.append(e_stageB)
                                    if e_x1:
                                        w_op.append(e_x1[-1])
                                    elif e_prev_comb is not None:
                                        w_op.append(e_prev_comb)
                                kb.op("tensor", opmm, wait=w_op)
                                wo_r.set_free(sw, tick(kb, "tensor"))
                            e_op = kb.mark("tensor")

                            def uev(p):
                                p.tensor_tensor(out=u_sb[0][:], in0=pa[0], in1=u_sb[0][:],
                                                op=ALU.add)
                                ins = p.tensor_tensor(out=u_sb[1][:], in0=pa[1], in1=u_sb[1][:],
                                                      op=ALU.add)
                                return ins
                            kb.op("vector", uev, wait=[e_op] + e_xo)
                            e_u = kb.mark("vector")

                            for ic in range(32):
                                bw, evw = wl_r.get()
                                sw = wl_r.slot
                                e_w = kb.dma(
                                    "sync", bw[:],
                                    wlT_all.rearrange("(rk r) c -> r rk c", rk=8)
                                    [128 * ic:128 * ic + 128,
                                     hc // 512:hc // 512 + 2, :],
                                    wait=evw, sem=wl_r.sem())

                                def ylmm(p, bw=bw, ic=ic, ts_=ts_, G=G):
                                    for ii, t in enumerate(ts_):
                                        for hx in range(2):
                                            ins = p.matmul(
                                                out=pa[ii][:, 512 * hx:512 * hx + 512],
                                                lhsT=xg[t - 4 * G][:, 128 * ic:128 * ic + 128],
                                                rhs=bw[:, 512 * hx:512 * hx + 512],
                                                start=(ic == 0), stop=False)
                                    return ins
                                kb.op("tensor", ylmm, wait=[e_w] + ([e_u] if ic == 0 else []))
                                wl_r.set_free(sw, tick(kb, "tensor"))

                            def ylbias(p):
                                for ii in range(2):
                                    for hx in range(2):
                                        ins = p.matmul(
                                            out=pa[ii][:, 512 * hx:512 * hx + 512],
                                            lhsT=ones_row[:],
                                            rhs=bias_sb[0:1, 512 * hx:512 * hx + 512],
                                            start=False, stop=True)
                                return ins
                            kb.op("tensor", ylbias, wait=e_bld)
                            e_yl = kb.mark("tensor")

                            def ylev(p):
                                p.tensor_copy(out=yl_sb[0], in_=pa[0])
                                return p.tensor_copy(out=yl_sb[1], in_=pa[1])
                            kb.op("vector", ylev,
                                  wait=[e_yl, ptb.free[0], ptb.free[1]])
                            e_ylev = kb.mark("vector")

                            for ic in range(32):
                                bw, evw = wh_r.get()
                                sw = wh_r.slot
                                e_w = kb.dma(
                                    "gpsimd", bw[:],
                                    whT_all.rearrange("(rk r) c -> r rk c", rk=8)
                                    [128 * ic:128 * ic + 128,
                                     hc // 512:hc // 512 + 2, :],
                                    wait=evw, sem=wh_r.sem())

                                def yhmm(p, bw=bw, ic=ic, ts_=ts_, G=G):
                                    for ii, t in enumerate(ts_):
                                        for hx in range(2):
                                            ins = p.matmul(
                                                out=pa[ii][:, 512 * hx:512 * hx + 512],
                                                lhsT=xg[t - 4 * G][:, 128 * ic:128 * ic + 128],
                                                rhs=bw[:, 512 * hx:512 * hx + 512],
                                                start=(ic == 0), stop=False)
                                    return ins
                                kb.op("tensor", yhmm, wait=[e_w] + ([e_ylev] if ic == 0 else []))
                                wh_r.set_free(sw, tick(kb, "tensor"))

                            def yhbias(p):
                                for ii in range(2):
                                    for hx in range(2):
                                        ins = p.matmul(
                                            out=pa[ii][:, 512 * hx:512 * hx + 512],
                                            lhsT=ones_row[:],
                                            rhs=bias_sb[0:1, 512 * hx:512 * hx + 512],
                                            start=False, stop=True)
                                return ins
                            kb.op("tensor", yhbias, wait=e_bld)
                            e_bias_used = tick(kb, "tensor")
                            e_yh = kb.mark("tensor")

                            def comb(p, hc=hc):
                                p.tensor_scalar(out=yl_sb[0], in0=yl_sb[0],
                                                scalar1=om_sb[0][:], scalar2=None, op0=ALU.mult)
                                p.tensor_scalar(out=yl_sb[1], in0=yl_sb[1],
                                                scalar1=om_sb[1][:], scalar2=None, op0=ALU.mult)
                                p.scalar_tensor_tensor(out=yl_sb[0], in0=pa[0],
                                                       scalar=mk_sb[0][:], in1=yl_sb[0],
                                                       op0=ALU.mult, op1=ALU.add)
                                p.scalar_tensor_tensor(out=yl_sb[1], in0=pa[1],
                                                       scalar=mk_sb[1][:], in1=yl_sb[1],
                                                       op0=ALU.mult, op1=ALU.add)
                                p.tensor_tensor(out=x1_sb[0][:, hc:hc + 1024], in0=yl_sb[0],
                                                in1=u_sb[0][:], op=ALU.add)
                                ins = p.tensor_tensor(out=x1_sb[1][:, hc:hc + 1024],
                                                      in0=yl_sb[1], in1=u_sb[1][:],
                                                      op=ALU.add)
                                return ins
                            kb.op("vector", comb, wait=[e_yh, e_om, e_ylev])
                            e_x1.append(kb.mark("vector"))
                            ptb.free[0] = e_x1[-1]
                            ptb.free[1] = e_x1[-1]
                            if stop == "C" + str(qtr) and pair == 0:
                                ev = kb.dma("gpsimd", x1_out[0:128, 0:1024],
                                            x1_sb[0][:, 0:1024], wait=e_x1[-1], sem="stdbg")
                                kb.wait("gpsimd", ev)
                                pass
                                pass
                        e_ffn_done = e_x1[-1]
                        e_prev_comb = e_ffn_done
                        if stop == "CF" and pair == 0:
                            ev = kb.dma("gpsimd", x1_out[0:128, :], x1_sb[0][:],
                                        wait=e_ffn_done, sem="stdbg")
                            kb.wait("gpsimd", ev)
                            pass
                            pass
                        for j in (2 * pair, 2 * pair + 1):
                            e_xg[j] = e_ffn_done

                        for ii, t in enumerate(ts_):
                            store_evs.append(kb.dma("gpsimd", x1_out[128 * t:128 * t + 128, :],
                                                    x1_sb[ii][:], wait=e_ffn_done,
                                                    sem=f"stx1_{ii}"))
                            e_x1st = store_evs[-1]

                            def stats_v(p, ii=ii):
                                p.tensor_reduce(out=ssum[:], in_=x1_sb[ii][:], axis=AX.X,
                                                op=ALU.add)
                                p.drain()
                                p.tensor_scalar(out=negm[:], in0=ssum[:], scalar1=-1.0 / D,
                                                scalar2=None, op0=ALU.mult)
                                return p.drain()
                            kb.op("vector", stats_v, wait=e_ffn_done)
                            e_negm = kb.mark("vector")

                            def stats_s(p, ii=ii):
                                p.activation(out=xn1bf[:], in_=x1_sb[ii][:], func=AF.Square,
                                             accum_out=ssq[:])
                                p.drain()
                                return p.activation(out=msq[:], in_=negm[:], func=AF.Square)
                            kb.op("scalar", stats_s, wait=[e_ffn_done, e_negm])
                            e_msq = kb.mark("scalar")

                            def var_v(p):
                                return p.scalar_tensor_tensor(out=var[:], in0=ssq[:],
                                                              scalar=1.0 / D, in1=msq[:],
                                                              op0=ALU.mult, op1=ALU.subtract)
                            kb.op("vector", var_v, wait=e_msq)
                            e_var = kb.mark("vector")

                            def std_s(p):
                                return p.activation(out=stdv[:], in_=var[:], func=AF.Sqrt,
                                                    bias=epsb[:])
                            kb.op("scalar", std_s, wait=e_var)
                            e_std = kb.mark("scalar")

                            def xn_v(p, ii=ii):
                                p.reciprocal(out=rstd[:], in_=stdv[:])
                                p.drain()
                                return p.tensor_scalar(out=xn1bf[:], in0=x1_sb[ii][:],
                                                       scalar1=negm[:], scalar2=rstd[:],
                                                       op0=ALU.add, op1=ALU.mult)
                            kb.op("vector", xn_v, wait=e_std)
                            e_xn1 = kb.mark("vector")
                            ln_free[ii] = [e_xn1, e_x1st]
                            if stop == "L1" and pair == 0 and ii == 0:
                                ev = kb.dma("gpsimd", x1_out[0:128, 0:D // 2], xn1bf[:].bitcast(F32),
                                            wait=e_xn1, sem="stdbg")
                                kb.wait("gpsimd", [ev, e_x1st])
                                pass
                                pass

                            e_tr_ev = [None, None]
                            for half2 in range(1 if stop in ("L2A", "L2B") else 2):
                                if half2 == 0 and ii == 0 and pair == 0 and G == 0:
                                    pass
                                bxt = ptb.bufs[half2]
                                evxt = ptb.free[half2]
                                for g4 in range(4):
                                    grp = 4 * half2 + g4

                                    def tr(p, grp=grp):
                                        for q4 in range(4):
                                            icb = 4 * grp + q4
                                            ins = p.matmul(
                                                out=ptq[grp % 2][:, 128 * q4:128 * q4 + 128],
                                                lhsT=xn1bf[:, 128 * icb:128 * icb + 128],
                                                rhs=ident_bf[:], start=True, stop=True)
                                        return ins
                                    kb.op("tensor", tr,
                                          wait=[e_xn1] + ([ev_evac[grp % 2]]
                                                          if ev_evac[grp % 2] else []))
                                    e_trm = kb.mark("tensor")
                                    kb.op("vector",
                                          lambda p, bxt=bxt, g4=g4, grp=grp:
                                          p.tensor_copy(out=bxt[:, 512 * g4:512 * g4 + 512],
                                                        in_=ptq[grp % 2]),
                                          wait=[e_trm] + ([evxt] if g4 == 0 and evxt else []))
                                    e_tr_ev[half2] = kb.mark("vector")
                                    ev_evac[grp % 2] = e_tr_ev[half2]
                                if stop != "L2B":
                                    store_evs.append(kb.dma(
                                        "gpsimd", xnT1[t][:, 2048 * half2:2048 * half2 + 2048],
                                        bxt[:], wait=e_tr_ev[half2], sem=f"stxn{half2}"))
                                    ptb.free[half2] = store_evs[-1]
                            e_tr_ev = e_tr_ev[0] if stop in ("L2A", "L2B") else e_tr_ev[1]
                            if stop in ("L2", "L2A", "L2B") and pair == 0 and ii == 0:
                                if stop == "L2B":
                                    ev = kb.dma("gpsimd", x1_out[0:128, 0:1024],
                                                ptb.bufs[0][:].bitcast(F32), wait=e_tr_ev,
                                                sem="stdbg")
                                    kb.wait("gpsimd", [ev] + store_evs)
                                    pass
                                    pass
                                kb.wait("gpsimd", store_evs)
                                pass
                                pass
                            def msum_mm(p):
                                for icb in range(32):
                                    ins = p.matmul(out=pmsum[:, icb:icb + 1],
                                                   lhsT=xn1bf[:, 128 * icb:128 * icb + 128],
                                                   rhs=ones_col[:], start=True, stop=True)
                                return ins
                            kb.op("tensor", msum_mm, wait=e_tr_ev)
                            e_msmm = kb.mark("tensor")

                            def msum_acc(p):
                                p.tensor_copy(out=msev[:], in_=pmsum)
                                p.drain()
                                return p.tensor_tensor(out=msum_f[:], in0=msum_f[:],
                                                       in1=msev[:], op=ALU.add)
                            kb.op("vector", msum_acc, wait=e_msmm)
                            e_msacc = kb.mark("vector")
                        if stop == "LN" and pair == 0:
                            ev = kb.dma("gpsimd", x1_out[0:128, 0:32], msum_f[:],
                                        wait=e_msacc, sem="stdbg")
                            kb.wait("gpsimd", ev)
                            kb.wait("gpsimd", store_evs)
                            pass
                            pass
                    e_ctx_free = e_op

                store_evs.append(kb.dma("gpsimd", msum_out, msum_f[:], wait=e_msacc,
                                        sem="stms"))
                kb.wait("gpsimd", store_evs)
                pass
        _pc()
        kb_barrier(kb)
        evms = kb_collective(kb, "ms", [msum_d], [msum_all_t],
                             [[2 * c, 2 * c + 1] for c in range(B)])
        for eng in ENGS:
            kb.wait(eng, evms)

        def _pd():
            kcache, vcache, bt = kcache_a, vcache_a, bt_a
            msum_pair = msum_all_t.rearrange("(two p) c -> two p c", two=2)
            wqfb = wqf_all
            keysb = keysT_all[0:KQ, :]
            identin = identin_a
            kT1, v1 = kT1_d, v1_d
            coeffs = coeffs_d
            kg_d, vg_d = kg_d_t, vg_d_t
            with ExitStack() as ctx:
                sb = lambda n, sh, dt: ctx.enter_context(nc.sbuf_tensor("PD" + n, sh, dt))
                gbuf = Rot([sb(f"gbuf{i}", [128, 4096], F32) for i in range(2)], "sg")
                idxs = sb("idxs", [128, 2], I32)
                ktile = Rot([sb(f"ktile{i}", [128, 128], F32) for i in range(2)], "skt")
                kbf = sb("kbf", [128, 128], BF16)
                kbf2 = sb("kbf2", [128, 128], BF16)
                ktr_b = Rot([sb(f"ktr{i}", [128, 128], BF16) for i in range(2)])
                ms0 = sb("ms0", [128, 32], F32)
                ms1 = sb("ms1", [128, 32], F32)
                xbar = sb("xbar", [128, 32], BF16)
                wqf_sb = sb("wqf_sb", [128, 32 * KQ], BF16)
                qv = sb("qv", [128, 1], BF16)
                keys_sb = sb("keys_sb", [128, E], BF16)
                sc8 = sb("sc8", [1, E], F32)
                mx8 = sb("mx8", [1, 1], F32)
                nmx8 = sb("nmx8", [1, 1], F32)
                ex8 = sb("ex8", [1, E], F32)
                sm8 = sb("sm8", [1, 1], F32)
                co8 = sb("co8", [1, E], F32)
                identf = sb("identf", [128, 128], F32)
                ident_bf = sb("ident_bf", [128, 128], BF16)
                pool_t = ctx.enter_context(nc.psum_tensor("PDpool", [128, 2048], F32))
                pool = pool_t[:]
                ptr = [pool[:, 512:640], pool[:, 640:768]]
                pq128 = pool[:, 1024:1025]
                psc = pool[0:1, 1536:1544]


                e_id = kb.dma("sync", identf[:], identin, sem="sprep")
                e_bt0 = kb.dma("sync", idxs[:, 0:1],
                               bt[0:128].rearrange("(p one) -> p one", one=1), sem="sprep")
                e_bt = kb.dma("sync", idxs[:, 1:2],
                              bt[128:256].rearrange("(p one) -> p one", one=1), sem="sprep")
                kb.op("vector", lambda p: p.tensor_copy(out=ident_bf[:], in_=identf[:]),
                      wait=e_id)
                e_prep = kb.mark("vector")

                e_g = []
                for src, dst in [(kcache, kg_d), (vcache, vg_d)]:
                    for hh in range(2):
                        for z in range(4):
                            bg, evg = gbuf.get()
                            sg_ = gbuf.slot

                            def gat(p, src=src, bg=bg, hh=hh, z=z):
                                return p.indirect_dma_start(
                                    out=bg[:], out_offset=None,
                                    in_=src,
                                    in_offset=IndirectOffsetOnAxis(idxs[:, hh:hh + 1], 0),
                                    element_offset=4096 * z)
                            kb.wait("gpsimd", [e_bt] + ([evg] if evg else []))
                            ev1 = kb._bump("sgat", 16)
                            hname = kb.sem("sgat")[0]
                            kb.q["gpsimd"].append(
                                lambda p, fn=gat, hname=hname: fn(p).then_inc(hname, 16))
                            ev2 = kb.dma("gpsimd", dst[128 * hh:128 * hh + 128,
                                                      4096 * z:4096 * z + 4096],
                                         bg[:], wait=ev1, sem="sgst")
                            gbuf.set_free(sg_, ev2)
                            e_g.append(ev2)

                st_evs = []
                ev_tr_ev = [None, None]
                e_vst_prev = None
                e_tr_prev = None
                for kvh in range(NKV):
                    for ch in range(32):
                        bkt, evkt = ktile.get()
                        skt_ = ktile.slot
                        e_t = kb.dma("sync", bkt[:],
                                     vg_d.rearrange("(c q) (h ps d) -> c q h ps d",
                                                    q=8, h=NKV, ps=16)[ch][:, kvh, :, :],
                                     wait=([evkt] + e_g) if (ch == 0 and kvh == 0) else evkt,
                                     sem=ktile.sem())

                        def cast_v(p, bkt=bkt):
                            return p.tensor_copy(out=kbf[:], in_=bkt[:])
                        kb.op("vector", cast_v, wait=[e_t] + ([e_vst_prev] if e_vst_prev else []))
                        e_c = kb.mark("vector")
                        ktile.set_free(skt_, e_c)
                        st_evs.append(kb.dma(
                            "gpsimd",
                            v1.rearrange("(h p) (c dd) -> h p c dd", p=128,
                                         dd=128)[kvh, :, ch, :],
                            kbf[:], wait=e_c, sem="stv"))
                        e_vst_prev = st_evs[-1]

                        bkt2, evkt2 = ktile.get()
                        skt2_ = ktile.slot
                        e_t2 = kb.dma("sync", bkt2[:],
                                      kg_d.rearrange("(c q) (h ps d) -> c q h ps d",
                                                     q=8, h=NKV, ps=16)[ch][:, kvh, :, :],
                                      wait=evkt2, sem=ktile.sem())

                        def cast_k(p, bkt2=bkt2):
                            return p.tensor_copy(out=kbf2[:], in_=bkt2[:])
                        kb.op("vector", cast_k,
                              wait=[e_t2] + ([e_tr_prev] if e_tr_prev else []))
                        e_c2 = kb.mark("vector")
                        ktile.set_free(skt2_, e_c2)

                        def trk(p, ch=ch):
                            return p.matmul(out=ptr[ch % 2], lhsT=kbf2[:], rhs=ident_bf[:],
                                            start=True, stop=True)
                        kb.op("tensor", trk, wait=[e_c2, e_prep] +
                              ([ev_tr_ev[ch % 2]] if ev_tr_ev[ch % 2] else []))
                        e_tr = kb.mark("tensor")
                        e_tr_prev = e_tr
                        btr, evtr = ktr_b.get()
                        str_ = ktr_b.slot
                        kb.op("vector",
                              lambda p, btr=btr, ch=ch: p.tensor_copy(out=btr[:], in_=ptr[ch % 2]),
                              wait=[e_tr, evtr])
                        e_te = kb.mark("vector")
                        ev_tr_ev[ch % 2] = e_te
                        st_evs.append(kb.dma("gpsimd", kT1[kvh, :, 128 * ch:128 * ch + 128],
                                             btr[:], wait=e_te, sem="stk"))
                        ktr_b.set_free(str_, st_evs[-1])

                e_m0 = kb.dma("sync", ms0[:], msum_pair[0], sem="sms")
                e_m1 = kb.dma("sync", ms1[:], msum_pair[1], sem="sms")
                e_wqf = None
                for z in range(32):
                    e_wqf = kb.dma("sync", wqf_sb[:, KQ * z:KQ * z + KQ],
                                   wqfb[128 * z:128 * z + 128, :], sem="swqf")
                e_keys = kb.dma("sync", keys_sb[:], keysb, sem="swqf")

                def xbar_v(p):
                    p.tensor_tensor(out=ms0[:], in0=ms0[:], in1=ms1[:], op=ALU.add)
                    p.drain()
                    return p.tensor_scalar(out=xbar[:], in0=ms0[:], scalar1=1.0 / S,
                                           scalar2=None, op0=ALU.mult)
                kb.op("vector", xbar_v, wait=[e_m0, e_m1])
                e_xb = kb.mark("vector")

                def qmm(p):
                    for z in range(32):
                        ins = p.matmul(out=pq128, lhsT=wqf_sb[:, KQ * z:KQ * z + KQ],
                                       rhs=xbar[:, z:z + 1], start=(z == 0), stop=(z == 31))
                    return ins
                kb.op("tensor", qmm, wait=[e_xb, e_wqf])
                e_q = kb.mark("tensor")
                kb.op("vector", lambda p: p.tensor_copy(out=qv[:], in_=pq128), wait=e_q)
                e_qv = kb.mark("vector")

                def scmm(p):
                    return p.matmul(out=psc, lhsT=qv[:], rhs=keys_sb[:], start=True, stop=True)
                kb.op("tensor", scmm, wait=[e_qv, e_keys])
                e_sc = kb.mark("tensor")

                def smax(p):
                    p.tensor_scalar(out=sc8[:], in0=psc, scalar1=1.0 / np.sqrt(float(KQ)),
                                    scalar2=None, op0=ALU.mult)
                    p.drain()
                    p.tensor_reduce(out=mx8[:], in_=sc8[:], axis=AX.X, op=ALU.max)
                    p.drain()
                    return p.tensor_scalar(out=nmx8[:], in0=mx8[:], scalar1=-1.0,
                                           scalar2=None, op0=ALU.mult)
                kb.op("vector", smax, wait=e_sc)
                e_mx = kb.mark("vector")

                def sexp(p):
                    return p.activation(out=ex8[:], in_=sc8[:], func=AF.Exp, bias=nmx8[:],
                                        accum_out=sm8[:])
                kb.op("scalar", sexp, wait=e_mx)
                e_ex = kb.mark("scalar")

                def sdiv(p):
                    p.reciprocal(out=sm8[:], in_=sm8[:])
                    p.drain()
                    return p.tensor_scalar(out=co8[:], in0=ex8[:], scalar1=sm8[:],
                                           scalar2=None, op0=ALU.mult)
                kb.op("vector", sdiv, wait=e_ex)
                e_co = kb.mark("vector")
                st_evs.append(kb.dma("gpsimd", coeffs, co8[:], wait=e_co, sem="stc"))

                kb.wait("gpsimd", st_evs)
                pass
        _pd()
        kb_barrier(kb)

        def _pe(n_groups=4):
            xnT1o = xnT1_d.rearrange("(t p) d -> t p d", p=128)
            kT1, v1 = kT1_d, v1_d
            x1_in = x1_d
            wq1b, wo1b, worgb = wq1_all, wo1_all, worg_all
            downb, upb = down_all, up_all
            co_in = coeffs_d
            rqc, rqs = rqc1_a, rqs1_a
            identin = identin_a
            out_d = out_a
            with ExitStack() as ctx:
                sb = lambda n, sh, dt: ctx.enter_context(nc.sbuf_tensor("PE" + n, sh, dt))
                xg = [sb(f"xg{t}", [128, D], BF16) for t in range(4)]
                wqc = Rot([sb(f"wqc{i}", [128, 512], BF16) for i in range(2)], "swqc")
                q_sb = sb("q_sb", [128, 512], F32)
                qt1 = sb("qt1", [128, 512], F32)
                qt2 = sb("qt2", [128, 512], F32)
                qrbf = sb("qrbf", [128, 512], BF16)
                rqc_sb = [sb(f"rqc{t}", [128, HD], F32) for t in range(4)]
                rqs_sb = [sb(f"rqs{t}", [128, HD], F32) for t in range(4)]
                qT_sb = sb("qT_sb", [128, NH * 512], BF16)
                kt_sb = sb("kt_sb", [128, S], BF16)
                v_sb = sb("v_sb", [128, S], BF16)
                ptb = Rot([sb(f"ptb{i}", [128, 2048], BF16) for i in range(2)])
                absq = sb("absq", [128, 512], BF16)
                maxkf = sb("maxkf", [128, 1], F32)
                maxkb = sb("maxkb", [128, 1], BF16)
                bstar = sb("bstar", [1, 1], F32)
                btmp = sb("btmp", [1, 1], F32)
                bstar_b = sb("bstar_b", [1, 1], BF16)
                nb_sb = sb("nb_sb", [128, 1], F32)
                sum_sb = sb("sum_sb", [1, 512], F32)
                rec_bf = sb("rec_bf", [1, 512], BF16)
                recb_sb = sb("recb_sb", [128, 512], F32)
                ctx_sb = sb("ctx_sb", [128, NH * 512], BF16)
                ones_row = sb("ones_row", [1, 128], BF16)
                ones_col = sb("ones_col", [128, 1], BF16)
                co_row = sb("co_row", [1, E], F32)
                coex = sb("coex", [1, ER], BF16)
                cb_sb = sb("cb_sb", [128, ER], F32)
                hsc = [sb(f"hsc{t}", [128, ER], BF16) for t in range(4)]
                hT_sb = [sb(f"hT{t}", [128, ER], BF16) for t in range(4)]
                wch = Rot([sb(f"wch{i}", [128, 1024], BF16) for i in range(2)], "swch")
                wch2 = Rot([sb(f"wch2{i}", [128, 1024], BF16) for i in range(2)], "swch2")
                upc = sb("upc", [128, 4 * 1024], BF16)
                u_sb = [sb(f"u_sb{i}", [128, 1024], F32) for i in range(2)]
                out_sb = [sb(f"out_sb{i}", [128, 1024], F32) for i in range(2)]
                identf = sb("identf", [128, 128], F32)
                ident_bf = sb("ident_bf", [128, 128], BF16)

                pool_t = ctx.enter_context(nc.psum_tensor("PEpool", [128, 4096], F32))
                pool = pool_t[:]
                pq = [pool[:, 0:512], pool[:, 512:1024]]
                ptq = [pool[:, 2048:2560], pool[:, 2560:3072]]
                pst = pool[:, 0:2048]
                pctx = pool[:, 2048:2560]
                psum_s = pool[0:1, 2560:3072]
                pnb = pool[:, 3072:3073]
                prec = pool[:, 3584:4096]
                pa = [pool[:, 0:1024], pool[:, 1024:2048]]
                ph = pool[:, 2048:2560]
                pcb = pool[:, 3584:4096]


                e_id = kb.dma("sync", identf[:], identin, sem="sprep")
                e_co = kb.dma("sync", co_row[:], co_in, sem="sprep")

                def prep_v(p):
                    p.tensor_copy(out=ident_bf[:], in_=identf[:])
                    p.memset(ones_row[:], 1.0)
                    p.memset(ones_col[:], 1.0)
                    return p.tensor_copy(
                        out=coex[:].rearrange("o (e r) -> o e r", e=E),
                        in_=co_row[:].unsqueeze(2).broadcast_to([1, E, R]))
                kb.op("vector", prep_v, wait=[e_id, e_co])
                e_prep = kb.mark("vector")

                def cbmm(p):
                    return p.matmul(out=pcb[:, 0:512], lhsT=ones_row[:], rhs=coex[:],
                                    start=True, stop=True)
                kb.op("tensor", cbmm, wait=e_prep)
                e_cb = kb.mark("tensor")
                kb.op("vector", lambda p: p.tensor_copy(out=cb_sb[:], in_=pcb[:, 0:512]),
                      wait=e_cb)
                e_cbs = kb.mark("vector")

                store_evs = []
                e_up_used = None
                e_xg = [None] * 4
                e_qT_free = None
                e_ctx_free = None
                e_kv_free = None
                e_out_free = [None, None]
                e_hT_free = [None] * 4
                e_prev_comb = None

                for G in range(n_groups):
                    e_xg_ld = []
                    for j in range(4):
                        t = 4 * G + j
                        e_xg_ld.append(kb.dma("sync", xg[j][:], xnT1o[t], wait=e_xg[j],
                                              sem=f"sxg1_{j}"))
                        e_xg_ld.append(kb.dma("sync", rqc_sb[j][:], rqc[128 * t:128 * t + 128, :],
                                              sem=f"srq{j}"))
                        e_xg_ld.append(kb.dma("sync", rqs_sb[j][:], rqs[128 * t:128 * t + 128, :],
                                              sem=f"srq{j}"))
                    for r in range(8):
                        for tp in range(2):
                            for ic in range(32):
                                bw, evw = wqc.get()
                                sw = wqc.slot
                                e_w = kb.dma("sync" if sw == 0 else "scalar", bw[:],
                                             wq1b[128 * ic:128 * ic + 128,
                                                  512 * r:512 * r + 512], wait=evw,
                                             sem=wqc.sem())

                                def qmm(p, bw=bw, ic=ic, tp=tp):
                                    for jj in range(2):
                                        j = 2 * tp + jj
                                        ins = p.matmul(out=pq[jj], rhs=bw[:],
                                                       lhsT=xg[j][:, 128 * ic:128 * ic + 128],
                                                       start=(ic == 0), stop=(ic == 31))
                                    return ins
                                w0 = [e_w]
                                if ic == 0:
                                    w0 += e_xg_ld
                                    if e_qT_free is not None and r == 0 and tp == 0:
                                        w0.append(e_qT_free)
                                kb.op("tensor", qmm, wait=w0)
                                wqc.set_free(sw, tick(kb, "tensor"))
                            e_pq = kb.mark("tensor")
                            for jj in range(2):
                                j = 2 * tp + jj

                                def roper(p, jj=jj, j=j):
                                    p.tensor_copy(out=q_sb[:], in_=pq[jj])
                                    p.drain()
                                    kview = q_sb[:].rearrange("p (h d) -> p h d", h=4)
                                    cos4 = rqc_sb[j][:].rearrange("p (o d) -> p o d", o=1) \
                                        .broadcast_to([128, 4, HD])
                                    k4 = q_sb[:].rearrange("p (h two pr) -> p h two pr",
                                                           h=4, two=2)
                                    o4 = qt2[:].rearrange("p (h two pr) -> p h two pr",
                                                          h=4, two=2)
                                    s2 = rqs_sb[j][:].rearrange("p (two pr) -> p two pr", two=2)
                                    p.tensor_tensor(out=qt1[:].rearrange("p (h d) -> p h d", h=4),
                                                    in0=kview, in1=cos4, op=ALU.mult)
                                    p.tensor_tensor(out=o4[:, :, 0:1, :], in0=k4[:, :, 1:2, :],
                                                    in1=s2[:, 0:1, :].unsqueeze(1)
                                                    .broadcast_to([128, 4, 1, HD // 2]),
                                                    op=ALU.mult)
                                    p.tensor_tensor(out=o4[:, :, 1:2, :], in0=k4[:, :, 0:1, :],
                                                    in1=s2[:, 1:2, :].unsqueeze(1)
                                                    .broadcast_to([128, 4, 1, HD // 2]),
                                                    op=ALU.mult)
                                    p.drain()
                                    p.tensor_tensor(out=qrbf[:], in0=qt1[:], in1=qt2[:],
                                                    op=ALU.add)
                                    return p.drain()
                                kb.op("vector", roper, wait=e_pq)
                                e_rope = kb.mark("vector")

                                def qtr4(p, jj=jj):
                                    for hq in range(4):
                                        ins = p.matmul(
                                            out=ptq[jj][:, 128 * hq:128 * hq + 128],
                                            lhsT=qrbf[:, 128 * hq:128 * hq + 128],
                                            rhs=ident_bf[:], start=True, stop=True)
                                    return ins
                                kb.op("tensor", qtr4, wait=[e_rope, e_prep])
                                e_qtr = kb.mark("tensor")

                                def qev(p, jj=jj, j=j, r=r):
                                    for hq in range(4):
                                        head = 4 * r + hq
                                        ins = p.tensor_copy(
                                            out=qT_sb[:, 512 * head + 128 * j:
                                                      512 * head + 128 * j + 128],
                                            in_=ptq[jj][:, 128 * hq:128 * hq + 128])
                                    return ins
                                kb.op("vector", qev, wait=e_qtr)
                                e_qev = kb.mark("vector")
                    e_stageA = e_qev

                    e_hT = []
                    for j in range(4):
                        for ic in range(32):
                            bw, evw = wch.get()
                            sw = wch.slot
                            e_dn = kb.dma("gpsimd", bw[:, 0:ER],
                                          downb[128 * ic:128 * ic + 128, :], wait=evw,
                                          sem="sdn" + str(wch.slot))

                            def hmm(p, bw=bw, ic=ic, j=j):
                                return p.matmul(out=ph, lhsT=xg[j][:, 128 * ic:128 * ic + 128],
                                                rhs=bw[:, 0:ER], start=(ic == 0),
                                                stop=(ic == 31))
                            kb.op("tensor", hmm, wait=[e_dn] +
                                  ([e_stageA] if ic == 0 else []) +
                                  ([e_hT_free[j]] if ic == 0 and e_hT_free[j] else []) +
                                  ([e_hT[-1]] if ic == 0 and e_hT else []))
                            wch.set_free(sw, tick(kb, "tensor"))
                        e_hmm = kb.mark("tensor")

                        def hscale(p, j=j):
                            return p.tensor_tensor(out=hsc[j][:], in0=ph, in1=cb_sb[:],
                                                   op=ALU.mult)
                        kb.op("vector", hscale, wait=[e_hmm, e_cbs])
                        e_hs = kb.mark("vector")

                        def htr(p, j=j):
                            for z in range(4):
                                ins = p.matmul(out=ptq[j % 2][:, 128 * z:128 * z + 128],
                                               lhsT=hsc[j][:, 128 * z:128 * z + 128],
                                               rhs=ident_bf[:], start=True, stop=True)
                            return ins
                        kb.op("tensor", htr, wait=[e_hs])
                        e_htr = kb.mark("tensor")

                        def hev(p, j=j):
                            return p.tensor_copy(out=hT_sb[j][:], in_=ptq[j % 2])
                        kb.op("vector", hev, wait=e_htr)
                        e_hT.append(kb.mark("vector"))

                    nkc = 32
                    e_ctx_done = []
                    for kvg in range(NKV):
                        bk, bv = kt_sb, v_sb
                        evkv = e_kv_free
                        e_k = kb.dma("scalar", bk[:], kT1[kvg, :, :],
                                     wait=evkv, sem="sk")
                        e_v = kb.dma("scalar", bv[:],
                                     v1[128 * kvg:128 * kvg + 128, :],
                                     wait=evkv, sem="sv")

                        def maxk_v(p, bk=bk):
                            p.tensor_reduce(out=maxkf[:], in_=bk[:], axis=AX.X,
                                            op=ALU.max, apply_absolute_value=True)
                            p.drain()
                            return p.tensor_copy(out=maxkb[:], in_=maxkf[:])
                        kb.op("vector", maxk_v, wait=e_k)
                        e_maxk = kb.mark("vector")

                        e_bnd_ev = None
                        for z in range(4):
                            def absq_s(p, kvg=kvg, z=z):
                                return p.activation(
                                    out=absq[:],
                                    in_=qT_sb[:, 2048 * kvg + 512 * z:2048 * kvg + 512 * z + 512],
                                    func=AF.Abs)
                            kb.op("scalar", absq_s,
                                  wait=[e_stageA] + ([e_bz] if z else []))
                            e_absq = kb.mark("scalar")

                            def bz(p, z=z):
                                return p.matmul(out=pool[0:1, 3072:3584], lhsT=maxkb[:],
                                                rhs=absq[:], start=True, stop=True)
                            kb.op("tensor", bz, wait=[e_maxk, e_absq] +
                                  ([e_bnd_ev] if e_bnd_ev is not None else []))
                            e_bz = kb.mark("tensor")

                            def bred(p, z=z):
                                p.tensor_copy(out=sum_sb[:], in_=pool[0:1, 3072:3584])
                                p.drain()
                                p.tensor_reduce(out=btmp[:], in_=sum_sb[:], axis=AX.X,
                                                op=ALU.max)
                                p.drain()
                                if z == 0:
                                    return p.tensor_copy(out=bstar[:], in_=btmp[:])
                                return p.tensor_tensor(out=bstar[:], in0=bstar[:],
                                                       in1=btmp[:], op=ALU.max)
                            kb.op("vector", bred, wait=e_bz)
                            e_bnd_ev = kb.mark("vector")

                        def bstar_v(p):
                            return p.tensor_scalar(out=bstar_b[:], in0=bstar[:], scalar1=-1.0,
                                                   scalar2=None, op0=ALU.mult)
                        kb.op("vector", bstar_v, wait=e_bnd_ev)
                        e_bstar = kb.mark("vector")

                        def nbmm(p):
                            return p.matmul(out=pnb, lhsT=ones_row[:], rhs=bstar_b[:],
                                            start=True, stop=True)
                        kb.op("tensor", nbmm, wait=e_bstar)
                        e_nb = kb.mark("tensor")
                        kb.op("vector", lambda p: p.tensor_copy(out=nb_sb[:], in_=pnb),
                              wait=e_nb)
                        e_nbs = kb.mark("vector")

                        for qh in range(4):
                            head = 4 * kvg + qh
                            qcol = 512 * head
                            for bi in range(8):
                                kchunk = list(range(4 * bi, 4 * bi + 4))

                                def smm(p, kchunk=kchunk, bk=bk, qcol=qcol):
                                    for ii, kc in enumerate(kchunk):
                                        ins = p.matmul(out=pst[:, 512 * ii:512 * ii + 512],
                                                       lhsT=bk[:, 128 * kc:128 * kc + 128],
                                                       rhs=qT_sb[:, qcol:qcol + 512],
                                                       start=True, stop=True)
                                    return ins
                                kb.op("tensor", smm, wait=[e_nbs, e_k])
                                e_smm = kb.mark("tensor")

                                bpt, evpt = ptb.get()
                                spt = ptb.slot

                                def expo(p, bpt=bpt):
                                    return p.activation(out=bpt[:], in_=pst[:],
                                                        func=AF.Exp, bias=nb_sb[:])
                                kb.op("scalar", expo, wait=[e_smm, evpt])
                                e_exp = kb.mark("scalar")

                                def pvmm(p, kchunk=kchunk, bv=bv, bpt=bpt,
                                         first=(kchunk[0] == 0), last=(kchunk[-1] == 31)):
                                    for kx, kc in enumerate(kchunk):
                                        p.matmul(out=pctx,
                                                 lhsT=bv[:, 128 * kc:128 * kc + 128],
                                                 rhs=bpt[:, 512 * kx:512 * kx + 512],
                                                 start=(first and kx == 0),
                                                 stop=(last and kx == 3))
                                        ins = p.matmul(out=psum_s,
                                                       lhsT=ones_col[:],
                                                       rhs=bpt[:, 512 * kx:512 * kx + 512],
                                                       start=(first and kx == 0),
                                                       stop=(last and kx == 3))
                                    return ins
                                kb.op("tensor", pvmm, wait=[e_exp, e_v])
                                e_pv_t = tick(kb, "tensor")
                                ptb.set_free(spt, e_pv_t)
                            e_pv = kb.mark("tensor")

                            def sums_v(p):
                                p.tensor_copy(out=sum_sb[:], in_=psum_s)
                                p.drain()
                                p.reciprocal(out=sum_sb[:], in_=sum_sb[:])
                                p.drain()
                                return p.tensor_copy(out=rec_bf[:], in_=sum_sb[:])
                            kb.op("vector", sums_v, wait=e_pv)
                            e_rec = kb.mark("vector")

                            def recmm(p):
                                return p.matmul(out=prec, lhsT=ones_row[:], rhs=rec_bf[:],
                                                start=True, stop=True)
                            kb.op("tensor", recmm, wait=[e_rec, e_cbs])
                            e_recb = kb.mark("tensor")

                            def ctxev(p, qcol=qcol):
                                p.tensor_copy(out=recb_sb[:], in_=prec)
                                p.drain()
                                return p.tensor_tensor(out=ctx_sb[:, qcol:qcol + 512],
                                                       in0=pctx, in1=recb_sb[:], op=ALU.mult)
                            w = [e_recb]
                            if e_ctx_free is not None and head == 0:
                                w.append(e_ctx_free)
                            kb.op("vector", ctxev, wait=w)
                            e_ctx_done.append(kb.mark("vector"))
                        e_kv_free = e_ctx_done[-1]
                    e_stageB = e_ctx_done[-1]
                    e_qT_free = e_stageB

                    for pair in range(2):
                        ts_ = [4 * G + 2 * pair, 4 * G + 2 * pair + 1]
                        e_x1 = []
                        for qtr in range(4):
                            hc = 1024 * qtr
                            e_xo = []
                            for ii, t in enumerate(ts_):
                                e_xo.append(kb.dma("sync", u_sb[ii][:],
                                                   x1_in[128 * t:128 * t + 128, hc:hc + 1024],
                                                   wait=(e_x1[-1] if e_x1 else
                                                         (e_prev_comb if e_prev_comb
                                                          else e_prep)), sem=f"sxo{ii}"))
                            for ic in range(32):
                                bw, evw = wch.get()
                                sw = wch.slot
                                e_w = kb.dma("scalar", bw[:],
                                             wo1b[128 * ic:128 * ic + 128, hc:hc + 1024],
                                             wait=evw, sem=wch.sem())

                                def opmm(p, bw=bw, ic=ic, ts_=ts_, G=G):
                                    for ii, t in enumerate(ts_):
                                        j = t - 4 * G
                                        for hx in range(2):
                                            ins = p.matmul(
                                                out=pa[ii][:, 512 * hx:512 * hx + 512],
                                                lhsT=ctx_sb[:, 512 * ic + 128 * j:
                                                            512 * ic + 128 * j + 128],
                                                rhs=bw[:, 512 * hx:512 * hx + 512],
                                                start=(ic == 0), stop=False)
                                    return ins
                                w_op = [e_w]
                                if ic == 0:
                                    w_op.append(e_stageB)
                                    if e_x1:
                                        w_op.append(e_x1[-1])
                                    elif e_prev_comb is not None:
                                        w_op.append(e_prev_comb)
                                kb.op("tensor", opmm, wait=w_op)
                                wch.set_free(sw, tick(kb, "tensor"))
                            for ic in range(32):
                                bw, evw = wch2.get()
                                sw = wch2.slot
                                e_w = kb.dma("gpsimd", bw[:],
                                             worgb[128 * ic:128 * ic + 128, hc:hc + 1024],
                                             wait=evw, sem=wch2.sem())

                                def bamm(p, bw=bw, ic=ic, ts_=ts_, G=G):
                                    for ii, t in enumerate(ts_):
                                        j = t - 4 * G
                                        for hx in range(2):
                                            ins = p.matmul(
                                                out=pa[ii][:, 512 * hx:512 * hx + 512],
                                                lhsT=xg[j][:, 128 * ic:128 * ic + 128],
                                                rhs=bw[:, 512 * hx:512 * hx + 512],
                                                start=False, stop=False)
                                    return ins
                                kb.op("tensor", bamm, wait=[e_w])
                                wch2.set_free(sw, tick(kb, "tensor"))
                            for z in range(4):
                                e_up = kb.dma("sync", upc[:, 1024 * z:1024 * z + 1024],
                                              upb[128 * z:128 * z + 128, hc:hc + 1024],
                                              sem="sup", wait=e_up_used if z == 0 else None)

                            def dmm(p, ts_=ts_, G=G):
                                for ii, t in enumerate(ts_):
                                    j = t - 4 * G
                                    for z in range(4):
                                        last = (z == 3)
                                        for hx in range(2):
                                            ins = p.matmul(
                                                out=pa[ii][:, 512 * hx:512 * hx + 512],
                                                lhsT=hT_sb[j][:, 128 * z:128 * z + 128],
                                                rhs=upc[:, 1024 * z + 512 * hx:
                                                        1024 * z + 512 * hx + 512],
                                                start=False, stop=last)
                                return ins
                            kb.op("tensor", dmm, wait=[e_up] + e_hT)
                            e_up_used = tick(kb, "tensor")
                            e_mm = kb.mark("tensor")

                            def comb(p, hc=hc):
                                p.scalar_tensor_tensor(out=out_sb[0][:], in0=pa[0],
                                                       scalar=1.0, in1=u_sb[0][:],
                                                       op0=ALU.mult, op1=ALU.add)
                                ins = p.scalar_tensor_tensor(out=out_sb[1][:], in0=pa[1],
                                                             scalar=1.0, in1=u_sb[1][:],
                                                             op0=ALU.mult, op1=ALU.add)
                                return ins
                            kb.op("vector", comb, wait=[e_mm] + e_xo +
                                  [e_out_free[0], e_out_free[1]])
                            e_x1.append(kb.mark("vector"))

                            for ii, t in enumerate(ts_):
                                store_evs.append(kb.dma(
                                    "gpsimd", out_d[128 * t:128 * t + 128, hc:hc + 1024],
                                    out_sb[ii][:], wait=e_x1[-1], sem=f"sto{ii}"))
                                e_out_free[ii] = store_evs[-1]
                        e_prev_comb = e_x1[-1]
                        for j in (2 * pair, 2 * pair + 1):
                            e_xg[j] = e_prev_comb
                            e_hT_free[j] = e_prev_comb
                    e_ctx_free = e_mm

                kb.wait("gpsimd", store_evs)
                pass
        _pe()
        kb_barrier(kb)
        if dbg:
            with ExitStack() as dctx:
                dsb = dctx.enter_context(nc.sbuf_tensor("dsb", [128, 4096], F32))
                for nm, src in [("dbg_wq0a", wq0_all[512:640, :]),
                                ("dbg_wlta", wlT_all[4096:4224, 0:512]),
                                ("dbg_kt0a", kT0_all[512:640, 0:4096]),
                                ("dbg_v0a", v0_all[512:640, 0:4096]),
                                ("dbg_xnt0", xnT0_d[0:128, :]),
                                ("dbg_x1", x1_d[0:128, :]),
                                ("dbg_xnt1", xnT1_d[0:128, :]),
                                ("dbg_msum", msum_all_t[0:128, 0:32]),
                                ("dbg_co", coeffs_d),
                                ("dbg_kt1", kT1_d[0][0:128, 0:4096]),
                                ("dbg_v1", v1_d[0:128, 0:4096]),
                                ("dbg_mask", maskv_d[0:128]
                                 .rearrange("(p one) -> p one", one=1))]:
                    shp = list(src.shape)
                    dt_ = src.dtype
                    dout = nc.dram_tensor(nm, shp, dt_, kind="ExternalOutput").ap()
                    ev = kb.dma("gpsimd", dout, src, sem="sdbg")
                kb.wait("gpsimd", ev)
        kb.build(block)
    return nc


def kernel_unfused(**inputs):
    h = host_prep(inputs)
    cores = list(range(NCORE))
    own_g = {hh: [2 * t + hh for t in range(OWN_T)] for hh in range(2)}
    own_rows = {hh: np.concatenate([np.arange(128 * g, 128 * g + 128)
                                    for g in own_g[hh]]) for hh in range(2)}
    sq = np.float32(1.0 / np.sqrt(128.0))
    triu = np.triu(np.ones((128, 128), np.float32))
    onesm = np.ones((128, 128), np.float32)
    zerosm = np.zeros((128, 128), np.float32)

    r1 = run_bass_kernel_spmd(build_prep(), prep_in_maps(h), core_ids=cores)
    w = assemble_prep(r1)

    maps2 = []
    for core in cores:
        c, hh = core // 2, core % 2
        maps2.append(dict(
            xc=h["x"][c],
            wk_cols=np.ascontiguousarray(w["wk0"][:, 512 * hh:512 * hh + 512]),
            wv_cols=np.ascontiguousarray(w["wv0"][:, 512 * hh:512 * hh + 512]),
            ropec=h["rope0_cos_t"], ropes=h["rope0_sin_t"], identin=h["ident"]))
    r2 = run_bass_kernel_spmd(build_pa(), maps2, core_ids=cores)

    maps3 = []
    for core in cores:
        c, hh = core // 2, core % 2
        kT0 = np.concatenate([np.asarray(r2.results[2 * c]["kT0h"]),
                              np.asarray(r2.results[2 * c + 1]["kT0h"])], axis=0)
        v0 = np.concatenate([np.asarray(r2.results[2 * c]["v0h"]),
                             np.asarray(r2.results[2 * c + 1]["v0h"])], axis=0)
        xnT0 = np.asarray(r2.results[core]["xnT0"])
        maskv = np.asarray(r2.results[core]["maskv"])
        rows = own_rows[hh]
        maps3.append(dict(
            xnT0o=np.ascontiguousarray(xnT0[own_g[hh]]),
            kT0=kT0, v0=v0,
            maskvo=np.ascontiguousarray(maskv[rows]),
            xo_in=np.ascontiguousarray(h["x"][c][rows]),
            wq0b=w["wq0"], wo0b=w["wo0"], wlT=w["wlT"], whT=w["whT"],
            bias_in=h["bias"],
            rqc=np.ascontiguousarray(h["rope0_cos_t"][rows] * sq),
            rqs=np.ascontiguousarray(h["rope0_sin_t"][rows] * sq),
            m0in=(triu if hh == 0 else onesm),
            m1in=(zerosm if hh == 0 else triu),
            identin=h["ident"]))
    r3 = run_bass_kernel_spmd(build_pc(), maps3, core_ids=cores)

    maps4 = []
    for core in cores:
        c, hh = core // 2, core % 2
        msum_pair = np.stack([np.asarray(r3.results[2 * c]["msum_out"]),
                              np.asarray(r3.results[2 * c + 1]["msum_out"])])
        maps4.append(dict(
            kcache=h["kcache"], vcache=h["vcache"], bt=h["bt"],
            msum_pair=msum_pair, wqfb=w["wqf"], keysb=w["keysT"],
            identin=h["ident"]))
    r4 = run_bass_kernel_spmd(build_pd(), maps4, core_ids=cores)

    maps5 = []
    for core in cores:
        c, hh = core // 2, core % 2
        rows = own_rows[hh]
        maps5.append(dict(
            xnT1o=np.asarray(r3.results[core]["xnT1"]),
            kT1=np.asarray(r4.results[core]["kT1"]),
            v1=np.asarray(r4.results[core]["v1"]),
            x1_in=np.asarray(r3.results[core]["x1_out"]),
            wq1b=w["wq1"], wo1b=w["wo1"], worgb=w["worg"],
            downb=w["down"], upb=w["up"],
            co_in=np.asarray(r4.results[core]["coeffs"]),
            rqc=np.ascontiguousarray(h["rope1_cos_t"][rows] * sq),
            rqs=np.ascontiguousarray(h["rope1_sin_t"][rows] * sq),
            identin=h["ident"]))
    r5 = run_bass_kernel_spmd(build_pe(), maps5, core_ids=cores)

    out = np.empty((B, S, D), np.float32)
    for core in cores:
        c, hh = core // 2, core % 2
        out[c][own_rows[hh]] = np.asarray(r5.results[core]["out_d"])
    return out


def fused_in_maps(h):
    sq = np.float32(1.0 / np.sqrt(128.0))
    triu = np.triu(np.ones((128, 128), np.float32))
    onesm = np.ones((128, 128), np.float32)
    zerosm = np.zeros((128, 128), np.float32)
    maps = []
    for core in range(NCORE):
        c, hh = core // 2, core % 2
        rows = np.concatenate([np.arange(128 * (2 * t + hh), 128 * (2 * t + hh) + 128)
                               for t in range(OWN_T)])
        tix = (np.array([(2 * t + hh) * 128 for t in range(OWN_T)])[None, :]
               + np.arange(128)[:, None]).astype(np.int32)
        m = {"identin": h["ident"], "keysT_s": h["keysT"]}
        for name, rowsn, cols in FCAST_MATS:
            rs = rowsn // NCORE
            m[name + "_s"] = np.ascontiguousarray(h[name][rs * core:rs * core + rs])
        m["qw_s"] = np.ascontiguousarray(h["qw"][512 * core:512 * core + 512])
        m["lut4_s"] = np.ascontiguousarray(h["lut4"][512 * core:512 * core + 512])
        m["lut8_s"] = np.ascontiguousarray(h["lut8"][512 * core:512 * core + 512])
        m["xc"] = h["x"][c]
        m["wkc_f"] = np.ascontiguousarray(h["wk0"][:, 512 * hh:512 * hh + 512])
        m["wvc_f"] = np.ascontiguousarray(h["wv0"][:, 512 * hh:512 * hh + 512])
        m["ropec"] = h["rope0_cos_t"]
        m["ropes"] = h["rope0_sin_t"]
        m["xo_in"] = np.ascontiguousarray(h["x"][c][rows])
        m["rqc0"] = np.ascontiguousarray(h["rope0_cos_t"][rows] * sq)
        m["rqs0"] = np.ascontiguousarray(h["rope0_sin_t"][rows] * sq)
        m["m0in"] = triu if hh == 0 else onesm
        m["m1in"] = zerosm if hh == 0 else triu
        m["bias_in"] = h["bias"]
        m["tix"] = tix
        m["kcache"] = h["kcache"]
        m["vcache"] = h["vcache"]
        m["bt"] = h["bt"]
        m["rqc1"] = np.ascontiguousarray(h["rope1_cos_t"][rows] * sq)
        m["rqs1"] = np.ascontiguousarray(h["rope1_sin_t"][rows] * sq)
        maps.append(m)
    return maps


def kernel(**inputs):
    h = host_prep(inputs)
    cores = list(range(NCORE))
    nc = build_fused()
    res = run_bass_kernel_spmd(nc, fused_in_maps(h), core_ids=cores)
    out = np.empty((B, S, D), np.float32)
    for core in cores:
        c, hh = core // 2, core % 2
        rows = np.concatenate([np.arange(128 * (2 * t + hh), 128 * (2 * t + hh) + 128)
                               for t in range(OWN_T)])
        out[c][rows] = np.asarray(res.results[core]["out_d"])
    return out
